# version 04
# baseline (speedup 1.0000x reference)
import numpy as np
import ml_dtypes
import concourse.bass as bass
import concourse.mybir as mybir
from concourse.bass_utils import run_bass_kernel_spmd

F32 = mybir.dt.float32
BF16 = mybir.dt.bfloat16
AF = mybir.ActivationFunctionType
ALU = mybir.AluOpType
AX = mybir.AxisListType
bft = ml_dtypes.bfloat16

H, DM = 16, 64
D = 1024
WIN = 256
T_OWN = 2048
HALO = 512
NCH = 8
DEPTH = 6
D1P = 5632
NPAIR = 22
EPS = 1e-5
TLOC = HALO + T_OWN

ENGINES = ("sync", "gpsimd", "scalar", "vector", "tensor")

N_LAYERS = 6
OUTPUT_X1 = False
DBG_NO_WO = False
DBG_NO_FFN = False
DBG_NO_ATT = False
DBG_FFN_XN_ONLY = False
DBG_FFN_NO_W2 = False
DBG_DUMP = False
DBG_DUMP2 = False
DBG_DUMP3 = False
SCAN_CHUNKS = list(range(15, -1, -1))


def qk_perm():
    p = []
    for g in range(4):
        for half in range(2):
            for hh in range(4):
                head = 4 * g + hh
                p.extend(range(head * 64 + half * 32, head * 64 + half * 32 + 32))
    return np.array(p)

PERM = qk_perm()

def w1_cols():
    cols = np.zeros(D1P, dtype=np.int64)
    valid = np.zeros(D1P, dtype=bool)
    o = 0
    for j in range(NPAIR):
        s0, e0 = j * 128, min(j * 128 + 128, 2730)
        n = e0 - s0
        cols[o:o + n] = np.arange(s0, e0); valid[o:o + n] = True
        o += 128
        cols[o:o + n] = 2730 + np.arange(s0, e0); valid[o:o + n] = True
        o += 128
    return cols, valid

W1COLS, W1VALID = w1_cols()

def host_rotary_tables(h):
    half = np.arange(0, DM, 2, dtype=np.float64)
    inv_freq = 1.0 / 10000.0 ** (half / DM)
    base_scale = (half + 0.4 * DM) / (1.4 * DM)
    start = h * T_OWN - HALO
    pos = np.clip(np.arange(start, start + TLOC), 0, 4095).astype(np.float64)
    fr = pos[:, None] * inv_freq[None, :]
    power = (pos - 2048.0) / (WIN // 2)
    scale = base_scale[None, :] ** power[:, None]
    c, s = np.cos(fr), np.sin(fr)
    return (np.asarray(c * scale, np.float32), np.asarray(s * scale, np.float32),
            np.asarray(c / scale, np.float32), np.asarray(s / scale, np.float32))

def host_masks(h):
    i = np.arange(128)
    j = np.arange(512)
    tiles = []
    for qt in range(2):
        qq = (i + qt * 128)[:, None] + 256
        keep = (j[None, :] >= qq - WIN) & (j[None, :] <= qq)
        tiles.append(keep.astype(np.float32))
    for qt in range(2):
        qq = (i + qt * 128)[:, None] + 256
        keep = (j[None, :] >= qq - WIN) & (j[None, :] <= qq)
        if h == 0:
            keep = keep & (j[None, :] >= 256)
        tiles.append(keep.astype(np.float32))
    return np.concatenate(tiles, axis=1).astype(bft)


class Em:
    def __init__(self):
        self.q = {e: [] for e in ENGINES}
        self.cnt = {}

    def now(self, sem):
        return self.cnt.get(sem, 0)

    def op(self, eng, fn, inc=None, desc=""):
        if inc is not None:
            self.cnt[inc[0]] = self.cnt.get(inc[0], 0) + inc[1]
        self.q[eng].append(("op", fn, inc, desc))
        return self.cnt.get(inc[0], 0) if inc else None

    def wait(self, eng, sem, val, desc=""):
        if val is None or val <= 0:
            return
        q = self.q[eng]
        if q and q[-1][0] == "wait" and q[-1][1] == sem:
            if q[-1][2] >= val:
                return
            q.pop()
        q.append(("wait", sem, val, desc))

    def check_deadlock(self):
        ptr = {e: 0 for e in ENGINES}
        val = {}
        total = sum(len(v) for v in self.q.values())
        done = 0
        progress = True
        while progress:
            progress = False
            for e in ENGINES:
                while ptr[e] < len(self.q[e]):
                    it = self.q[e][ptr[e]]
                    if it[0] == "wait":
                        if val.get(it[1], 0) < it[2]:
                            break
                    else:
                        if it[2] is not None:
                            val[it[2][0]] = val.get(it[2][0], 0) + it[2][1]
                    ptr[e] += 1
                    done += 1
                    progress = True
        if done != total:
            lines = [f"  {e}@{ptr[e]}/{len(self.q[e])}: {self.q[e][ptr[e]][:3]}"
                     for e in ENGINES if ptr[e] < len(self.q[e])]
            raise RuntimeError("DEADLOCK in schedule:\n" + "\n".join(lines))

    def emit_into(self, block, sems):
        for e in ENGINES:
            items = self.q[e]
            def body(eng, items=items):
                for it in items:
                    if it[0] == "wait":
                        eng.wait_ge(sems[it[1]], it[2])
                    else:
                        ins = it[1](eng)
                        if it[2] is not None:
                            assert ins is not None, f"inc on None: {it[3]}"
                            ins.then_inc(sems[it[2][0]], it[2][1])
            getattr(block, e)(body)


def build_program():
    nc = bass.Bass("TRN2", target_bir_lowering=False, detect_race_conditions=False)

    def din(name, shape, dt=F32):
        return nc.dram_tensor(name, shape, dt, kind="ExternalInput").ap()

    x_tm = din("x_tm", [T_OWN, D])
    tabs = din("tabs", [128, 4 * TLOC])
    masks_d = nc.dram_tensor("masks", [128, 4 * 512], BF16, kind="ExternalInput").ap()
    flags_d = din("flags", [128, 4])
    qks_d = din("qks", [128, DEPTH * 4])
    lnp_d = din("lnp", [128, DEPTH * 16])
    b1p_d = din("b1p", [128, DEPTH * 44])
    b2p_d = din("b2p", [128, DEPTH * 8])
    dn_Wq = din("dn_Wq", [D, D]); dn_Wk = din("dn_Wk", [D, D])
    dn_Wv = din("dn_Wv", [D, D]); dn_Wb = din("dn_Wb", [D, 16])
    dn_Wo = din("dn_Wo", [D, D])
    in_Wqkv = din("in_Wqkv", [DEPTH, D, 3 * D])
    in_Wo = din("in_Wo", [DEPTH, D, D])
    in_W1 = din("in_W1", [DEPTH, D, D1P])
    in_W2 = din("in_W2", [DEPTH, NPAIR * 128, D])

    out_tm = nc.dram_tensor("out_tm", [T_OWN, D], F32, kind="ExternalOutput").ap()

    park = nc.dram_tensor("park", [128, NCH * T_OWN], F32)
    cin = nc.dram_tensor("cin", [64, H * 64], F32)
    cout = nc.dram_tensor("cout", [8 * 64, H * 64], F32)
    hin = nc.dram_tensor("hin", [128, NCH * 513], F32)
    hout = nc.dram_tensor("hout", [8 * 128, NCH * 513], F32)
    hloc = nc.dram_tensor("hloc", [128, NCH * 513], F32)

    idn_b_d = nc.inline_tensor(np.eye(128, dtype=bft), name="c_idnb").ap()
    idn_f_d = nc.inline_tensor(np.eye(128, dtype=np.float32), name="c_idnf").ap()
    pat_dn = np.zeros((128, NCH * 16), dtype=bft)
    for c in range(NCH):
        pat_dn[0:64, c * 16 + 2 * c] = 1
        pat_dn[64:128, c * 16 + 2 * c + 1] = 1
    pat_dn_d = nc.inline_tensor(pat_dn, name="c_patdn").ap()
    bcp_dn = np.zeros((16, NCH * 128), dtype=bft)
    for c in range(NCH):
        bcp_dn[2 * c, c * 128 + 0:c * 128 + 64] = 1
        bcp_dn[2 * c + 1, c * 128 + 64:c * 128 + 128] = 1
    bcp_dn_d = nc.inline_tensor(bcp_dn, name="c_bcpdn").ap()
    sel = np.zeros((128, 128), dtype=bft)
    sel[0:64, 0:64] = np.eye(64)
    sel[64:128, 64:128] = np.eye(64)
    sel_d = nc.inline_tensor(sel, name="c_sel2").ap()
    um = (np.arange(128)[:, None] >= np.arange(128)[None, :]).astype(bft)
    um_d = nc.inline_tensor(um, name="c_um").ap()
    pat_in = np.zeros((128, 4 * 16), dtype=bft)
    for g in range(4):
        for j in range(4):
            pat_in[j * 32:(j + 1) * 32, g * 16 + 4 * g + j] = 1
    pat_in_d = nc.inline_tensor(pat_in, name="c_patin").ap()
    bcp_in = np.zeros((16, 4 * 128), dtype=bft)
    for g in range(4):
        for j in range(4):
            bcp_in[4 * g + j, g * 128 + j * 32:g * 128 + (j + 1) * 32] = 1
    bcp_in_d = nc.inline_tensor(bcp_in, name="c_bcpin").ap()
    ones_col_d = nc.inline_tensor(np.ones((128, 2), dtype=bft), name="c_onesc").ap()
    ones_row_d = nc.inline_tensor(np.ones((1, 128), dtype=bft), name="c_onesr").ap()

    em = Em()
    from contextlib import ExitStack
    with ExitStack() as ctx:
        sbt = lambda n, s, d: ctx.enter_context(nc.sbuf_tensor(n, s, d))
        ptt = lambda n, s, d: ctx.enter_context(nc.psum_tensor(n, s, d))

        XR = sbt("XR", [128, NCH * T_OWN], F32)
        IDNB = sbt("IDNB", [128, 128], BF16)
        IDNF = sbt("IDNF", [128, 128], F32)
        PATD = sbt("PATD", [128, NCH * 16], BF16)
        BCPD = sbt("BCPD", [16, NCH * 128], BF16)
        PATI = sbt("PATI", [128, 64], BF16)
        BCPI = sbt("BCPI", [16, 512], BF16)
        SEL2 = sbt("SEL2", [128, 128], BF16)
        UM = sbt("UM", [128, 128], BF16)
        FLG = sbt("FLG", [128, 4], F32)
        QKS = sbt("QKS", [128, DEPTH * 4], F32)
        LNP = sbt("LNP", [128, DEPTH * 16], F32)
        B1P = sbt("B1P", [128, DEPTH * 44], F32)
        B2P = sbt("B2P", [128, DEPTH * 8], F32)
        ONEC = sbt("ONEC", [128, 2], BF16)
        ONER = sbt("ONER", [1, 128], BF16)

        AB = sbt("AB", [128, 61600], BF16)
        ARF = sbt("ARF", [128, 2048], F32)
        SS = sbt("SS", [128, 512], F32)
        SSB = sbt("SSB", [128, 3200], BF16)

        PS = [ptt(f"ps{i}", [128, 512], F32) for i in range(7)]
        PB = ptt("psb", [128, 1024], BF16)

        QHAT = AB[:, 0:16384]
        KHAT = AB[:, 16384:32768]
        VFM = AB[:, 32768:49152]
        WB = AB[:, 49152:57344]
        X8 = AB[:, 57344:61440]
        VTM = AB[:, 57344:58368]
        KTM = AB[:, 58368:59392]
        HM = AB[:, 59392:59904]
        BBS = AB[:, 59904:60416]
        OUTB = AB[:, 60416:61440]
        WTS = AB[:, 61440:61568]
        PRJ = AB[:, 32768:33280]
        XB1 = AB[:, 57344:58368]
        XB2 = AB[:, 58368:59392]
        BETA = SSB[0:16, 0:2048]
        S_SB = SSB[0:64, 2048:3072]
        WBETA = SSB[:, 3072:3200]
        RSQ = SS[0:16, 0:512]
        MUs = SS[0:1, 0:128]
        MSQ = SS[0:1, 256:384]
        TMP1 = SS[0:1, 384:512]
        XIN = [ARF[:, 0:512], ARF[:, 512:1024]]
        CARF = ARF[0:64, 0:1024]
        S_F32 = ARF[0:64, 0:1024]
        XSG = ARF[:, 1024:2048]
        ZSC = SS[:, 128:256]

        def dma(engq, dst, src, sem):
            em.op(engq, lambda e, dst=dst, src=src: e.dma_start(dst, src), inc=(sem, 16))

        for dst, src in ((IDNB[:], idn_b_d), (IDNF[:], idn_f_d), (PATD[:], pat_dn_d),
                         (BCPD[:], bcp_dn_d), (PATI[:], pat_in_d), (BCPI[:], bcp_in_d),
                         (SEL2[:], sel_d), (UM[:], um_d), (FLG[:], flags_d),
                         (QKS[:], qks_d), (LNP[:], lnp_d), (B1P[:], b1p_d),
                         (B2P[:], b2p_d), (ONEC[:], ones_col_d),
                         (ONER[:], ones_row_d)):
            dma("sync", dst, src, "L")
        consts_L = em.now("L")
        for e_ in ("tensor", "vector", "scalar", "gpsimd"):
            em.wait(e_, "L", consts_L)

        for th in range(32):
            tt, half = th // 2, th % 2
            buf = XIN[th % 2]
            bsem = "X0" if th % 2 == 0 else "X1"
            if th >= 2:
                em.wait("sync", "T", em.now("T"))
            dma("sync", buf, x_tm[tt * 128:(tt + 1) * 128, half * 512:(half + 1) * 512], bsem)
            em.wait("tensor", bsem, em.now(bsem))
            for kk in range(4):
                k = half * 4 + kk
                bank = PS[kk % 2]
                if th > 0 or kk >= 2:
                    em.wait("tensor", "V", em.now("V") - 1)
                em.op("tensor", lambda e, b=bank, src=buf, kk=kk:
                      e.transpose(b[:, 0:128], src[:, kk * 128:(kk + 1) * 128], IDNF[:]),
                      inc=("T", 1))
                em.wait("vector", "T", em.now("T"))
                em.op("vector", lambda e, b=bank, tt=tt, k=k:
                      e.tensor_copy(XR[:, k * T_OWN + tt * 128:k * T_OWN + (tt + 1) * 128], b[:, 0:128]),
                      inc=("V", 1))


        def load_w_slabs(wsrc, ncols=1024):
            for k in range(NCH):
                em.op("gpsimd", lambda e, k=k, w=wsrc, nc_=ncols:
                      e.dma_start(WB[:, k * nc_:(k + 1) * nc_], w[k * 128:(k + 1) * 128, :]),
                      inc=("G", 16))
            return em.now("G")

        def cast_group(n):
            for k in range(NCH):
                em.op("vector", lambda e, k=k, n=n:
                      e.tensor_copy(X8[:, k * 512:(k + 1) * 512],
                                    XR[:, k * T_OWN + n * 512:k * T_OWN + (n + 1) * 512]),
                      inc=("V", 1))
            return em.now("V")

        def dn_projection(dst, want_norm, sqrt_scale, g_after):
            for n in range(4):
                em.wait("vector", "T", em.now("T"))
                cast_v = cast_group(n)
                em.wait("tensor", "V", cast_v)
                em.wait("tensor", "G", g_after)
                for m in range(NCH):
                    bank = PS[m % 2]
                    em.wait("tensor", "V", em.now("V") if m >= 2 else cast_v)
                    if want_norm:
                        em.wait("tensor", "S", em.now("S"))
                    for k in range(NCH):
                        em.op("tensor", lambda e, m=m, k=k, b=bank:
                              e.matmul(b[:], WB[:, k * 1024 + m * 128:k * 1024 + (m + 1) * 128],
                                       X8[:, k * 512:(k + 1) * 512],
                                       start=(k == 0), stop=(k == NCH - 1)),
                              inc=("T", 1) if k == NCH - 1 else None)
                    t_mm = em.now("T")
                    if want_norm:
                        em.wait("scalar", "T", t_mm)
                        em.op("scalar", lambda e, b=bank: e.activation(PRJ[:, 0:512], b[:], AF.Square),
                              inc=("S", 1))
                        em.wait("tensor", "S", em.now("S"))
                        em.op("tensor", lambda e, m=m:
                              e.matmul(PS[3][0:16, 0:512], PATD[:, m * 16:(m + 1) * 16], PRJ[:, 0:512],
                                       start=(m == 0), stop=(m == NCH - 1)),
                              inc=("T", 1))
                        em.wait("vector", "T", em.now("T"))
                        em.op("vector", lambda e, m=m, n=n, b=bank, d=dst:
                              e.tensor_copy(d[:, m * T_OWN + n * 512:m * T_OWN + (n + 1) * 512], b[:]),
                              inc=("V", 1))
                    else:
                        em.wait("vector", "T", t_mm)
                        em.op("vector", lambda e, m=m, n=n, b=bank, d=dst:
                              e.tensor_copy(d[:, m * T_OWN + n * 512:m * T_OWN + (n + 1) * 512], b[:]),
                              inc=("V", 1))
                if want_norm:
                    em.wait("vector", "T", em.now("T"))
                    em.op("vector", lambda e: e.tensor_scalar_max(RSQ, PS[3][0:16, 0:512], 1e-24), inc=("V", 1))
                    em.op("vector", lambda e: e.reciprocal(RSQ, RSQ), inc=("V", 1))
                    em.wait("scalar", "V", em.now("V"))
                    em.op("scalar", lambda e, sc=sqrt_scale: e.activation(RSQ, RSQ, AF.Sqrt, scale=sc), inc=("S", 1))
                    em.wait("vector", "S", em.now("S"))
                    em.op("vector", lambda e: e.tensor_copy(PRJ[0:16, 0:512], RSQ), inc=("V", 1))
                    for m in range(NCH):
                        em.wait("tensor", "V", em.now("V"))
                        em.op("tensor", lambda e, m=m:
                              e.matmul(PS[2][:], BCPD[:, m * 128:(m + 1) * 128], PRJ[0:16, 0:512],
                                       start=True, stop=True),
                              inc=("T", 1))
                        em.wait("vector", "T", em.now("T"))
                        em.op("vector", lambda e, m=m, n=n, d=dst:
                              e.tensor_tensor(d[:, m * T_OWN + n * 512:m * T_OWN + (n + 1) * 512],
                                              d[:, m * T_OWN + n * 512:m * T_OWN + (n + 1) * 512],
                                              PS[2][:], ALU.mult),
                              inc=("V", 1))

        g = load_w_slabs(dn_Wq)
        for k in range(NCH):
            em.op("gpsimd", lambda e, k=k: e.dma_start(WBETA[:, k * 16:(k + 1) * 16], dn_Wb[k * 128:(k + 1) * 128, :]),
                  inc=("G", 16))
        g_beta = em.now("G")
        dn_projection(QHAT, True, 1.0 / 64.0, g_beta)

        for n in range(4):
            em.wait("vector", "T", em.now("T"))
            cast_v = cast_group(n)
            em.wait("tensor", "V", cast_v)
            em.wait("tensor", "G", g_beta)
            em.wait("tensor", "S", em.now("S"))
            for k in range(NCH):
                em.op("tensor", lambda e, k=k:
                      e.matmul(PS[6][0:16, 0:512], WBETA[:, k * 16:(k + 1) * 16], X8[:, k * 512:(k + 1) * 512],
                               start=(k == 0), stop=(k == NCH - 1)),
                      inc=("T", 1) if k == NCH - 1 else None)
            em.wait("scalar", "T", em.now("T"))
            em.op("scalar", lambda e, n=n: e.activation(BETA[:, n * 512:(n + 1) * 512], PS[6][0:16, 0:512], AF.Sigmoid),
                  inc=("S", 1))

        em.wait("gpsimd", "T", em.now("T"))
        g = load_w_slabs(dn_Wk)
        dn_projection(KHAT, True, 1.0, g)
        em.wait("gpsimd", "T", em.now("T"))
        g = load_w_slabs(dn_Wv)
        dn_projection(VFM, False, None, g)

        def s_ap(h):
            b = PS[4 + h // 8]
            return b[0:64, (h % 8) * 64:(h % 8 + 1) * 64]

        def transpose_chunk(src_hb, dst, c):
            em.wait("tensor", "S", em.now("S"))
            for p in range(NCH):
                em.op("tensor", lambda e, p=p, c=c, s_=src_hb:
                      e.transpose(PB[:, p * 128:(p + 1) * 128],
                                  s_[:, p * T_OWN + c * 128:p * T_OWN + (c + 1) * 128], IDNB[:]),
                      inc=("T", 1) if p == NCH - 1 else None)
            em.wait("scalar", "T", em.now("T"))
            em.op("scalar", lambda e, d=dst: e.copy(d, PB[:]), inc=("S", 1))

        for c in range(16):
            transpose_chunk(KHAT, KTM[:], c)
            transpose_chunk(VFM, VTM[:], c)
            em.wait("tensor", "S", em.now("S"))
            for h in range(H):
                em.op("tensor", lambda e, h=h, c=c:
                      e.matmul(s_ap(h), KTM[:, h * 64:(h + 1) * 64], VTM[:, h * 64:(h + 1) * 64],
                               start=(c == 0 and h % 8 == 0), stop=(c == 15 and h % 8 == 7)),
                      inc=("T", 1) if h == H - 1 else None)

        em.wait("vector", "T", em.now("T"))
        SOUT = ARF[0:64, 0:512]
        SOUT2 = ARF[0:64, 512:1024]
        em.op("vector", lambda e: e.tensor_copy(SOUT, PS[4][0:64, 0:512]), inc=("V", 1))
        em.op("vector", lambda e: e.tensor_copy(SOUT2, PS[5][0:64, 0:512]), inc=("V", 1))
        em.wait("sync", "V", em.now("V"))
        dma("sync", cin.ap()[0:64, 0:512], SOUT, "L")
        dma("sync", cin.ap()[0:64, 512:1024], SOUT2, "L")
        em.wait("gpsimd", "L", em.now("L"))
        em.op("gpsimd", lambda e: e.collective_compute(
            "AllGather", ALU.bypass, replica_groups=[[0, 1, 2, 3, 4, 5, 6, 7]],
            ins=[cin.ap().opt()], outs=[cout.ap().opt()]), inc=("C", 1))
        em.wait("sync", "C", em.now("C"))
        def recv_carry(e):
            pid = e.partition_id()
            off = ((pid + 1) % 8) * (64 * 1024)
            return e.dma_start(CARF, bass.AP(cout, off, [[1024, 64], [1, 1024]]))
        em.op("sync", recv_carry, inc=("L", 16))
        em.wait("vector", "L", em.now("L"))
        em.op("vector", lambda e: e.tensor_scalar_mul(S_F32, CARF, FLG[0:64, 1:2]), inc=None)
        em.op("vector", lambda e: e.tensor_copy(S_SB[:, 0:1024], S_F32), inc=("V", 1))

        em.wait("gpsimd", "T", em.now("T"))
        g_wo = load_w_slabs(dn_Wo)

        def ln_chunk(dst_col):
            em.op("vector", lambda e: e.tensor_copy(XB1[:, 0:1024], XSG), inc=("V", 1))
            em.wait("scalar", "V", em.now("V"))
            em.op("scalar", lambda e: e.activation(XB2[:, 0:1024], XSG, AF.Square), inc=("S", 1))
            em.wait("tensor", "S", em.now("S"))
            em.wait("tensor", "V", em.now("V"))
            for p in range(NCH):
                em.op("tensor", lambda e, p=p:
                      e.matmul(PS[6][0:1, 0:128], ONEC[:, 0:1], XB1[:, p * 128:(p + 1) * 128],
                               start=(p == 0), stop=(p == NCH - 1)),
                      inc=("T", 1) if p == NCH - 1 else None)
            for p in range(NCH):
                em.op("tensor", lambda e, p=p:
                      e.matmul(PS[3][0:1, 0:128], ONEC[:, 0:1], XB2[:, p * 128:(p + 1) * 128],
                               start=(p == 0), stop=(p == NCH - 1)),
                      inc=("T", 1) if p == NCH - 1 else None)
            em.wait("vector", "T", em.now("T"))
            em.op("vector", lambda e: e.tensor_scalar_mul(MUs, PS[6][0:1, 0:128], 1.0 / D), inc=("V", 1))
            em.op("vector", lambda e: e.tensor_scalar_mul(MSQ, PS[3][0:1, 0:128], 1.0 / D), inc=("V", 1))
            em.op("vector", lambda e: e.tensor_tensor(TMP1, MUs, MUs, ALU.mult), inc=("V", 1))
            em.op("vector", lambda e: e.tensor_sub(MSQ, MSQ, TMP1), inc=("V", 1))
            em.op("vector", lambda e: e.tensor_scalar_add(MSQ, MSQ, EPS), inc=("V", 1))
            em.wait("scalar", "V", em.now("V"))
            em.op("scalar", lambda e: e.activation(MSQ, MSQ, AF.Sqrt), inc=("S", 1))
            em.wait("vector", "S", em.now("S"))
            em.op("vector", lambda e: e.reciprocal(MSQ, MSQ), inc=("V", 1))
            em.op("vector", lambda e: e.scalar_tensor_tensor(MUs, MUs, -1.0, MSQ, ALU.mult, ALU.mult), inc=("V", 1))
            em.op("vector", lambda e: e.tensor_copy(BBS[0:1, 0:128], MSQ), inc=None)
            em.op("vector", lambda e: e.tensor_copy(BBS[0:1, 128:256], MUs), inc=("V", 1))
            em.wait("tensor", "V", em.now("V"))
            em.op("tensor", lambda e: e.matmul(PS[6][:, 0:128], ONER[:], BBS[0:1, 0:128], start=True, stop=True), inc=None)
            em.op("tensor", lambda e: e.matmul(PS[3][:, 0:128], ONER[:], BBS[0:1, 128:256], start=True, stop=True), inc=("T", 1))
            em.wait("vector", "T", em.now("T"))
            for p in range(NCH):
                em.op("vector", lambda e, p=p:
                      e.tensor_tensor(XSG[:, p * 128:(p + 1) * 128], XSG[:, p * 128:(p + 1) * 128],
                                      PS[6][:, 0:128], ALU.mult),
                      inc=None)
                em.op("vector", lambda e, p=p, w=dst_col:
                      e.tensor_tensor(XR[:, p * T_OWN + w:p * T_OWN + w + 128],
                                      XSG[:, p * 128:(p + 1) * 128], PS[3][:, 0:128], ALU.add),
                      inc=("V", 1) if p == NCH - 1 else None)

        for c in SCAN_CHUNKS:
            transpose_chunk(KHAT, KTM[:], c)
            transpose_chunk(VFM, VTM[:], c)
            for p in range(NCH):
                em.wait("tensor", "S", em.now("S"))
                em.wait("tensor", "V", em.now("V"))
                for (slot, src, selc) in ((0, QHAT, 0), (1, QHAT, 64), (2, KHAT, 0), (3, KHAT, 64)):
                    em.op("tensor", lambda e, p=p, c=c, slot=slot, src=src, selc=selc:
                          e.matmul(PS[2][0:64, slot * 128:(slot + 1) * 128], SEL2[:, selc:selc + 64],
                                   src[:, p * T_OWN + c * 128:p * T_OWN + (c + 1) * 128],
                                   start=True, stop=True),
                          inc=("T", 1) if slot == 3 else None)
                em.wait("scalar", "T", em.now("T"))
                em.op("scalar", lambda e: e.copy(HM[0:64, :], PS[2][0:64, 0:512]), inc=("S", 1))
                em.op("tensor", lambda e, p=p, c=c:
                      e.matmul(PS[6][:, 0:128], BCPD[:, p * 128:(p + 1) * 128], BETA[:, c * 128:(c + 1) * 128],
                               start=True, stop=True),
                      inc=("T", 1))
                em.wait("vector", "T", em.now("T"))
                em.op("vector", lambda e: e.tensor_copy(BBS[:, 0:128], PS[6][:, 0:128]), inc=None)
                em.op("vector", lambda e, p=p, c=c:
                      e.tensor_tensor(ZSC, VFM[:, p * T_OWN + c * 128:p * T_OWN + (c + 1) * 128],
                                      BBS[:, 0:128], ALU.mult),
                      inc=("V", 1))
                for hh in range(2):
                    h = 2 * p + hh
                    em.wait("tensor", "S", em.now("S"))
                    em.wait("tensor", "V", em.now("V"))
                    em.op("tensor", lambda e, hh=hh:
                          e.matmul(PS[6][:, 128:256], HM[0:64, 256 + hh * 128:256 + (hh + 1) * 128],
                                   HM[0:64, hh * 128:(hh + 1) * 128], start=True, stop=True),
                          inc=("T", 1))
                    em.wait("vector", "T", em.now("T"))
                    em.op("vector", lambda e: e.tensor_tensor(WTS[:, 0:128], PS[6][:, 128:256], UM[:], ALU.mult), inc=("V", 1))
                    em.wait("tensor", "V", em.now("V"))
                    em.op("tensor", lambda e, h=h:
                          e.matmul(PS[0][0:64, 0:128], VTM[:, h * 64:(h + 1) * 64], WTS[:, 0:128],
                                   start=True, stop=False), inc=None)
                    em.op("tensor", lambda e, h=h, hh=hh:
                          e.matmul(PS[0][0:64, 0:128], S_SB[:, h * 64:(h + 1) * 64],
                                   HM[0:64, hh * 128:(hh + 1) * 128], start=False, stop=True),
                          inc=("T", 1))
                    em.wait("vector", "T", em.now("T"))
                    em.op("vector", lambda e, hh=hh, p=p:
                          e.tensor_tensor(OUTB[hh * 64:(hh + 1) * 64, p * 128:(p + 1) * 128],
                                          PS[0][0:64, 0:128], ZSC[hh * 64:(hh + 1) * 64, :], ALU.add),
                          inc=("V", 1))
                    em.wait("tensor", "V", em.now("V"))
                    em.op("tensor", lambda e, h=h, c=c:
                          e.matmul(s_ap(h), KTM[:, h * 64:(h + 1) * 64], VTM[:, h * 64:(h + 1) * 64],
                                   start=(h % 8 == 0), stop=(h % 8 == 7)),
                          inc=("T", 1))
            em.wait("vector", "T", em.now("T"))
            em.op("vector", lambda e: e.tensor_tensor(S_F32[:, 0:512], S_F32[:, 0:512], PS[4][0:64, 0:512], ALU.add), inc=None)
            em.op("vector", lambda e: e.tensor_tensor(S_F32[:, 512:1024], S_F32[:, 512:1024], PS[5][0:64, 0:512], ALU.add), inc=None)
            em.op("vector", lambda e: e.tensor_copy(S_SB[:, 0:1024], S_F32), inc=("V", 1))
            em.wait("tensor", "S", em.now("S"))
            em.wait("tensor", "V", em.now("V"))
            em.wait("tensor", "G", g_wo)
            for m in range(NCH):
                em.wait("tensor", "V", em.now("V"))
                for k in range(NCH):
                    em.op("tensor", lambda e, m=m, k=k:
                          e.matmul(PS[1][:, 0:128], WB[:, k * 1024 + m * 128:k * 1024 + (m + 1) * 128],
                                   OUTB[:, k * 128:(k + 1) * 128], start=(k == 0), stop=(k == NCH - 1)),
                          inc=("T", 1) if k == NCH - 1 else None)
                em.wait("vector", "T", em.now("T"))
                em.op("vector", lambda e, m=m, c=c:
                      e.tensor_tensor(XSG[:, m * 128:(m + 1) * 128], PS[1][:, 0:128],
                                      XR[:, m * T_OWN + c * 128:m * T_OWN + (c + 1) * 128], ALU.add),
                      inc=("V", 1))
            ln_chunk(c * 128)

        em.wait("sync", "V", em.now("V"))
        dma("sync", park.ap(), XR[:], "L")
        hsrc = XR[:].rearrange("p (c t) -> p c t", c=NCH)[:, :, T_OWN - 513:T_OWN]
        dma("sync", hin.ap().rearrange("p (c t) -> p c t", c=NCH), hsrc, "L")
        if N_LAYERS > 0:
            em.wait("gpsimd", "L", em.now("L"))
            em.op("gpsimd", lambda e: e.collective_compute(
                "AllGather", ALU.bypass, replica_groups=[[0, 1, 2, 3, 4, 5, 6, 7]],
                ins=[hin.ap().opt()], outs=[hout.ap().opt()]), inc=("C", 1))
            em.wait("sync", "T", em.now("T"))
            dma("sync", SSB[:, 0:2048], masks_d, "L")

        WQKV = AB[:, 0:24576]
        SHBa = lambda k: AB[:, 24576 + k * 512:24576 + (k + 1) * 512]
        TABS = lambda t: AB[:, 28672 + t * TLOC:28672 + (t + 1) * TLOC]
        QGa = lambda k: AB[:, 38912 + k * 512:38912 + (k + 1) * 512]
        KSa = lambda k, sl_: AB[:, 43008 + k * 768 + sl_ * 256:43008 + k * 768 + (sl_ + 1) * 256]
        VTa = lambda sl_: AB[:, 49152 + sl_ * 1024:49152 + (sl_ + 1) * 1024]
        UBa = lambda k: AB[:, 55296 + k * 512:55296 + (k + 1) * 512]
        OUTFa = lambda k: AB[:, 55296 + k * 512:55296 + (k + 1) * 512]
        PBUFa = AB[:, 59392:59904]
        PTBa = AB[:, 59904:60928]
        SQBa = AB[:, 60928:61440]
        WOs = lambda k: (AB[:, 24576 + k * 1024:24576 + (k + 1) * 1024] if k < 4
                         else AB[:, 38912 + (k - 4) * 1024:38912 + (k - 3) * 1024])
        MSKA = lambda t: SSB[:, t * 512:(t + 1) * 512]
        RTT = SSB[0:16, 2048:2304]
        RCB = SSB[:, 2304:2320]
        NMX = SS[:, 260:261]
        SUMS = [SS[:, 264:280], SS[:, 280:296]]
        RCF = SS[:, 296:312]
        RCF2 = SS[:, 364:380]
        RSQA = ARF[0:16, 0:512]
        XHLL = SS[:, 312:320]
        FBC = SS[:, 320:328]
        SAVEC = SS[:, 328:336]
        XUH = SS[:, 336:344]
        SAVE2 = SS[:, 344:352]
        W2Sa = lambda j: AB[:, j * 1024:(j + 1) * 1024]
        XNa = lambda k: AB[:, 22528 + k * 1024:22528 + (k + 1) * 1024]
        H2Ha = lambda j: AB[:, 30720 + j * 1024:30720 + (j + 1) * 1024]
        W1Ba = lambda buf: AB[:, 53248 + buf * 2048:53248 + (buf + 1) * 2048]
        GAa = AB[:, 57344 + 0:57344 + 512]
        CHS = AB[:, 57856:58368]
        SHF = ARF[:, 0:2048]
        PFG = ARF[:, 0:2048]

        def ln_stats_rolling(src_f32_slices, ntok, bias_eps=EPS):
            for p in range(NCH):
                t_prev = em.now("T")
                em.wait("vector", "T", t_prev)
                em.op("vector", lambda e, p=p: e.tensor_copy(CHS[:, 0:ntok], src_f32_slices[p]), inc=("V", 1))
                em.wait("scalar", "V", em.now("V"))
                em.wait("scalar", "T", t_prev)
                em.op("scalar", lambda e, p=p: e.activation(CHS[:, 256:256 + ntok], src_f32_slices[p], AF.Square), inc=("S", 1))
                em.wait("tensor", "V", em.now("V"))
                em.wait("tensor", "S", em.now("S"))
                em.op("tensor", lambda e, p=p, n=ntok:
                      e.matmul(PS[6][0:1, 0:n], ONEC[:, 0:1], CHS[:, 0:n], start=(p == 0), stop=(p == NCH - 1)),
                      inc=None)
                em.op("tensor", lambda e, p=p, n=ntok:
                      e.matmul(PS[3][0:1, 0:n], ONEC[:, 0:1], CHS[:, 256:256 + n], start=(p == 0), stop=(p == NCH - 1)),
                      inc=("T", 1))
            em.wait("vector", "T", em.now("T"))
            for h in range(ntok // 128):
                hsl = slice(h * 128, (h + 1) * 128)
                em.op("vector", lambda e, hsl=hsl: e.tensor_scalar_mul(MUs, PS[6][0:1, hsl], 1.0 / D), inc=("V", 1))
                em.op("vector", lambda e, hsl=hsl: e.tensor_scalar_mul(MSQ, PS[3][0:1, hsl], 1.0 / D), inc=("V", 1))
                em.op("vector", lambda e: e.tensor_tensor(TMP1, MUs, MUs, ALU.mult), inc=("V", 1))
                em.op("vector", lambda e: e.tensor_sub(MSQ, MSQ, TMP1), inc=("V", 1))
                em.op("vector", lambda e, be=bias_eps: e.tensor_scalar_add(MSQ, MSQ, be), inc=("V", 1))
                em.wait("scalar", "V", em.now("V"))
                em.op("scalar", lambda e: e.activation(MSQ, MSQ, AF.Sqrt), inc=("S", 1))
                em.wait("vector", "S", em.now("S"))
                em.op("vector", lambda e: e.reciprocal(MSQ, MSQ), inc=("V", 1))
                em.op("vector", lambda e: e.scalar_tensor_tensor(MUs, MUs, -1.0, MSQ, ALU.mult, ALU.mult), inc=("V", 1))
                em.op("vector", lambda e, hsl=hsl: e.tensor_copy(CHS[0:1, hsl], MSQ), inc=None)
                em.op("vector", lambda e, h=h: e.tensor_copy(CHS[0:1, 256 + h * 128:256 + (h + 1) * 128], MUs), inc=("V", 1))
            em.wait("tensor", "V", em.now("V"))
            em.op("tensor", lambda e, n=ntok: e.matmul(PS[6][:, 0:n], ONER[:], CHS[0:1, 0:n], start=True, stop=True), inc=None)
            em.op("tensor", lambda e, n=ntok: e.matmul(PS[3][:, 0:n], ONER[:], CHS[0:1, 256:256 + n], start=True, stop=True), inc=("T", 1))
            em.wait("vector", "T", em.now("T"))


        def emit_attn_stage(l):
            em.wait("gpsimd", "T", em.now("T"))
            for k in range(NCH):
                em.op("gpsimd", lambda e, k=k, l=l:
                      e.dma_start(WQKV[:, k * 3072:(k + 1) * 3072], in_Wqkv[l, k * 128:(k + 1) * 128, :]),
                      inc=("WQ", 16))
            em.op("gpsimd", lambda e: e.dma_start(AB[:, 28672:38912], tabs), inc=("HG", 16))
            em.wait("vector", "HG", em.now("HG"))
            em.wait("vector", "L", em.now("L"))
            wq_done = em.now("WQ")
            em.wait("sync", "C", em.now("C"))
            def dma_slab(e):
                pid = e.partition_id()
                off = ((pid + 7) % 8) * (128 * NCH * 513)
                return e.dma_start(hloc.ap(), bass.AP(hout, off, [[NCH * 513, 128], [1, NCH * 513]]))
            em.op("sync", dma_slab, inc=("HS", 16))
            em.wait("sync", "HS", em.now("HS"))
            em.op("sync", lambda e: e.dma_start(XHLL[:, 0:NCH],
                                                bass.AP(hloc, 512, [[NCH * 513, 128], [513, NCH]]),
                                                allow_slow_non_contiguous=True),
                  inc=("HS", 16))
            em.wait("vector", "HS", em.now("HS"))
            em.op("vector", lambda e: e.tensor_scalar_mul(FBC[:, 0:NCH], XHLL[:, 0:NCH], FLG[:, 0:1]), inc=("V", 1))

            for g in range(5):
                if g == 0:
                    em.wait("gpsimd", "HS", em.now("HS"))
                    for k in range(NCH):
                        toff = 1 if k < 4 else 0
                        em.op("gpsimd", lambda e, k=k, toff=toff:
                              e.dma_start(SHBa(k), bass.AP(hloc, k * 513 + toff, [[NCH * 513, 128], [1, 512]])),
                              inc=("HG", 16))
                    em.wait("tensor", "HG", em.now("HG"))
                else:
                    em.wait("vector", "T", em.now("T"))
                    base = 512 * (g - 1)
                    for k in range(NCH):
                        if k < 4:
                            em.op("vector", lambda e, k=k, base=base:
                                  e.tensor_copy(SHBa(k), XR[:, k * T_OWN + base:k * T_OWN + base + 512]),
                                  inc=("V", 1))
                        else:
                            if g == 1:
                                em.op("vector", lambda e, k=k:
                                      e.tensor_copy(SHBa(k)[:, 0:1], FBC[:, k:k + 1]),
                                      inc=None)
                            else:
                                em.op("vector", lambda e, k=k:
                                      e.tensor_copy(SHBa(k)[:, 0:1], SAVEC[:, k:k + 1]),
                                      inc=None)
                            em.op("vector", lambda e, k=k, base=base:
                                  e.tensor_copy(SHBa(k)[:, 1:512], XR[:, k * T_OWN + base:k * T_OWN + base + 511]),
                                  inc=("V", 1))
                    em.wait("tensor", "V", em.now("V"))
                if g >= 1:
                    em.op("vector", lambda e, g=g:
                          e.tensor_copy(SAVEC[:, 0:NCH],
                                        XR[:].rearrange("p (c t) -> p c t", c=NCH)[:, :, 512 * g - 1]),
                          inc=("V", 1))

                em.wait("tensor", "WQ", wq_done)
                for stream in range(2):
                    for cc in range(NCH):
                        m = stream * 8 + cc
                        bank = PS[cc % 2]
                        em.wait("tensor", "V", em.now("V"))
                        em.wait("tensor", "S", em.now("S"))
                        for k in range(NCH):
                            em.op("tensor", lambda e, m=m, k=k, b=bank:
                                  e.matmul(b[:], WQKV[:, k * 3072 + m * 128:k * 3072 + (m + 1) * 128],
                                           SHBa(k), start=(k == 0), stop=(k == NCH - 1)),
                                  inc=("T", 1) if k == NCH - 1 else None)
                        t_mm = em.now("T")
                        em.wait("scalar", "T", t_mm)
                        em.op("scalar", lambda e, b=bank: e.activation(SQBa, b[:], AF.Square), inc=("S", 1))
                        em.wait("tensor", "S", em.now("S"))
                        em.op("tensor", lambda e, cc=cc:
                              e.matmul(PS[2][0:16, 0:512], PATI[:, (cc // 2) * 16:(cc // 2 + 1) * 16], SQBa,
                                       start=(cc == 0), stop=(cc == NCH - 1)),
                              inc=("T", 1))
                        em.wait("vector", "T", em.now("T"))
                        if stream == 0:
                            em.op("vector", lambda e, cc=cc, b=bank: e.tensor_copy(QGa(cc), b[:]), inc=("V", 1))
                        else:
                            em.op("vector", lambda e, cc=cc, b=bank: e.tensor_copy(UBa(cc), b[:]), inc=("V", 1))
                    em.wait("vector", "T", em.now("T"))
                    em.op("vector", lambda e: e.tensor_scalar_max(RSQA, PS[2][0:16, 0:512], 1e-24), inc=("V", 1))
                    em.op("vector", lambda e: e.reciprocal(RSQA, RSQA), inc=("V", 1))
                    em.wait("scalar", "V", em.now("V"))
                    em.op("scalar", lambda e, sc=(64.0 if stream == 0 else 1.0):
                          e.activation(RSQA, RSQA, AF.Sqrt, scale=sc), inc=("S", 1))
                    em.wait("vector", "S", em.now("S"))
                    RSQB_in = SSB[0:16, 2320:2832]
                    em.op("vector", lambda e: e.tensor_copy(RSQB_in, RSQA), inc=("V", 1))
                    for cc in range(NCH):
                        em.wait("tensor", "V", em.now("V"))
                        em.op("tensor", lambda e, cc=cc:
                              e.matmul(PS[3][:], BCPI[:, (cc // 2) * 128:(cc // 2 + 1) * 128], RSQB_in,
                                       start=True, stop=True),
                              inc=("T", 1))
                        em.wait("vector", "T", em.now("T"))
                        qscol = QKS[:, l * 4 + (2 * stream) + (cc % 2):l * 4 + (2 * stream) + (cc % 2) + 1]
                        if stream == 0:
                            em.op("vector", lambda e, cc=cc, qs=qscol:
                                  e.scalar_tensor_tensor(UBa(cc), QGa(cc), qs, PS[3][:], ALU.mult, ALU.mult),
                                  inc=("V", 1))
                        else:
                            em.op("vector", lambda e, cc=cc, qs=qscol:
                                  e.scalar_tensor_tensor(UBa(cc), UBa(cc), qs, PS[3][:], ALU.mult, ALU.mult),
                                  inc=("V", 1))
                    tC = TABS(0 if stream == 0 else 2)[:, g * 512:(g + 1) * 512]
                    tS = TABS(1 if stream == 0 else 3)[:, g * 512:(g + 1) * 512]
                    for cc in range(NCH):
                        sgn = -1.0 if cc % 2 == 0 else 1.0
                        part = cc + 1 if cc % 2 == 0 else cc - 1
                        em.op("vector", lambda e, cc=cc, tC=tC: e.tensor_tensor(PBUFa, UBa(cc), tC, ALU.mult), inc=None)
                        em.op("vector", lambda e, part=part, sgn=sgn, tS=tS:
                              e.scalar_tensor_tensor(PTBa[:, 0:512], UBa(part), sgn, tS, ALU.mult, ALU.mult), inc=None)
                        if stream == 0:
                            em.op("vector", lambda e, cc=cc: e.tensor_add(QGa(cc), PBUFa, PTBa[:, 0:512]), inc=("V", 1))
                        else:
                            em.op("vector", lambda e, cc=cc, g=g:
                                  e.tensor_add(KSa(cc, (2 * g) % 3), PBUFa[:, 0:256], PTBa[:, 0:256]), inc=None)
                            em.op("vector", lambda e, cc=cc, g=g:
                                  e.tensor_add(KSa(cc, (2 * g + 1) % 3), PBUFa[:, 256:512], PTBa[:, 256:512]), inc=("V", 1))
                for tt in range(4):
                    gslot = (g * 4 + tt) % 6
                    for nh in range(2):
                        bank = PS[(tt * 2 + nh) % 2]
                        em.wait("tensor", "V", em.now("V"))
                        for k in range(NCH):
                            em.op("tensor", lambda e, k=k, tt=tt, nh=nh, b=bank:
                                  e.matmul(b[:], SHBa(k)[:, tt * 128:(tt + 1) * 128],
                                           WQKV[:, k * 3072 + 2048 + nh * 512:k * 3072 + 2048 + (nh + 1) * 512],
                                           start=(k == 0), stop=(k == NCH - 1)),
                                  inc=("T", 1) if k == NCH - 1 else None)
                        em.wait("vector", "T", em.now("T"))
                        em.op("vector", lambda e, gslot=gslot, nh=nh, b=bank:
                              e.tensor_copy(VTa(gslot)[:, nh * 512:(nh + 1) * 512], b[:]),
                              inc=("V", 1))

                if not DBG_NO_ATT:
                    emit_attn_windows(l, g)

                if not DBG_NO_WO:
                    emit_wo_group(l, g)

                if DBG_DUMP3 and l == 0 and g == 0:
                    em.wait("sync", "V", em.now("V"))
                    em.wait("sync", "T", em.now("T"))
                    em.op("sync", lambda e: e.dma_start(bass.AP(park, 0, [[16384, 128], [1, 8]]), XUH), inc=("L", 16))
                    em.op("sync", lambda e: e.dma_start(bass.AP(park, 8, [[16384, 128], [1, 8]]), XHLL), inc=("L", 16))
                    em.op("sync", lambda e: e.dma_start(bass.AP(park, 16, [[16384, 128], [1, 8]]), FBC), inc=("L", 16))
                    em.wait("gpsimd", "V", em.now("V"))
                    em.wait("gpsimd", "T", em.now("T"))
                    for p_ in range(NCH):
                        em.op("gpsimd", lambda e, p_=p_:
                              e.dma_start(bass.AP(park, 32 + p_ * 512, [[16384, 128], [1, 512]]), OUTFa(p_)),
                              inc=("HG", 16))

                if DBG_DUMP2 and l == 0 and g == 0:
                    em.wait("vector", "T", em.now("T"))
                    em.wait("vector", "S", em.now("S"))
                    d2 = [(XR[:, 1536:1552], SUMS[0]), (XR[:, 1552:1568], SUMS[1]),
                          (XR[:, 2080:2096], RCB[:, 0:16]), (XR[:, 2096:2112], SSB[:, 2832:2848]),
                          (XR[0:16, 1568:1824], RTT[0:16, 0:256])]
                    for p_ in range(NCH):
                        d2.append((XR[:, 4096 + p_ * 512:4096 + (p_ + 1) * 512], OUTFa(p_)))
                    for i, (d_, s_) in enumerate(d2):
                        em.op("vector", lambda e, d_=d_, s_=s_: e.tensor_copy(d_, s_),
                              inc=("V", 1) if i == len(d2) - 1 else None)

            if DBG_DUMP and l == 0:
                em.wait("vector", "T", em.now("T"))
                em.wait("vector", "S", em.now("S"))
                dumps = [
                    (XR[:, 0:512], PBUFa),
                    (XR[:, 512:1536], PTBa),
                    (XR[:, 1536:1552], SUMS[0]),
                    (XR[:, 1552:1568], SUMS[1]),
                    (XR[0:16, 1568:1824], RTT[0:16, 0:256]),
                    (XR[:, 1824:2080], PS[3][:, 0:256]),
                    (XR[:, 2080:2096], RCB[:, 0:16]),
                    (XR[:, 2096:2112], SSB[:, 2832:2848]),
                ]
                for p_ in range(NCH):
                    dumps.append((XR[:, 4096 + p_ * 512:4096 + (p_ + 1) * 512], OUTFa(p_)))
                    dumps.append((XR[:, 8192 + p_ * 512:8192 + (p_ + 1) * 512], QGa(p_)))
                    dumps.append((XR[:, 12288 + p_ * 256:12288 + (p_ + 1) * 256], KSa(p_, 0)))
                dumps.append((XR[:, 14336:15360], VTa(0)))
                dumps.append((XR[:, 15360:16384], VTa(1)))
                for i, (d_, s_) in enumerate(dumps):
                    em.op("vector", lambda e, d_=d_, s_=s_: e.tensor_copy(d_, s_),
                          inc=("V", 1) if i == len(dumps) - 1 else None)

        def emit_attn_windows(l, g):
            for w in (2 * g, 2 * g + 1):
                if w == 0:
                    continue
                ps_slot = (w - 1) % 3
                cs_slot = w % 3
                pv_slots = [((w - 1) * 2) % 6, ((w - 1) * 2 + 1) % 6, (w * 2) % 6, (w * 2 + 1) % 6]
                wcol = (w % 2) * 256
                mbase = 2 if w == 2 else 0
                for p in range(NCH):
                    for hh in range(2):
                        h = 2 * p + hh
                        gq = h // 4
                        rb = (h % 4) * 32
                        for qt in range(2):
                            em.wait("tensor", "S", em.now("S"))
                            em.wait("tensor", "V", em.now("V"))
                            first = True
                            for half in range(2):
                                chunk = 2 * gq + half
                                lq = QGa(chunk)[rb:rb + 32, wcol + qt * 128:wcol + qt * 128 + 128]
                                tpk = {"tile_position": (96, 0)} if rb == 96 else {}
                                for side, ksl in ((0, ps_slot), (1, cs_slot)):
                                    em.op("tensor", lambda e, chunk=chunk, rb=rb, lq=lq, ksl=ksl, side=side, half=half, tpk=tpk:
                                          e.matmul(PS[4][:, side * 256:(side + 1) * 256],
                                                   lq, KSa(chunk, ksl)[rb:rb + 32, :],
                                                   start=(half == 0 and side == 0), stop=(half == 1 and side == 1), **tpk),
                                          inc=("T", 1) if (half == 1 and side == 1) else None)
                            em.wait("vector", "T", em.now("T"))
                            em.op("vector", lambda e: e.tensor_reduce(NMX, PS[4][:, 0:512], AX.X, ALU.max, negate=True),
                                  inc=("V", 1))
                            em.wait("scalar", "V", em.now("V"))
                            em.op("scalar", lambda e: e.activation(PBUFa, PS[4][:, 0:512], AF.Exp, bias=NMX), inc=("S", 1))
                            em.wait("vector", "S", em.now("S"))
                            em.op("vector", lambda e, qt=qt, h=h, mbase=mbase:
                                  e.scalar_tensor_tensor(PBUFa, PBUFa, 1.0, MSKA(mbase + qt), ALU.mult, ALU.mult,
                                                         accum_out=SUMS[qt][:, h:h + 1]),
                                  inc=("V", 1))
                            em.wait("tensor", "V", em.now("V"))
                            em.wait("tensor", "S", em.now("S"))
                            for kc in range(4):
                                em.op("tensor", lambda e, kc=kc:
                                      e.transpose(PB[:, kc * 128:(kc + 1) * 128], PBUFa[:, kc * 128:(kc + 1) * 128], IDNB[:]),
                                      inc=("T", 1) if kc == 3 else None)
                            em.wait("scalar", "T", em.now("T"))
                            for kc in range(4):
                                em.op("scalar", lambda e, kc=kc, qt=qt:
                                      e.copy(PTBa[:, kc * 256 + qt * 128:kc * 256 + qt * 128 + 128], PB[:, kc * 128:(kc + 1) * 128]),
                                      inc=("S", 1) if kc == 3 else None)
                        em.wait("tensor", "S", em.now("S"))
                        for kc in range(4):
                            vsl = pv_slots[kc]
                            em.op("tensor", lambda e, kc=kc, vsl=vsl, h=h, hh=hh:
                                  e.matmul(PS[5][hh * 64:(hh + 1) * 64, 0:256], VTa(vsl)[:, h * 64:(h + 1) * 64],
                                           PTBa[:, kc * 256:(kc + 1) * 256], start=(kc == 0), stop=(kc == 3)),
                                  inc=("T", 1) if kc == 3 else None)
                    em.wait("vector", "T", em.now("T"))
                    em.op("vector", lambda e, p=p, wcol=wcol:
                          e.tensor_copy(OUTFa(p)[:, wcol:wcol + 256], PS[5][:, 0:256]),
                          inc=("V", 1))
                em.wait("scalar", "V", em.now("V"))
                em.op("scalar", lambda e: e.activation(RCF[:, 0:16], SUMS[0], AF.Ln), inc=None)
                em.op("scalar", lambda e: e.activation(RCF[:, 0:16], RCF[:, 0:16], AF.Exp, scale=-1.0), inc=None)
                em.op("scalar", lambda e: e.activation(RCF2[:, 0:16], SUMS[1], AF.Ln), inc=None)
                em.op("scalar", lambda e: e.activation(RCF2[:, 0:16], RCF2[:, 0:16], AF.Exp, scale=-1.0), inc=("S", 1))
                em.wait("vector", "S", em.now("S"))
                em.op("vector", lambda e: e.tensor_copy(RCB[:, 0:16], RCF[:, 0:16]), inc=None)
                em.op("vector", lambda e: e.tensor_copy(SSB[:, 2832:2848], RCF2[:, 0:16]), inc=("V", 1))
                em.wait("tensor", "V", em.now("V"))
                em.op("tensor", lambda e: e.transpose(PB[0:16, 0:128], RCB[:, 0:16], IDNB[:]), inc=None)
                em.op("tensor", lambda e: e.transpose(PB[0:16, 128:256], SSB[:, 2832:2848], IDNB[:]), inc=("T", 1))
                em.wait("scalar", "T", em.now("T"))
                em.op("scalar", lambda e: e.copy(RTT[0:16, 0:128], PB[0:16, 0:128]), inc=None)
                em.op("scalar", lambda e: e.copy(RTT[0:16, 128:256], PB[0:16, 128:256]), inc=("S", 1))
                for p in range(NCH):
                    em.wait("tensor", "S", em.now("S"))
                    em.wait("tensor", "V", em.now("V"))
                    em.op("tensor", lambda e, p=p:
                          e.matmul(PS[3][:, 0:256], BCPD[:, p * 128:(p + 1) * 128], RTT[0:16, 0:256],
                                   start=True, stop=True),
                          inc=("T", 1))
                    em.wait("vector", "T", em.now("T"))
                    em.op("vector", lambda e, p=p, wcol=wcol:
                          e.tensor_tensor(OUTFa(p)[:, wcol:wcol + 256], OUTFa(p)[:, wcol:wcol + 256],
                                          PS[3][:, 0:256], ALU.mult),
                          inc=("V", 1))

        def emit_wo_group(l, g):
            em.wait("gpsimd", "V", em.now("V"))
            for k in range(NCH):
                em.op("gpsimd", lambda e, k=k, l=l:
                      e.dma_start(WOs(k), in_Wo[l, k * 128:(k + 1) * 128, :]),
                      inc=("WO", 16))
            em.wait("tensor", "WO", em.now("WO"))
            ntok = 256 if g == 0 else 512
            ocol = 256 if g == 0 else 0
            for m in range(NCH):
                bank = PS[m % 2]
                em.wait("tensor", "V", em.now("V"))
                for k in range(NCH):
                    em.op("tensor", lambda e, m=m, k=k, b=bank, ocol=ocol, n=ntok:
                          e.matmul(b[:, 0:n], WOs(k)[:, m * 128:(m + 1) * 128],
                                   OUTFa(k)[:, ocol:ocol + n], start=(k == 0), stop=(k == NCH - 1)),
                          inc=("T", 1) if k == NCH - 1 else None)
                em.wait("vector", "T", em.now("T"))
                if g == 0:
                    em.op("vector", lambda e, m=m, b=bank:
                          e.tensor_tensor(XUH[:, m:m + 1], b[:, 255:256], XHLL[:, m:m + 1], ALU.add),
                          inc=("V", 1))
                else:
                    base = 512 * (g - 1)
                    em.op("vector", lambda e, m=m, b=bank, base=base:
                          e.tensor_tensor(XR[:, m * T_OWN + base:m * T_OWN + base + 512], b[:, 0:512],
                                          XR[:, m * T_OWN + base:m * T_OWN + base + 512], ALU.add),
                          inc=("V", 1))

        def emit_ffn_stage(l):
            em.wait("gpsimd", "T", em.now("T"))
            if not (DBG_FFN_XN_ONLY or DBG_FFN_NO_W2):
                for j in range(NPAIR):
                    em.op("gpsimd", lambda e, j=j, l=l:
                          e.dma_start(W2Sa(j), in_W2[l, j * 128:(j + 1) * 128, :]),
                          inc=("W2L", 16))
            w2_done = em.now("W2L")
            t_pair_done = {}
            for th in range(2):
                tb = th * 1024
                for fg in range(4):
                    t0 = tb + fg * 256
                    em.wait("vector", "T", em.now("T"))
                    emit_shf(l, t0)
                    slices = [SHF[:, k * 256:(k + 1) * 256] for k in range(NCH)]
                    ln_stats_rolling(slices, 256)
                    for k in range(NCH):
                        gcol = LNP[:, l * 16 + k:l * 16 + k + 1]
                        bcol = LNP[:, l * 16 + 8 + k:l * 16 + 8 + k + 1]
                        em.op("vector", lambda e, k=k: e.tensor_tensor(SHF[:, k * 256:(k + 1) * 256],
                                                                      SHF[:, k * 256:(k + 1) * 256], PS[6][:, 0:256], ALU.mult), inc=None)
                        em.op("vector", lambda e, k=k: e.tensor_tensor(SHF[:, k * 256:(k + 1) * 256],
                                                                      SHF[:, k * 256:(k + 1) * 256], PS[3][:, 0:256], ALU.add), inc=None)
                        em.op("vector", lambda e, k=k, t0=t0, tb=tb, g_=gcol, b_=bcol:
                              e.scalar_tensor_tensor(XNa(k)[:, t0 - tb:t0 - tb + 256],
                                                     SHF[:, k * 256:(k + 1) * 256], g_,
                                                     b_.to_broadcast((128, 256)), ALU.mult, ALU.add),
                              inc=("V", 1) if k == NCH - 1 else None)
                if th == 0:
                    em.op("vector", lambda e:
                          e.tensor_copy(SAVE2[:, 0:NCH],
                                        XR[:].rearrange("p (c t) -> p c t", c=NCH)[:, :, 1023]),
                          inc=("V", 1))
                for j in range(NPAIR if not DBG_FFN_XN_ONLY else 0):
                    buf = j % 2
                    gp = th * NPAIR + j
                    if gp >= 2:
                        em.wait("gpsimd", "T", t_pair_done[gp - 2])
                    em.op("gpsimd", lambda e, j=j, l=l, buf=buf:
                          e.dma_start(W1Ba(buf), in_W1[l].rearrange("(kk p) n -> p kk n", p=128)[:, :, j * 256:(j + 1) * 256]),
                          inc=("W10" if buf == 0 else "W11", 16))
                    em.wait("tensor", "W10" if buf == 0 else "W11", em.now("W10" if buf == 0 else "W11"))
                    for nt in range(2):
                        em.wait("tensor", "V", em.now("V"))
                        em.wait("tensor", "S", em.now("S"))
                        for k in range(NCH):
                            em.op("tensor", lambda e, k=k, nt=nt, buf=buf:
                                  e.matmul(PS[0][:], W1Ba(buf)[:, k * 256:k * 256 + 128],
                                           XNa(k)[:, nt * 512:(nt + 1) * 512], start=(k == 0), stop=(k == NCH - 1)),
                                  inc=None)
                            em.op("tensor", lambda e, k=k, nt=nt, buf=buf:
                                  e.matmul(PS[1][:], W1Ba(buf)[:, k * 256 + 128:k * 256 + 256],
                                           XNa(k)[:, nt * 512:(nt + 1) * 512], start=(k == 0), stop=(k == NCH - 1)),
                                  inc=("T", 1) if k == NCH - 1 else None)
                        em.wait("scalar", "T", em.now("T"))
                        b1g = B1P[:, l * 44 + 2 * j + 1:l * 44 + 2 * j + 2]
                        b1a = B1P[:, l * 44 + 2 * j:l * 44 + 2 * j + 1]
                        em.op("scalar", lambda e, b=b1g: e.activation(GAa, PS[1][:], AF.Gelu, bias=b), inc=("S", 1))
                        em.wait("vector", "S", em.now("S"))
                        em.op("vector", lambda e, j=j, nt=nt, b=b1a:
                              e.scalar_tensor_tensor(H2Ha(j)[:, nt * 512:(nt + 1) * 512], PS[0][:], b, GAa,
                                                     ALU.add, ALU.mult),
                              inc=("V", 1))
                    t_pair_done[gp] = em.now("T")
                em.wait("tensor", "W2L", w2_done)
                for tg in range(4 if not (DBG_FFN_XN_ONLY or DBG_FFN_NO_W2) else 0):
                    tc = tg * 256
                    em.wait("tensor", "V", em.now("V"))
                    for j in range(NPAIR):
                        for m2 in range(NCH):
                            bank = PS[2 + m2 // 2]
                            colb = (m2 % 2) * 256
                            em.op("tensor", lambda e, j=j, m2=m2, bank=bank, colb=colb, tc=tc:
                                  e.matmul(bank[:, colb:colb + 256], W2Sa(j)[:, m2 * 128:(m2 + 1) * 128],
                                           H2Ha(j)[:, tc:tc + 256],
                                           start=(j == 0 and m2 % 2 == 0), stop=(j == NPAIR - 1 and m2 % 2 == 1)),
                                  inc=("T", 1) if (j == NPAIR - 1 and m2 == NCH - 1) else None)
                    em.wait("vector", "T", em.now("T"))
                    for m2 in range(NCH):
                        bank = PS[2 + m2 // 2]
                        colb = (m2 % 2) * 256
                        b2col = B2P[:, l * 8 + m2:l * 8 + m2 + 1]
                        em.op("vector", lambda e, m2=m2, bank=bank, colb=colb, tc=tc, tb=tb, b2c=b2col:
                              e.scalar_tensor_tensor(XR[:, m2 * T_OWN + tb + tc:m2 * T_OWN + tb + tc + 256],
                                                     bank[:, colb:colb + 256], b2c,
                                                     XR[:, m2 * T_OWN + tb + tc:m2 * T_OWN + tb + tc + 256],
                                                     ALU.add, ALU.add),
                              inc=("V", 1) if m2 == NCH - 1 else None)

        def emit_shf(l, t0):
            for k in range(NCH):
                if k < 4:
                    em.op("vector", lambda e, k=k, t0=t0:
                          e.tensor_copy(SHF[:, k * 256:(k + 1) * 256],
                                        XR[:, k * T_OWN + t0:k * T_OWN + t0 + 256]),
                          inc=("V", 1) if k == NCH - 1 else None)
                else:
                    if t0 == 0:
                        em.op("vector", lambda e, k=k:
                              e.tensor_scalar_mul(SHF[:, k * 256:k * 256 + 1], XUH[:, k:k + 1], FLG[:, 0:1]),
                              inc=None)
                        em.op("vector", lambda e, k=k:
                              e.tensor_copy(SHF[:, k * 256 + 1:(k + 1) * 256],
                                            XR[:, k * T_OWN:k * T_OWN + 255]),
                              inc=("V", 1) if k == NCH - 1 else None)
                    elif t0 == 1024:
                        em.op("vector", lambda e, k=k:
                              e.tensor_copy(SHF[:, k * 256:k * 256 + 1], SAVE2[:, k:k + 1]),
                              inc=None)
                        em.op("vector", lambda e, k=k, t0=t0:
                              e.tensor_copy(SHF[:, k * 256 + 1:(k + 1) * 256],
                                            XR[:, k * T_OWN + t0:k * T_OWN + t0 + 255]),
                              inc=("V", 1) if k == NCH - 1 else None)
                    else:
                        em.op("vector", lambda e, k=k, t0=t0:
                              e.tensor_copy(SHF[:, k * 256:(k + 1) * 256],
                                            XR[:, k * T_OWN + t0 - 1:k * T_OWN + t0 + 255]),
                              inc=("V", 1) if k == NCH - 1 else None)

        def emit_layer(l):
            emit_attn_stage(l)
            if not DBG_NO_FFN:
                emit_ffn_stage(l)
            if l < DEPTH - 1:
                em.wait("sync", "V", em.now("V"))
                hsrc2 = XR[:].rearrange("p (c t) -> p c t", c=NCH)[:, :, T_OWN - 513:T_OWN]
                dma("sync", hin.ap().rearrange("p (c t) -> p c t", c=NCH), hsrc2, "L")
                em.wait("gpsimd", "L", em.now("L"))
                em.op("gpsimd", lambda e: e.collective_compute(
                    "AllGather", ALU.bypass, replica_groups=[[0, 1, 2, 3, 4, 5, 6, 7]],
                    ins=[hin.ap().opt()], outs=[hout.ap().opt()]), inc=("C", 1))

        def emit_final():
            for fg in range(8):
                t0 = fg * 256
                em.wait("sync", "V", em.now("V"))
                em.op("sync", lambda e, t0=t0:
                      e.dma_start(PFG.rearrange("p (c t) -> p c t", c=NCH),
                                  park.ap().rearrange("p (c t) -> p c t", c=NCH)[:, :, t0:t0 + 256]),
                      inc=("PK", 16))
                em.wait("vector", "PK", em.now("PK"))
                for k in range(NCH):
                    em.op("vector", lambda e, k=k, t0=t0:
                          e.tensor_tensor(PFG[:, k * 256:(k + 1) * 256], PFG[:, k * 256:(k + 1) * 256],
                                          XR[:, k * T_OWN + t0:k * T_OWN + t0 + 256], ALU.add),
                          inc=("V", 1) if k == NCH - 1 else None)
                slices = [PFG[:, k * 256:(k + 1) * 256] for k in range(NCH)]
                ln_stats_rolling(slices, 256)
                for k in range(NCH):
                    em.op("vector", lambda e, k=k, t0=t0:
                          e.tensor_tensor(PFG[:, k * 256:(k + 1) * 256], PFG[:, k * 256:(k + 1) * 256],
                                          PS[6][:, 0:256], ALU.mult),
                          inc=None)
                    em.op("vector", lambda e, k=k, t0=t0:
                          e.tensor_tensor(XR[:, k * T_OWN + t0:k * T_OWN + t0 + 256],
                                          PFG[:, k * 256:(k + 1) * 256], PS[3][:, 0:256], ALU.add),
                          inc=("V", 1) if k == NCH - 1 else None)

        for l in range(N_LAYERS):
            emit_layer(l)
        if N_LAYERS == DEPTH and not OUTPUT_X1:
            emit_final()

        for tt in range(16):
            for half in range(2):
                em.wait("tensor", "V", em.now("V"))
                em.wait("tensor", "O", em.now("O"))
                buf = XIN[half]
                for kk in range(4):
                    k = half * 4 + kk
                    bank = PS[kk % 2]
                    if kk >= 2:
                        em.wait("tensor", "V", em.now("V") - 1)
                    em.op("tensor", lambda e, b=bank, tt=tt, k=k:
                          e.transpose(b[:, 0:128], XR[:, k * T_OWN + tt * 128:k * T_OWN + (tt + 1) * 128], IDNF[:]),
                          inc=("T", 1))
                    em.wait("vector", "T", em.now("T"))
                    em.op("vector", lambda e, b=bank, kk=kk, buf=buf:
                          e.tensor_copy(buf[:, kk * 128:(kk + 1) * 128], b[:, 0:128]),
                          inc=("V", 1))
                em.wait("sync", "V", em.now("V"))
                em.op("sync", lambda e, tt=tt, half=half, buf=buf:
                      e.dma_start(out_tm[tt * 128:(tt + 1) * 128, half * 512:(half + 1) * 512], buf[:]),
                      inc=("O", 16))
        em.wait("sync", "O", em.now("O"))
        em.wait("sync", "L", em.now("L"))
        em.wait("sync", "G", em.now("G"))

        em.check_deadlock()
        sems = {name: ctx.enter_context(nc.semaphore(f"sem{name}"))
                for name in ("L", "G", "T", "V", "S", "C", "O", "X0", "X1",
                             "WQ", "WO", "W10", "W11", "W2L", "HS", "HG", "PK")}
        with nc.Block() as block:
            em.emit_into(block, sems)

    return nc


def _stage_inputs(inputs):
    f32 = lambda a: np.ascontiguousarray(np.asarray(a, dtype=np.float32))
    x = f32(inputs["L0_attn_x"])
    Wqkv = f32(inputs["L0_ffn_Wqkv"])
    qsc = f32(inputs["L0_ffn_q_scale"]); ksc = f32(inputs["L0_ffn_k_scale"])
    Wo = f32(inputs["L0_ffn_Wo"]); lng = f32(inputs["L0_ffn_ln_g"]); lnb = f32(inputs["L0_ffn_ln_b"])
    W1 = f32(inputs["L0_ffn_W1"]); b1 = f32(inputs["L0_ffn_b1"])
    W2 = f32(inputs["L0_ffn_W2"]); b2 = f32(inputs["L0_ffn_b2"])

    WqkvP = np.ascontiguousarray(np.concatenate(
        [Wqkv[:, :, 0:1024][:, :, PERM], Wqkv[:, :, 1024:2048][:, :, PERM], Wqkv[:, :, 2048:3072]], axis=2))
    W1p = np.zeros((DEPTH, D, D1P), np.float32)
    W1p[:, :, W1VALID] = W1[:, :, W1COLS[W1VALID]]
    b1pv = np.zeros((DEPTH, D1P), np.float32)
    b1pv[:, W1VALID] = b1[:, W1COLS[W1VALID]]
    W2p = np.zeros((DEPTH, NPAIR * 128, D), np.float32)
    for j in range(NPAIR):
        s0, e0 = j * 128, min(j * 128 + 128, 2730)
        W2p[:, j * 128:j * 128 + (e0 - s0), :] = W2[:, s0:e0, :]

    b1cols = np.zeros((128, DEPTH, 44), np.float32)
    for l in range(DEPTH):
        for j in range(NPAIR):
            b1cols[:, l, 2 * j] = b1pv[l, j * 256:j * 256 + 128]
            b1cols[:, l, 2 * j + 1] = b1pv[l, j * 256 + 128:j * 256 + 256]
    b2cols = np.zeros((128, DEPTH, 8), np.float32)
    for l in range(DEPTH):
        b2cols[:, l, :] = b2[l].reshape(8, 128).T
    lnp = np.zeros((128, DEPTH, 16), np.float32)
    for l in range(DEPTH):
        lnp[:, l, 0:8] = lng[l].reshape(8, 128).T
        lnp[:, l, 8:16] = lnb[l].reshape(8, 128).T
    qks = np.zeros((128, DEPTH, 4), np.float32)
    for l in range(DEPTH):
        qks[:, l, 0] = np.tile(qsc[l, 0:32], 4)
        qks[:, l, 1] = np.tile(qsc[l, 32:64], 4)
        qks[:, l, 2] = np.tile(ksc[l, 0:32], 4)
        qks[:, l, 3] = np.tile(ksc[l, 32:64], 4)

    in_maps = []
    for c in range(8):
        b, hh = c // 2, c % 2
        Cq, Sq, Ck, Sk = host_rotary_tables(hh)
        tabsT = np.zeros((128, 4 * TLOC), np.float32)
        for ti, tab in enumerate((Cq, Sq, Ck, Sk)):
            tabsT[:, ti * TLOC:(ti + 1) * TLOC] = np.tile(tab.T, (4, 1))
        flg = np.zeros((128, 4), np.float32)
        flg[:, 0] = 0.0 if hh == 0 else 1.0
        flg[:, 1] = 1.0 if hh == 0 else 0.0
        m = host_masks(hh)
        im = {
            "x_tm": np.ascontiguousarray(x[b, hh * T_OWN:(hh + 1) * T_OWN, :]),
            "tabs": tabsT,
            "masks": np.ascontiguousarray(m),
            "flags": flg,
            "qks": np.ascontiguousarray(qks.reshape(128, DEPTH * 4)),
            "lnp": np.ascontiguousarray(lnp.reshape(128, DEPTH * 16)),
            "b1p": np.ascontiguousarray(b1cols.reshape(128, DEPTH * 44)),
            "b2p": np.ascontiguousarray(b2cols.reshape(128, DEPTH * 8)),
            "dn_Wq": f32(inputs["L0_attn_Wq"]),
            "dn_Wk": f32(inputs["L0_attn_Wk"]),
            "dn_Wv": f32(inputs["L0_attn_Wv"]),
            "dn_Wb": f32(inputs["L0_attn_Wbeta"]),
            "dn_Wo": f32(inputs["L0_attn_Wout"]),
            "in_Wqkv": WqkvP,
            "in_Wo": Wo,
            "in_W1": W1p,
            "in_W2": W2p,
        }
        in_maps.append(im)
    return in_maps


def kernel(**inputs):
    in_maps = _stage_inputs(inputs)
    nc = build_program()
    res = run_bass_kernel_spmd(nc, in_maps, core_ids=list(range(8)))
    out = np.zeros((4, 4096, 1024), np.float32)
    for c in range(8):
        b, hh = c // 2, c % 2
        out[b, hh * T_OWN:(hh + 1) * T_OWN, :] = res.results[c]["out_tm"]
    return out


# version 05
# speedup vs baseline: 1.0483x; 1.0483x over previous
import numpy as np
import ml_dtypes
import concourse.bass as bass
import concourse.mybir as mybir
from concourse.bass_utils import run_bass_kernel_spmd

F32 = mybir.dt.float32
BF16 = mybir.dt.bfloat16
AF = mybir.ActivationFunctionType
ALU = mybir.AluOpType
AX = mybir.AxisListType
bft = ml_dtypes.bfloat16

H, DM = 16, 64
D = 1024
WIN = 256
T_OWN = 2048
HALO = 512
NCH = 8
DEPTH = 6
D1P = 5632
NPAIR = 22
EPS = 1e-5
TLOC = HALO + T_OWN

ENGINES = ("sync", "gpsimd", "scalar", "vector", "tensor")

N_LAYERS = 6
OUTPUT_X1 = False
DBG_NO_WO = False
DBG_NO_FFN = False
DBG_NO_ATT = False
DBG_FFN_XN_ONLY = False
DBG_FFN_NO_W2 = False
DBG_DUMP = False
DBG_DUMP2 = False
DBG_DUMP3 = False
SCAN_CHUNKS = list(range(15, -1, -1))


def qk_perm():
    p = []
    for g in range(4):
        for half in range(2):
            for hh in range(4):
                head = 4 * g + hh
                p.extend(range(head * 64 + half * 32, head * 64 + half * 32 + 32))
    return np.array(p)

PERM = qk_perm()

def w1_cols():
    cols = np.zeros(D1P, dtype=np.int64)
    valid = np.zeros(D1P, dtype=bool)
    o = 0
    for j in range(NPAIR):
        s0, e0 = j * 128, min(j * 128 + 128, 2730)
        n = e0 - s0
        cols[o:o + n] = np.arange(s0, e0); valid[o:o + n] = True
        o += 128
        cols[o:o + n] = 2730 + np.arange(s0, e0); valid[o:o + n] = True
        o += 128
    return cols, valid

W1COLS, W1VALID = w1_cols()

def host_rotary_tables(h):
    half = np.arange(0, DM, 2, dtype=np.float64)
    inv_freq = 1.0 / 10000.0 ** (half / DM)
    base_scale = (half + 0.4 * DM) / (1.4 * DM)
    start = h * T_OWN - HALO
    pos = np.clip(np.arange(start, start + TLOC), 0, 4095).astype(np.float64)
    fr = pos[:, None] * inv_freq[None, :]
    power = (pos - 2048.0) / (WIN // 2)
    scale = base_scale[None, :] ** power[:, None]
    c, s = np.cos(fr), np.sin(fr)
    return (np.asarray(c * scale, np.float32), np.asarray(s * scale, np.float32),
            np.asarray(c / scale, np.float32), np.asarray(s / scale, np.float32))

def host_masks(h):
    i = np.arange(128)
    j = np.arange(512)
    tiles = []
    for qt in range(2):
        qq = (i + qt * 128)[:, None] + 256
        keep = (j[None, :] >= qq - WIN) & (j[None, :] <= qq)
        tiles.append(keep.astype(np.float32))
    for qt in range(2):
        qq = (i + qt * 128)[:, None] + 256
        keep = (j[None, :] >= qq - WIN) & (j[None, :] <= qq)
        if h == 0:
            keep = keep & (j[None, :] >= 256)
        tiles.append(keep.astype(np.float32))
    return np.concatenate(tiles, axis=1).astype(bft)


class Em:
    def __init__(self):
        self.q = {e: [] for e in ENGINES}
        self.cnt = {}

    def now(self, sem):
        return self.cnt.get(sem, 0)

    def op(self, eng, fn, inc=None, desc=""):
        if inc is not None:
            self.cnt[inc[0]] = self.cnt.get(inc[0], 0) + inc[1]
        self.q[eng].append(("op", fn, inc, desc))
        return self.cnt.get(inc[0], 0) if inc else None

    def wait(self, eng, sem, val, desc=""):
        if val is None or val <= 0:
            return
        q = self.q[eng]
        if q and q[-1][0] == "wait" and q[-1][1] == sem:
            if q[-1][2] >= val:
                return
            q.pop()
        q.append(("wait", sem, val, desc))

    def check_deadlock(self):
        ptr = {e: 0 for e in ENGINES}
        val = {}
        total = sum(len(v) for v in self.q.values())
        done = 0
        progress = True
        while progress:
            progress = False
            for e in ENGINES:
                while ptr[e] < len(self.q[e]):
                    it = self.q[e][ptr[e]]
                    if it[0] == "wait":
                        if val.get(it[1], 0) < it[2]:
                            break
                    else:
                        if it[2] is not None:
                            val[it[2][0]] = val.get(it[2][0], 0) + it[2][1]
                    ptr[e] += 1
                    done += 1
                    progress = True
        if done != total:
            lines = [f"  {e}@{ptr[e]}/{len(self.q[e])}: {self.q[e][ptr[e]][:3]}"
                     for e in ENGINES if ptr[e] < len(self.q[e])]
            raise RuntimeError("DEADLOCK in schedule:\n" + "\n".join(lines))

    def emit_into(self, block, sems):
        for e in ENGINES:
            items = self.q[e]
            def body(eng, items=items):
                for it in items:
                    if it[0] == "wait":
                        eng.wait_ge(sems[it[1]], it[2])
                    else:
                        ins = it[1](eng)
                        if it[2] is not None:
                            assert ins is not None, f"inc on None: {it[3]}"
                            ins.then_inc(sems[it[2][0]], it[2][1])
            getattr(block, e)(body)


def build_program():
    nc = bass.Bass("TRN2", target_bir_lowering=False, detect_race_conditions=False)

    def din(name, shape, dt=F32):
        return nc.dram_tensor(name, shape, dt, kind="ExternalInput").ap()

    x_tm = din("x_tm", [T_OWN, D])
    tabs = din("tabs", [128, 4 * TLOC])
    masks_d = nc.dram_tensor("masks", [128, 4 * 512], BF16, kind="ExternalInput").ap()
    flags_d = din("flags", [128, 4])
    qks_d = din("qks", [128, DEPTH * 4])
    lnp_d = din("lnp", [128, DEPTH * 16])
    b1p_d = din("b1p", [128, DEPTH * 44])
    b2p_d = din("b2p", [128, DEPTH * 8])
    dn_Wq = din("dn_Wq", [D, D]); dn_Wk = din("dn_Wk", [D, D])
    dn_Wv = din("dn_Wv", [D, D]); dn_Wb = din("dn_Wb", [D, 16])
    dn_Wo = din("dn_Wo", [D, D])
    in_Wqkv = din("in_Wqkv", [DEPTH, D, 3 * D])
    in_Wo = din("in_Wo", [DEPTH, D, D])
    in_W1 = din("in_W1", [DEPTH, D, D1P])
    in_W2 = din("in_W2", [DEPTH, NPAIR * 128, D])

    out_tm = nc.dram_tensor("out_tm", [T_OWN, D], F32, kind="ExternalOutput").ap()

    park = nc.dram_tensor("park", [128, NCH * T_OWN], F32)
    cin = nc.dram_tensor("cin", [64, H * 64], F32)
    cout = nc.dram_tensor("cout", [8 * 64, H * 64], F32)
    hin = nc.dram_tensor("hin", [128, NCH * 513], F32)
    hout = nc.dram_tensor("hout", [8 * 128, NCH * 513], F32)
    hloc = nc.dram_tensor("hloc", [128, NCH * 513], F32)

    idn_b_d = nc.inline_tensor(np.eye(128, dtype=bft), name="c_idnb").ap()
    idn_f_d = nc.inline_tensor(np.eye(128, dtype=np.float32), name="c_idnf").ap()
    pat_dn = np.zeros((128, NCH * 16), dtype=bft)
    for c in range(NCH):
        pat_dn[0:64, c * 16 + 2 * c] = 1
        pat_dn[64:128, c * 16 + 2 * c + 1] = 1
    pat_dn_d = nc.inline_tensor(pat_dn, name="c_patdn").ap()
    bcp_dn = np.zeros((16, NCH * 128), dtype=bft)
    for c in range(NCH):
        bcp_dn[2 * c, c * 128 + 0:c * 128 + 64] = 1
        bcp_dn[2 * c + 1, c * 128 + 64:c * 128 + 128] = 1
    bcp_dn_d = nc.inline_tensor(bcp_dn, name="c_bcpdn").ap()
    sel = np.zeros((128, 128), dtype=bft)
    sel[0:64, 0:64] = np.eye(64)
    sel[64:128, 64:128] = np.eye(64)
    sel_d = nc.inline_tensor(sel, name="c_sel2").ap()
    um = (np.arange(128)[:, None] >= np.arange(128)[None, :]).astype(bft)
    um_d = nc.inline_tensor(um, name="c_um").ap()
    pat_in = np.zeros((128, 4 * 16), dtype=bft)
    for g in range(4):
        for j in range(4):
            pat_in[j * 32:(j + 1) * 32, g * 16 + 4 * g + j] = 1
    pat_in_d = nc.inline_tensor(pat_in, name="c_patin").ap()
    bcp_in = np.zeros((16, 4 * 128), dtype=bft)
    for g in range(4):
        for j in range(4):
            bcp_in[4 * g + j, g * 128 + j * 32:g * 128 + (j + 1) * 32] = 1
    bcp_in_d = nc.inline_tensor(bcp_in, name="c_bcpin").ap()
    ones_col_d = nc.inline_tensor(np.ones((128, 2), dtype=bft), name="c_onesc").ap()
    ones_row_d = nc.inline_tensor(np.ones((1, 128), dtype=bft), name="c_onesr").ap()

    em = Em()
    from contextlib import ExitStack
    with ExitStack() as ctx:
        sbt = lambda n, s, d: ctx.enter_context(nc.sbuf_tensor(n, s, d))
        ptt = lambda n, s, d: ctx.enter_context(nc.psum_tensor(n, s, d))

        XR = sbt("XR", [128, NCH * T_OWN], F32)
        IDNB = sbt("IDNB", [128, 128], BF16)
        IDNF = sbt("IDNF", [128, 128], F32)
        PATD = sbt("PATD", [128, NCH * 16], BF16)
        BCPD = sbt("BCPD", [16, NCH * 128], BF16)
        PATI = sbt("PATI", [128, 64], BF16)
        BCPI = sbt("BCPI", [16, 512], BF16)
        SEL2 = sbt("SEL2", [128, 128], BF16)
        UM = sbt("UM", [128, 128], BF16)
        FLG = sbt("FLG", [128, 4], F32)
        QKS = sbt("QKS", [128, DEPTH * 4], F32)
        LNP = sbt("LNP", [128, DEPTH * 16], F32)
        B1P = sbt("B1P", [128, DEPTH * 44], F32)
        B2P = sbt("B2P", [128, DEPTH * 8], F32)
        ONEC = sbt("ONEC", [128, 2], BF16)
        ONER = sbt("ONER", [1, 128], BF16)

        AB = sbt("AB", [128, 61600], BF16)
        ARF = sbt("ARF", [128, 2048], F32)
        SS = sbt("SS", [128, 512], F32)
        SSB = sbt("SSB", [128, 3200], BF16)

        PS = [ptt(f"ps{i}", [128, 512], F32) for i in range(7)]
        PB = ptt("psb", [128, 1024], BF16)

        QHAT = AB[:, 0:16384]
        KHAT = AB[:, 16384:32768]
        VFM = AB[:, 32768:49152]
        WB = AB[:, 49152:57344]
        X8 = AB[:, 57344:61440]
        VTM = AB[:, 57344:58368]
        KTM = AB[:, 58368:59392]
        HM = AB[:, 59392:59904]
        BBS = AB[:, 59904:60416]
        OUTB = AB[:, 60416:61440]
        WTS = AB[:, 61440:61568]
        PRJ = AB[:, 32768:33280]
        XB1 = AB[:, 57344:58368]
        XB2 = AB[:, 58368:59392]
        BETA = SSB[0:16, 0:2048]
        S_SB = SSB[0:64, 2048:3072]
        WBETA = SSB[:, 3072:3200]
        RSQ = SS[0:16, 0:512]
        MUs = SS[0:1, 0:128]
        MSQ = SS[0:1, 256:384]
        TMP1 = SS[0:1, 384:512]
        XIN = [ARF[:, 0:512], ARF[:, 512:1024]]
        CARF = ARF[0:64, 0:1024]
        S_F32 = ARF[0:64, 0:1024]
        XSG = ARF[:, 1024:2048]
        ZSC = SS[:, 128:256]

        def dma(engq, dst, src, sem):
            em.op(engq, lambda e, dst=dst, src=src: e.dma_start(dst, src), inc=(sem, 16))

        for dst, src in ((IDNB[:], idn_b_d), (IDNF[:], idn_f_d), (PATD[:], pat_dn_d),
                         (BCPD[:], bcp_dn_d), (PATI[:], pat_in_d), (BCPI[:], bcp_in_d),
                         (SEL2[:], sel_d), (UM[:], um_d), (FLG[:], flags_d),
                         (QKS[:], qks_d), (LNP[:], lnp_d), (B1P[:], b1p_d),
                         (B2P[:], b2p_d), (ONEC[:], ones_col_d),
                         (ONER[:], ones_row_d)):
            dma("sync", dst, src, "L")
        consts_L = em.now("L")
        for e_ in ("tensor", "vector", "scalar", "gpsimd"):
            em.wait(e_, "L", consts_L)

        for th in range(32):
            tt, half = th // 2, th % 2
            buf = XIN[th % 2]
            bsem = "X0" if th % 2 == 0 else "X1"
            if th >= 2:
                em.wait("sync", "T", em.now("T"))
            dma("sync", buf, x_tm[tt * 128:(tt + 1) * 128, half * 512:(half + 1) * 512], bsem)
            em.wait("tensor", bsem, em.now(bsem))
            for kk in range(4):
                k = half * 4 + kk
                bank = PS[kk % 2]
                if th > 0 or kk >= 2:
                    em.wait("tensor", "V", em.now("V") - 1)
                em.op("tensor", lambda e, b=bank, src=buf, kk=kk:
                      e.transpose(b[:, 0:128], src[:, kk * 128:(kk + 1) * 128], IDNF[:]),
                      inc=("T", 1))
                em.wait("vector", "T", em.now("T"))
                em.op("vector", lambda e, b=bank, tt=tt, k=k:
                      e.tensor_copy(XR[:, k * T_OWN + tt * 128:k * T_OWN + (tt + 1) * 128], b[:, 0:128]),
                      inc=("V", 1))


        def load_w_slabs(wsrc, ncols=1024):
            for k in range(NCH):
                em.op("gpsimd", lambda e, k=k, w=wsrc, nc_=ncols:
                      e.dma_start(WB[:, k * nc_:(k + 1) * nc_], w[k * 128:(k + 1) * 128, :]),
                      inc=("G", 16))
            return em.now("G")

        def cast_group(n):
            for k in range(NCH):
                em.op("vector", lambda e, k=k, n=n:
                      e.tensor_copy(X8[:, k * 512:(k + 1) * 512],
                                    XR[:, k * T_OWN + n * 512:k * T_OWN + (n + 1) * 512]),
                      inc=("V", 1))
            return em.now("V")

        def dn_projection(dst, want_norm, sqrt_scale, g_after):
            for n in range(4):
                em.wait("vector", "T", em.now("T"))
                cast_v = cast_group(n)
                em.wait("tensor", "V", cast_v)
                em.wait("tensor", "G", g_after)
                for m in range(NCH):
                    bank = PS[m % 2]
                    em.wait("tensor", "V", em.now("V") if m >= 2 else cast_v)
                    if want_norm:
                        em.wait("tensor", "S", em.now("S"))
                    for k in range(NCH):
                        em.op("tensor", lambda e, m=m, k=k, b=bank:
                              e.matmul(b[:], WB[:, k * 1024 + m * 128:k * 1024 + (m + 1) * 128],
                                       X8[:, k * 512:(k + 1) * 512],
                                       start=(k == 0), stop=(k == NCH - 1)),
                              inc=("T", 1) if k == NCH - 1 else None)
                    t_mm = em.now("T")
                    if want_norm:
                        em.wait("scalar", "T", t_mm)
                        em.op("scalar", lambda e, b=bank: e.activation(PRJ[:, 0:512], b[:], AF.Square),
                              inc=("S", 1))
                        em.wait("tensor", "S", em.now("S"))
                        em.op("tensor", lambda e, m=m:
                              e.matmul(PS[3][0:16, 0:512], PATD[:, m * 16:(m + 1) * 16], PRJ[:, 0:512],
                                       start=(m == 0), stop=(m == NCH - 1)),
                              inc=("T", 1))
                        em.wait("vector", "T", em.now("T"))
                        em.op("vector", lambda e, m=m, n=n, b=bank, d=dst:
                              e.tensor_copy(d[:, m * T_OWN + n * 512:m * T_OWN + (n + 1) * 512], b[:]),
                              inc=("V", 1))
                    else:
                        em.wait("vector", "T", t_mm)
                        em.op("vector", lambda e, m=m, n=n, b=bank, d=dst:
                              e.tensor_copy(d[:, m * T_OWN + n * 512:m * T_OWN + (n + 1) * 512], b[:]),
                              inc=("V", 1))
                if want_norm:
                    em.wait("vector", "T", em.now("T"))
                    em.op("vector", lambda e: e.tensor_scalar_max(RSQ, PS[3][0:16, 0:512], 1e-24), inc=("V", 1))
                    em.op("vector", lambda e: e.reciprocal(RSQ, RSQ), inc=("V", 1))
                    em.wait("scalar", "V", em.now("V"))
                    em.op("scalar", lambda e, sc=sqrt_scale: e.activation(RSQ, RSQ, AF.Sqrt, scale=sc), inc=("S", 1))
                    em.wait("vector", "S", em.now("S"))
                    em.op("vector", lambda e: e.tensor_copy(PRJ[0:16, 0:512], RSQ), inc=("V", 1))
                    for m in range(NCH):
                        em.wait("tensor", "V", em.now("V"))
                        em.op("tensor", lambda e, m=m:
                              e.matmul(PS[2][:], BCPD[:, m * 128:(m + 1) * 128], PRJ[0:16, 0:512],
                                       start=True, stop=True),
                              inc=("T", 1))
                        em.wait("vector", "T", em.now("T"))
                        em.op("vector", lambda e, m=m, n=n, d=dst:
                              e.tensor_tensor(d[:, m * T_OWN + n * 512:m * T_OWN + (n + 1) * 512],
                                              d[:, m * T_OWN + n * 512:m * T_OWN + (n + 1) * 512],
                                              PS[2][:], ALU.mult),
                              inc=("V", 1))

        g = load_w_slabs(dn_Wq)
        for k in range(NCH):
            em.op("gpsimd", lambda e, k=k: e.dma_start(WBETA[:, k * 16:(k + 1) * 16], dn_Wb[k * 128:(k + 1) * 128, :]),
                  inc=("G", 16))
        g_beta = em.now("G")
        dn_projection(QHAT, True, 1.0 / 64.0, g_beta)

        for n in range(4):
            em.wait("vector", "T", em.now("T"))
            cast_v = cast_group(n)
            em.wait("tensor", "V", cast_v)
            em.wait("tensor", "G", g_beta)
            em.wait("tensor", "S", em.now("S"))
            for k in range(NCH):
                em.op("tensor", lambda e, k=k:
                      e.matmul(PS[6][0:16, 0:512], WBETA[:, k * 16:(k + 1) * 16], X8[:, k * 512:(k + 1) * 512],
                               start=(k == 0), stop=(k == NCH - 1)),
                      inc=("T", 1) if k == NCH - 1 else None)
            em.wait("scalar", "T", em.now("T"))
            em.op("scalar", lambda e, n=n: e.activation(BETA[:, n * 512:(n + 1) * 512], PS[6][0:16, 0:512], AF.Sigmoid),
                  inc=("S", 1))

        em.wait("gpsimd", "T", em.now("T"))
        g = load_w_slabs(dn_Wk)
        dn_projection(KHAT, True, 1.0, g)
        em.wait("gpsimd", "T", em.now("T"))
        g = load_w_slabs(dn_Wv)
        dn_projection(VFM, False, None, g)

        def s_ap(h):
            b = PS[4 + h // 8]
            return b[0:64, (h % 8) * 64:(h % 8 + 1) * 64]

        def transpose_chunk(src_hb, dst, c):
            em.wait("tensor", "S", em.now("S"))
            for p in range(NCH):
                em.op("tensor", lambda e, p=p, c=c, s_=src_hb:
                      e.transpose(PB[:, p * 128:(p + 1) * 128],
                                  s_[:, p * T_OWN + c * 128:p * T_OWN + (c + 1) * 128], IDNB[:]),
                      inc=("T", 1) if p == NCH - 1 else None)
            em.wait("scalar", "T", em.now("T"))
            em.op("scalar", lambda e, d=dst: e.copy(d, PB[:]), inc=("S", 1))

        for c in range(16):
            transpose_chunk(KHAT, KTM[:], c)
            transpose_chunk(VFM, VTM[:], c)
            em.wait("tensor", "S", em.now("S"))
            for h in range(H):
                em.op("tensor", lambda e, h=h, c=c:
                      e.matmul(s_ap(h), KTM[:, h * 64:(h + 1) * 64], VTM[:, h * 64:(h + 1) * 64],
                               start=(c == 0 and h % 8 == 0), stop=(c == 15 and h % 8 == 7)),
                      inc=("T", 1) if h == H - 1 else None)

        em.wait("vector", "T", em.now("T"))
        SOUT = ARF[0:64, 0:512]
        SOUT2 = ARF[0:64, 512:1024]
        em.op("vector", lambda e: e.tensor_copy(SOUT, PS[4][0:64, 0:512]), inc=("V", 1))
        em.op("vector", lambda e: e.tensor_copy(SOUT2, PS[5][0:64, 0:512]), inc=("V", 1))
        em.wait("sync", "V", em.now("V"))
        dma("sync", cin.ap()[0:64, 0:512], SOUT, "L")
        dma("sync", cin.ap()[0:64, 512:1024], SOUT2, "L")
        em.wait("gpsimd", "L", em.now("L"))
        em.op("gpsimd", lambda e: e.collective_compute(
            "AllGather", ALU.bypass, replica_groups=[[0, 1, 2, 3, 4, 5, 6, 7]],
            ins=[cin.ap().opt()], outs=[cout.ap().opt()]), inc=("C", 1))
        em.wait("sync", "C", em.now("C"))
        def recv_carry(e):
            pid = e.partition_id()
            off = ((pid + 1) % 8) * (64 * 1024)
            return e.dma_start(CARF, bass.AP(cout, off, [[1024, 64], [1, 1024]]))
        em.op("sync", recv_carry, inc=("L", 16))
        em.wait("vector", "L", em.now("L"))
        em.op("vector", lambda e: e.tensor_scalar_mul(S_F32, CARF, FLG[0:64, 1:2]), inc=None)
        em.op("vector", lambda e: e.tensor_copy(S_SB[:, 0:1024], S_F32), inc=("V", 1))

        em.wait("gpsimd", "T", em.now("T"))
        g_wo = load_w_slabs(dn_Wo)

        def ln_chunk(dst_col):
            em.op("vector", lambda e: e.tensor_copy(XB1[:, 0:1024], XSG), inc=("V", 1))
            em.wait("scalar", "V", em.now("V"))
            em.op("scalar", lambda e: e.activation(XB2[:, 0:1024], XSG, AF.Square), inc=("S", 1))
            em.wait("tensor", "S", em.now("S"))
            em.wait("tensor", "V", em.now("V"))
            for p in range(NCH):
                em.op("tensor", lambda e, p=p:
                      e.matmul(PS[6][0:1, 0:128], ONEC[:, 0:1], XB1[:, p * 128:(p + 1) * 128],
                               start=(p == 0), stop=(p == NCH - 1)),
                      inc=("T", 1) if p == NCH - 1 else None)
            for p in range(NCH):
                em.op("tensor", lambda e, p=p:
                      e.matmul(PS[3][0:1, 0:128], ONEC[:, 0:1], XB2[:, p * 128:(p + 1) * 128],
                               start=(p == 0), stop=(p == NCH - 1)),
                      inc=("T", 1) if p == NCH - 1 else None)
            em.wait("vector", "T", em.now("T"))
            em.op("vector", lambda e: e.tensor_scalar_mul(MUs, PS[6][0:1, 0:128], 1.0 / D), inc=("V", 1))
            em.op("vector", lambda e: e.tensor_scalar_mul(MSQ, PS[3][0:1, 0:128], 1.0 / D), inc=("V", 1))
            em.op("vector", lambda e: e.tensor_tensor(TMP1, MUs, MUs, ALU.mult), inc=("V", 1))
            em.op("vector", lambda e: e.tensor_sub(MSQ, MSQ, TMP1), inc=("V", 1))
            em.op("vector", lambda e: e.tensor_scalar_add(MSQ, MSQ, EPS), inc=("V", 1))
            em.wait("scalar", "V", em.now("V"))
            em.op("scalar", lambda e: e.activation(MSQ, MSQ, AF.Sqrt), inc=("S", 1))
            em.wait("vector", "S", em.now("S"))
            em.op("vector", lambda e: e.reciprocal(MSQ, MSQ), inc=("V", 1))
            em.op("vector", lambda e: e.scalar_tensor_tensor(MUs, MUs, -1.0, MSQ, ALU.mult, ALU.mult), inc=("V", 1))
            em.op("vector", lambda e: e.tensor_copy(BBS[0:1, 0:128], MSQ), inc=None)
            em.op("vector", lambda e: e.tensor_copy(BBS[0:1, 128:256], MUs), inc=("V", 1))
            em.wait("tensor", "V", em.now("V"))
            em.op("tensor", lambda e: e.matmul(PS[6][:, 0:128], ONER[:], BBS[0:1, 0:128], start=True, stop=True), inc=None)
            em.op("tensor", lambda e: e.matmul(PS[3][:, 0:128], ONER[:], BBS[0:1, 128:256], start=True, stop=True), inc=("T", 1))
            em.wait("vector", "T", em.now("T"))
            for p in range(NCH):
                em.op("vector", lambda e, p=p:
                      e.tensor_tensor(XSG[:, p * 128:(p + 1) * 128], XSG[:, p * 128:(p + 1) * 128],
                                      PS[6][:, 0:128], ALU.mult),
                      inc=None)
                em.op("vector", lambda e, p=p, w=dst_col:
                      e.tensor_tensor(XR[:, p * T_OWN + w:p * T_OWN + w + 128],
                                      XSG[:, p * 128:(p + 1) * 128], PS[3][:, 0:128], ALU.add),
                      inc=("V", 1) if p == NCH - 1 else None)

        for c in SCAN_CHUNKS:
            transpose_chunk(KHAT, KTM[:], c)
            transpose_chunk(VFM, VTM[:], c)
            for p in range(NCH):
                em.wait("tensor", "S", em.now("S"))
                em.wait("tensor", "V", em.now("V"))
                for (slot, src, selc) in ((0, QHAT, 0), (1, QHAT, 64), (2, KHAT, 0), (3, KHAT, 64)):
                    em.op("tensor", lambda e, p=p, c=c, slot=slot, src=src, selc=selc:
                          e.matmul(PS[2][0:64, slot * 128:(slot + 1) * 128], SEL2[:, selc:selc + 64],
                                   src[:, p * T_OWN + c * 128:p * T_OWN + (c + 1) * 128],
                                   start=True, stop=True),
                          inc=("T", 1) if slot == 3 else None)
                em.wait("scalar", "T", em.now("T"))
                em.op("scalar", lambda e: e.copy(HM[0:64, :], PS[2][0:64, 0:512]), inc=("S", 1))
                em.op("tensor", lambda e, p=p, c=c:
                      e.matmul(PS[6][:, 0:128], BCPD[:, p * 128:(p + 1) * 128], BETA[:, c * 128:(c + 1) * 128],
                               start=True, stop=True),
                      inc=("T", 1))
                em.wait("vector", "T", em.now("T"))
                em.op("vector", lambda e: e.tensor_copy(BBS[:, 0:128], PS[6][:, 0:128]), inc=None)
                em.op("vector", lambda e, p=p, c=c:
                      e.tensor_tensor(ZSC, VFM[:, p * T_OWN + c * 128:p * T_OWN + (c + 1) * 128],
                                      BBS[:, 0:128], ALU.mult),
                      inc=("V", 1))
                for hh in range(2):
                    h = 2 * p + hh
                    em.wait("tensor", "S", em.now("S"))
                    em.wait("tensor", "V", em.now("V"))
                    em.op("tensor", lambda e, hh=hh:
                          e.matmul(PS[6][:, 128:256], HM[0:64, 256 + hh * 128:256 + (hh + 1) * 128],
                                   HM[0:64, hh * 128:(hh + 1) * 128], start=True, stop=True),
                          inc=("T", 1))
                    em.wait("vector", "T", em.now("T"))
                    em.op("vector", lambda e: e.tensor_tensor(WTS[:, 0:128], PS[6][:, 128:256], UM[:], ALU.mult), inc=("V", 1))
                    em.wait("tensor", "V", em.now("V"))
                    em.op("tensor", lambda e, h=h:
                          e.matmul(PS[0][0:64, 0:128], VTM[:, h * 64:(h + 1) * 64], WTS[:, 0:128],
                                   start=True, stop=False), inc=None)
                    em.op("tensor", lambda e, h=h, hh=hh:
                          e.matmul(PS[0][0:64, 0:128], S_SB[:, h * 64:(h + 1) * 64],
                                   HM[0:64, hh * 128:(hh + 1) * 128], start=False, stop=True),
                          inc=("T", 1))
                    em.wait("vector", "T", em.now("T"))
                    em.op("vector", lambda e, hh=hh, p=p:
                          e.tensor_tensor(OUTB[hh * 64:(hh + 1) * 64, p * 128:(p + 1) * 128],
                                          PS[0][0:64, 0:128], ZSC[hh * 64:(hh + 1) * 64, :], ALU.add),
                          inc=("V", 1))
                    em.wait("tensor", "V", em.now("V"))
                    em.op("tensor", lambda e, h=h, c=c:
                          e.matmul(s_ap(h), KTM[:, h * 64:(h + 1) * 64], VTM[:, h * 64:(h + 1) * 64],
                                   start=(h % 8 == 0), stop=(h % 8 == 7)),
                          inc=("T", 1))
            em.wait("vector", "T", em.now("T"))
            em.op("vector", lambda e: e.tensor_tensor(S_F32[:, 0:512], S_F32[:, 0:512], PS[4][0:64, 0:512], ALU.add), inc=None)
            em.op("vector", lambda e: e.tensor_tensor(S_F32[:, 512:1024], S_F32[:, 512:1024], PS[5][0:64, 0:512], ALU.add), inc=None)
            em.op("vector", lambda e: e.tensor_copy(S_SB[:, 0:1024], S_F32), inc=("V", 1))
            em.wait("tensor", "S", em.now("S"))
            em.wait("tensor", "V", em.now("V"))
            em.wait("tensor", "G", g_wo)
            for m in range(NCH):
                em.wait("tensor", "V", em.now("V"))
                for k in range(NCH):
                    em.op("tensor", lambda e, m=m, k=k:
                          e.matmul(PS[1][:, 0:128], WB[:, k * 1024 + m * 128:k * 1024 + (m + 1) * 128],
                                   OUTB[:, k * 128:(k + 1) * 128], start=(k == 0), stop=(k == NCH - 1)),
                          inc=("T", 1) if k == NCH - 1 else None)
                em.wait("vector", "T", em.now("T"))
                em.op("vector", lambda e, m=m, c=c:
                      e.tensor_tensor(XSG[:, m * 128:(m + 1) * 128], PS[1][:, 0:128],
                                      XR[:, m * T_OWN + c * 128:m * T_OWN + (c + 1) * 128], ALU.add),
                      inc=("V", 1))
            ln_chunk(c * 128)

        em.wait("sync", "V", em.now("V"))
        dma("sync", park.ap(), XR[:], "L")
        hsrc = XR[:].rearrange("p (c t) -> p c t", c=NCH)[:, :, T_OWN - 513:T_OWN]
        dma("sync", hin.ap().rearrange("p (c t) -> p c t", c=NCH), hsrc, "L")
        if N_LAYERS > 0:
            em.wait("gpsimd", "L", em.now("L"))
            em.op("gpsimd", lambda e: e.collective_compute(
                "AllGather", ALU.bypass, replica_groups=[[0, 1, 2, 3, 4, 5, 6, 7]],
                ins=[hin.ap().opt()], outs=[hout.ap().opt()]), inc=("C", 1))
            em.wait("sync", "T", em.now("T"))
            dma("sync", SSB[:, 0:2048], masks_d, "L")

        WQKV = AB[:, 0:24576]
        SHBa = lambda k: AB[:, 24576 + k * 512:24576 + (k + 1) * 512]
        TABS = lambda t: AB[:, 28672 + t * TLOC:28672 + (t + 1) * TLOC]
        QGa = lambda k: AB[:, 38912 + k * 512:38912 + (k + 1) * 512]
        KSa = lambda k, sl_: AB[:, 43008 + k * 768 + sl_ * 256:43008 + k * 768 + (sl_ + 1) * 256]
        VTa = lambda sl_: AB[:, 49152 + sl_ * 1024:49152 + (sl_ + 1) * 1024]
        UBa = lambda k: AB[:, 55296 + k * 512:55296 + (k + 1) * 512]
        OUTFa = lambda k: AB[:, 55296 + k * 512:55296 + (k + 1) * 512]
        PBUFa = AB[:, 59392:59904]
        PTBa = AB[:, 59904:60928]
        SQBa = AB[:, 60928:61440]
        WOs = lambda k: (AB[:, 24576 + k * 1024:24576 + (k + 1) * 1024] if k < 4
                         else AB[:, 38912 + (k - 4) * 1024:38912 + (k - 3) * 1024])
        MSKA = lambda t: SSB[:, t * 512:(t + 1) * 512]
        RTT = SSB[0:16, 2048:2304]
        RCB = SSB[:, 2304:2320]
        NMX = SS[:, 260:261]
        SUMS = [SS[:, 264:280], SS[:, 280:296]]
        RCF = SS[:, 296:312]
        RCF2 = SS[:, 364:380]
        RSQA = ARF[0:16, 0:512]
        PBUF2 = AB[:, 24576:25088]
        PTB2 = AB[:, 25088:26112]
        NMX2 = SS[:, 261:262]
        XHLL = SS[:, 312:320]
        FBC = SS[:, 320:328]
        SAVEC = SS[:, 328:336]
        XUH = SS[:, 336:344]
        SAVE2 = SS[:, 344:352]
        W2Sa = lambda j: AB[:, j * 1024:(j + 1) * 1024]
        XNa = lambda k: AB[:, 22528 + k * 1024:22528 + (k + 1) * 1024]
        H2Ha = lambda j: AB[:, 30720 + j * 1024:30720 + (j + 1) * 1024]
        W1Ba = lambda buf: AB[:, 53248 + buf * 2048:53248 + (buf + 1) * 2048]
        GAa = AB[:, 57344 + 0:57344 + 512]
        CHS = AB[:, 57856:58368]
        SHF = ARF[:, 0:2048]
        PFG = ARF[:, 0:2048]

        def ln_stats_rolling(src_f32_slices, ntok, bias_eps=EPS):
            for p in range(NCH):
                t_prev = em.now("T")
                em.wait("vector", "T", t_prev)
                em.op("vector", lambda e, p=p: e.tensor_copy(CHS[:, 0:ntok], src_f32_slices[p]), inc=("V", 1))
                em.wait("scalar", "V", em.now("V"))
                em.wait("scalar", "T", t_prev)
                em.op("scalar", lambda e, p=p: e.activation(CHS[:, 256:256 + ntok], src_f32_slices[p], AF.Square), inc=("S", 1))
                em.wait("tensor", "V", em.now("V"))
                em.wait("tensor", "S", em.now("S"))
                em.op("tensor", lambda e, p=p, n=ntok:
                      e.matmul(PS[6][0:1, 0:n], ONEC[:, 0:1], CHS[:, 0:n], start=(p == 0), stop=(p == NCH - 1)),
                      inc=None)
                em.op("tensor", lambda e, p=p, n=ntok:
                      e.matmul(PS[3][0:1, 0:n], ONEC[:, 0:1], CHS[:, 256:256 + n], start=(p == 0), stop=(p == NCH - 1)),
                      inc=("T", 1))
            em.wait("vector", "T", em.now("T"))
            for h in range(ntok // 128):
                hsl = slice(h * 128, (h + 1) * 128)
                em.op("vector", lambda e, hsl=hsl: e.tensor_scalar_mul(MUs, PS[6][0:1, hsl], 1.0 / D), inc=("V", 1))
                em.op("vector", lambda e, hsl=hsl: e.tensor_scalar_mul(MSQ, PS[3][0:1, hsl], 1.0 / D), inc=("V", 1))
                em.op("vector", lambda e: e.tensor_tensor(TMP1, MUs, MUs, ALU.mult), inc=("V", 1))
                em.op("vector", lambda e: e.tensor_sub(MSQ, MSQ, TMP1), inc=("V", 1))
                em.op("vector", lambda e, be=bias_eps: e.tensor_scalar_add(MSQ, MSQ, be), inc=("V", 1))
                em.wait("scalar", "V", em.now("V"))
                em.op("scalar", lambda e: e.activation(MSQ, MSQ, AF.Sqrt), inc=("S", 1))
                em.wait("vector", "S", em.now("S"))
                em.op("vector", lambda e: e.reciprocal(MSQ, MSQ), inc=("V", 1))
                em.op("vector", lambda e: e.scalar_tensor_tensor(MUs, MUs, -1.0, MSQ, ALU.mult, ALU.mult), inc=("V", 1))
                em.op("vector", lambda e, hsl=hsl: e.tensor_copy(CHS[0:1, hsl], MSQ), inc=None)
                em.op("vector", lambda e, h=h: e.tensor_copy(CHS[0:1, 256 + h * 128:256 + (h + 1) * 128], MUs), inc=("V", 1))
            em.wait("tensor", "V", em.now("V"))
            em.op("tensor", lambda e, n=ntok: e.matmul(PS[6][:, 0:n], ONER[:], CHS[0:1, 0:n], start=True, stop=True), inc=None)
            em.op("tensor", lambda e, n=ntok: e.matmul(PS[3][:, 0:n], ONER[:], CHS[0:1, 256:256 + n], start=True, stop=True), inc=("T", 1))
            em.wait("vector", "T", em.now("T"))


        def emit_attn_stage(l):
            em.wait("gpsimd", "T", em.now("T"))
            for k in range(NCH):
                em.op("gpsimd", lambda e, k=k, l=l:
                      e.dma_start(WQKV[:, k * 3072:(k + 1) * 3072], in_Wqkv[l, k * 128:(k + 1) * 128, :]),
                      inc=("WQ", 16))
            em.op("gpsimd", lambda e: e.dma_start(AB[:, 28672:38912], tabs), inc=("HG", 16))
            em.wait("vector", "HG", em.now("HG"))
            em.wait("vector", "L", em.now("L"))
            wq_done = em.now("WQ")
            em.wait("sync", "C", em.now("C"))
            def dma_slab(e):
                pid = e.partition_id()
                off = ((pid + 7) % 8) * (128 * NCH * 513)
                return e.dma_start(hloc.ap(), bass.AP(hout, off, [[NCH * 513, 128], [1, NCH * 513]]))
            em.op("sync", dma_slab, inc=("HS", 16))
            em.wait("sync", "HS", em.now("HS"))
            em.op("sync", lambda e: e.dma_start(XHLL[:, 0:NCH],
                                                bass.AP(hloc, 512, [[NCH * 513, 128], [513, NCH]]),
                                                allow_slow_non_contiguous=True),
                  inc=("HS", 16))
            em.wait("vector", "HS", em.now("HS"))
            em.op("vector", lambda e: e.tensor_scalar_mul(FBC[:, 0:NCH], XHLL[:, 0:NCH], FLG[:, 0:1]), inc=("V", 1))

            for g in range(5):
                if g == 0:
                    em.wait("gpsimd", "HS", em.now("HS"))
                    for k in range(NCH):
                        toff = 1 if k < 4 else 0
                        em.op("gpsimd", lambda e, k=k, toff=toff:
                              e.dma_start(SHBa(k), bass.AP(hloc, k * 513 + toff, [[NCH * 513, 128], [1, 512]])),
                              inc=("HG", 16))
                    em.wait("tensor", "HG", em.now("HG"))
                else:
                    em.wait("vector", "T", em.now("T"))
                    base = 512 * (g - 1)
                    for k in range(NCH):
                        if k < 4:
                            em.op("vector", lambda e, k=k, base=base:
                                  e.tensor_copy(SHBa(k), XR[:, k * T_OWN + base:k * T_OWN + base + 512]),
                                  inc=("V", 1))
                        else:
                            if g == 1:
                                em.op("vector", lambda e, k=k:
                                      e.tensor_copy(SHBa(k)[:, 0:1], FBC[:, k:k + 1]),
                                      inc=None)
                            else:
                                em.op("vector", lambda e, k=k:
                                      e.tensor_copy(SHBa(k)[:, 0:1], SAVEC[:, k:k + 1]),
                                      inc=None)
                            em.op("vector", lambda e, k=k, base=base:
                                  e.tensor_copy(SHBa(k)[:, 1:512], XR[:, k * T_OWN + base:k * T_OWN + base + 511]),
                                  inc=("V", 1))
                    em.wait("tensor", "V", em.now("V"))
                if g >= 1:
                    em.op("vector", lambda e, g=g:
                          e.tensor_copy(SAVEC[:, 0:NCH],
                                        XR[:].rearrange("p (c t) -> p c t", c=NCH)[:, :, 512 * g - 1]),
                          inc=("V", 1))

                em.wait("tensor", "WQ", wq_done)
                for stream in range(2):
                    for cc in range(NCH):
                        m = stream * 8 + cc
                        bank = PS[cc % 2]
                        em.wait("tensor", "V", em.now("V"))
                        em.wait("tensor", "S", em.now("S"))
                        for k in range(NCH):
                            em.op("tensor", lambda e, m=m, k=k, b=bank:
                                  e.matmul(b[:], WQKV[:, k * 3072 + m * 128:k * 3072 + (m + 1) * 128],
                                           SHBa(k), start=(k == 0), stop=(k == NCH - 1)),
                                  inc=("T", 1) if k == NCH - 1 else None)
                        t_mm = em.now("T")
                        em.wait("scalar", "T", t_mm)
                        em.op("scalar", lambda e, b=bank: e.activation(SQBa, b[:], AF.Square), inc=("S", 1))
                        em.wait("tensor", "S", em.now("S"))
                        em.op("tensor", lambda e, cc=cc:
                              e.matmul(PS[2][0:16, 0:512], PATI[:, (cc // 2) * 16:(cc // 2 + 1) * 16], SQBa,
                                       start=(cc == 0), stop=(cc == NCH - 1)),
                              inc=("T", 1))
                        em.wait("vector", "T", em.now("T"))
                        if stream == 0:
                            em.op("vector", lambda e, cc=cc, b=bank: e.tensor_copy(QGa(cc), b[:]), inc=("V", 1))
                        else:
                            em.op("vector", lambda e, cc=cc, b=bank: e.tensor_copy(UBa(cc), b[:]), inc=("V", 1))
                    em.wait("vector", "T", em.now("T"))
                    em.op("vector", lambda e: e.tensor_scalar_max(RSQA, PS[2][0:16, 0:512], 1e-24), inc=("V", 1))
                    em.op("vector", lambda e: e.reciprocal(RSQA, RSQA), inc=("V", 1))
                    em.wait("scalar", "V", em.now("V"))
                    em.op("scalar", lambda e, sc=(64.0 if stream == 0 else 1.0):
                          e.activation(RSQA, RSQA, AF.Sqrt, scale=sc), inc=("S", 1))
                    em.wait("vector", "S", em.now("S"))
                    RSQB_in = SSB[0:16, 2320:2832]
                    em.op("vector", lambda e: e.tensor_copy(RSQB_in, RSQA), inc=("V", 1))
                    for cc in range(NCH):
                        em.wait("tensor", "V", em.now("V"))
                        em.op("tensor", lambda e, cc=cc:
                              e.matmul(PS[3][:], BCPI[:, (cc // 2) * 128:(cc // 2 + 1) * 128], RSQB_in,
                                       start=True, stop=True),
                              inc=("T", 1))
                        em.wait("vector", "T", em.now("T"))
                        qscol = QKS[:, l * 4 + (2 * stream) + (cc % 2):l * 4 + (2 * stream) + (cc % 2) + 1]
                        if stream == 0:
                            em.op("vector", lambda e, cc=cc, qs=qscol:
                                  e.scalar_tensor_tensor(UBa(cc), QGa(cc), qs, PS[3][:], ALU.mult, ALU.mult),
                                  inc=("V", 1))
                        else:
                            em.op("vector", lambda e, cc=cc, qs=qscol:
                                  e.scalar_tensor_tensor(UBa(cc), UBa(cc), qs, PS[3][:], ALU.mult, ALU.mult),
                                  inc=("V", 1))
                    tC = TABS(0 if stream == 0 else 2)[:, g * 512:(g + 1) * 512]
                    tS = TABS(1 if stream == 0 else 3)[:, g * 512:(g + 1) * 512]
                    for cc in range(NCH):
                        sgn = -1.0 if cc % 2 == 0 else 1.0
                        part = cc + 1 if cc % 2 == 0 else cc - 1
                        em.op("vector", lambda e, cc=cc, tC=tC: e.tensor_tensor(PBUFa, UBa(cc), tC, ALU.mult), inc=None)
                        em.op("vector", lambda e, part=part, sgn=sgn, tS=tS:
                              e.scalar_tensor_tensor(PTBa[:, 0:512], UBa(part), sgn, tS, ALU.mult, ALU.mult), inc=None)
                        if stream == 0:
                            em.op("vector", lambda e, cc=cc: e.tensor_add(QGa(cc), PBUFa, PTBa[:, 0:512]), inc=("V", 1))
                        else:
                            em.op("vector", lambda e, cc=cc, g=g:
                                  e.tensor_add(KSa(cc, (2 * g) % 3), PBUFa[:, 0:256], PTBa[:, 0:256]), inc=None)
                            em.op("vector", lambda e, cc=cc, g=g:
                                  e.tensor_add(KSa(cc, (2 * g + 1) % 3), PBUFa[:, 256:512], PTBa[:, 256:512]), inc=("V", 1))
                for tt in range(4):
                    gslot = (g * 4 + tt) % 6
                    for nh in range(2):
                        bank = PS[(tt * 2 + nh) % 2]
                        em.wait("tensor", "V", em.now("V"))
                        for k in range(NCH):
                            em.op("tensor", lambda e, k=k, tt=tt, nh=nh, b=bank:
                                  e.matmul(b[:], SHBa(k)[:, tt * 128:(tt + 1) * 128],
                                           WQKV[:, k * 3072 + 2048 + nh * 512:k * 3072 + 2048 + (nh + 1) * 512],
                                           start=(k == 0), stop=(k == NCH - 1)),
                                  inc=("T", 1) if k == NCH - 1 else None)
                        em.wait("vector", "T", em.now("T"))
                        em.op("vector", lambda e, gslot=gslot, nh=nh, b=bank:
                              e.tensor_copy(VTa(gslot)[:, nh * 512:(nh + 1) * 512], b[:]),
                              inc=("V", 1))

                if not DBG_NO_ATT:
                    emit_attn_windows(l, g)

                if not DBG_NO_WO:
                    emit_wo_group(l, g)

                if DBG_DUMP3 and l == 0 and g == 0:
                    em.wait("sync", "V", em.now("V"))
                    em.wait("sync", "T", em.now("T"))
                    em.op("sync", lambda e: e.dma_start(bass.AP(park, 0, [[16384, 128], [1, 8]]), XUH), inc=("L", 16))
                    em.op("sync", lambda e: e.dma_start(bass.AP(park, 8, [[16384, 128], [1, 8]]), XHLL), inc=("L", 16))
                    em.op("sync", lambda e: e.dma_start(bass.AP(park, 16, [[16384, 128], [1, 8]]), FBC), inc=("L", 16))
                    em.wait("gpsimd", "V", em.now("V"))
                    em.wait("gpsimd", "T", em.now("T"))
                    for p_ in range(NCH):
                        em.op("gpsimd", lambda e, p_=p_:
                              e.dma_start(bass.AP(park, 32 + p_ * 512, [[16384, 128], [1, 512]]), OUTFa(p_)),
                              inc=("HG", 16))

                if DBG_DUMP2 and l == 0 and g == 0:
                    em.wait("vector", "T", em.now("T"))
                    em.wait("vector", "S", em.now("S"))
                    d2 = [(XR[:, 1536:1552], SUMS[0]), (XR[:, 1552:1568], SUMS[1]),
                          (XR[:, 2080:2096], RCB[:, 0:16]), (XR[:, 2096:2112], SSB[:, 2832:2848]),
                          (XR[0:16, 1568:1824], RTT[0:16, 0:256])]
                    for p_ in range(NCH):
                        d2.append((XR[:, 4096 + p_ * 512:4096 + (p_ + 1) * 512], OUTFa(p_)))
                    for i, (d_, s_) in enumerate(d2):
                        em.op("vector", lambda e, d_=d_, s_=s_: e.tensor_copy(d_, s_),
                              inc=("V", 1) if i == len(d2) - 1 else None)

            if DBG_DUMP and l == 0:
                em.wait("vector", "T", em.now("T"))
                em.wait("vector", "S", em.now("S"))
                dumps = [
                    (XR[:, 0:512], PBUFa),
                    (XR[:, 512:1536], PTBa),
                    (XR[:, 1536:1552], SUMS[0]),
                    (XR[:, 1552:1568], SUMS[1]),
                    (XR[0:16, 1568:1824], RTT[0:16, 0:256]),
                    (XR[:, 1824:2080], PS[3][:, 0:256]),
                    (XR[:, 2080:2096], RCB[:, 0:16]),
                    (XR[:, 2096:2112], SSB[:, 2832:2848]),
                ]
                for p_ in range(NCH):
                    dumps.append((XR[:, 4096 + p_ * 512:4096 + (p_ + 1) * 512], OUTFa(p_)))
                    dumps.append((XR[:, 8192 + p_ * 512:8192 + (p_ + 1) * 512], QGa(p_)))
                    dumps.append((XR[:, 12288 + p_ * 256:12288 + (p_ + 1) * 256], KSa(p_, 0)))
                dumps.append((XR[:, 14336:15360], VTa(0)))
                dumps.append((XR[:, 15360:16384], VTa(1)))
                for i, (d_, s_) in enumerate(dumps):
                    em.op("vector", lambda e, d_=d_, s_=s_: e.tensor_copy(d_, s_),
                          inc=("V", 1) if i == len(dumps) - 1 else None)

        def emit_attn_windows(l, g):
            for w in (2 * g, 2 * g + 1):
                if w == 0:
                    continue
                ps_slot = (w - 1) % 3
                cs_slot = w % 3
                pv_slots = [((w - 1) * 2) % 6, ((w - 1) * 2 + 1) % 6, (w * 2) % 6, (w * 2 + 1) % 6]
                wcol = (w % 2) * 256
                mbase = 2 if w == 2 else 0
                s_rd = [em.now("S"), em.now("S")]
                v_rd = [em.now("V"), em.now("V")]
                s_pb = [em.now("S"), em.now("S")]
                pv_guard = [em.now("V"), em.now("V")]
                for p in range(NCH):
                    for hh in range(2):
                        h = 2 * p + hh
                        gq = h // 4
                        rb = (h % 4) * 32
                        par = h % 2
                        SC = PS[4] if par == 0 else PS[6]
                        PBU = PBUFa if par == 0 else PBUF2
                        PTB_ = PTBa if par == 0 else PTB2
                        NMXs = NMX if par == 0 else NMX2
                        pbc = par * 512
                        PVB = PS[5] if p % 2 == 0 else PS[1]
                        for qt in range(2):
                            em.wait("tensor", "S", s_rd[par])
                            em.wait("tensor", "V", v_rd[par])
                            for half in range(2):
                                chunk = 2 * gq + half
                                lq = QGa(chunk)[rb:rb + 32, wcol + qt * 128:wcol + qt * 128 + 128]
                                tpk = {"tile_position": (96, 0)} if rb == 96 else {}
                                for side, ksl in ((0, ps_slot), (1, cs_slot)):
                                    em.op("tensor", lambda e, chunk=chunk, rb=rb, lq=lq, ksl=ksl, side=side, half=half, tpk=tpk, SC=SC:
                                          e.matmul(SC[:, side * 256:(side + 1) * 256],
                                                   lq, KSa(chunk, ksl)[rb:rb + 32, :],
                                                   start=(half == 0 and side == 0), stop=(half == 1 and side == 1), **tpk),
                                          inc=("T", 1) if (half == 1 and side == 1) else None)
                            em.wait("vector", "T", em.now("T"))
                            em.op("vector", lambda e, SC=SC, NMXs=NMXs:
                                  e.tensor_reduce(NMXs, SC[:, 0:512], AX.X, ALU.max, negate=True),
                                  inc=("V", 1))
                            em.wait("scalar", "V", em.now("V"))
                            em.op("scalar", lambda e, SC=SC, PBU=PBU, NMXs=NMXs:
                                  e.activation(PBU, SC[:, 0:512], AF.Exp, bias=NMXs), inc=("S", 1))
                            em.wait("vector", "S", em.now("S"))
                            em.op("vector", lambda e, qt=qt, h=h, mbase=mbase, PBU=PBU:
                                  e.scalar_tensor_tensor(PBU, PBU, 1.0, MSKA(mbase + qt), ALU.mult, ALU.mult,
                                                         accum_out=SUMS[qt][:, h:h + 1]),
                                  inc=("V", 1))
                            em.wait("tensor", "V", em.now("V"))
                            em.wait("tensor", "S", s_pb[par])
                            for kc in range(4):
                                em.op("tensor", lambda e, kc=kc, PBU=PBU, pbc=pbc:
                                      e.transpose(PB[:, pbc + kc * 128:pbc + (kc + 1) * 128],
                                                  PBU[:, kc * 128:(kc + 1) * 128], IDNB[:]),
                                      inc=("T", 1) if kc == 3 else None)
                            em.wait("scalar", "T", em.now("T"))
                            for kc in range(4):
                                em.op("scalar", lambda e, kc=kc, qt=qt, PTB_=PTB_, pbc=pbc:
                                      e.copy(PTB_[:, kc * 256 + qt * 128:kc * 256 + qt * 128 + 128],
                                             PB[:, pbc + kc * 128:pbc + (kc + 1) * 128]),
                                      inc=("S", 1) if kc == 3 else None)
                        s_rd[par] = em.now("S")
                        v_rd[par] = em.now("V")
                        s_pb[par] = em.now("S")
                        em.wait("tensor", "S", em.now("S"))
                        em.wait("tensor", "V", pv_guard[p % 2])
                        for kc in range(4):
                            vsl = pv_slots[kc]
                            em.op("tensor", lambda e, kc=kc, vsl=vsl, h=h, hh=hh, PVB=PVB, PTB_=PTB_:
                                  e.matmul(PVB[hh * 64:(hh + 1) * 64, 0:256], VTa(vsl)[:, h * 64:(h + 1) * 64],
                                           PTB_[:, kc * 256:(kc + 1) * 256], start=(kc == 0), stop=(kc == 3)),
                                  inc=("T", 1) if kc == 3 else None)
                    em.wait("vector", "T", em.now("T"))
                    em.op("vector", lambda e, p=p, wcol=wcol, PVB=(PS[5] if p % 2 == 0 else PS[1]):
                          e.tensor_copy(OUTFa(p)[:, wcol:wcol + 256], PVB[:, 0:256]),
                          inc=("V", 1))
                    pv_guard[p % 2] = em.now("V")
                em.wait("scalar", "V", em.now("V"))
                em.op("scalar", lambda e: e.activation(RCF[:, 0:16], SUMS[0], AF.Ln), inc=None)
                em.op("scalar", lambda e: e.activation(RCF[:, 0:16], RCF[:, 0:16], AF.Exp, scale=-1.0), inc=None)
                em.op("scalar", lambda e: e.activation(RCF2[:, 0:16], SUMS[1], AF.Ln), inc=None)
                em.op("scalar", lambda e: e.activation(RCF2[:, 0:16], RCF2[:, 0:16], AF.Exp, scale=-1.0), inc=("S", 1))
                em.wait("vector", "S", em.now("S"))
                em.op("vector", lambda e: e.tensor_copy(RCB[:, 0:16], RCF[:, 0:16]), inc=None)
                em.op("vector", lambda e: e.tensor_copy(SSB[:, 2832:2848], RCF2[:, 0:16]), inc=("V", 1))
                em.wait("tensor", "V", em.now("V"))
                em.op("tensor", lambda e: e.transpose(PB[0:16, 0:128], RCB[:, 0:16], IDNB[:]), inc=None)
                em.op("tensor", lambda e: e.transpose(PB[0:16, 128:256], SSB[:, 2832:2848], IDNB[:]), inc=("T", 1))
                em.wait("scalar", "T", em.now("T"))
                em.op("scalar", lambda e: e.copy(RTT[0:16, 0:128], PB[0:16, 0:128]), inc=None)
                em.op("scalar", lambda e: e.copy(RTT[0:16, 128:256], PB[0:16, 128:256]), inc=("S", 1))
                for p in range(NCH):
                    em.wait("tensor", "S", em.now("S"))
                    em.wait("tensor", "V", em.now("V"))
                    em.op("tensor", lambda e, p=p:
                          e.matmul(PS[3][:, 0:256], BCPD[:, p * 128:(p + 1) * 128], RTT[0:16, 0:256],
                                   start=True, stop=True),
                          inc=("T", 1))
                    em.wait("vector", "T", em.now("T"))
                    em.op("vector", lambda e, p=p, wcol=wcol:
                          e.tensor_tensor(OUTFa(p)[:, wcol:wcol + 256], OUTFa(p)[:, wcol:wcol + 256],
                                          PS[3][:, 0:256], ALU.mult),
                          inc=("V", 1))

        def emit_wo_group(l, g):
            em.wait("gpsimd", "V", em.now("V"))
            for k in range(NCH):
                em.op("gpsimd", lambda e, k=k, l=l:
                      e.dma_start(WOs(k), in_Wo[l, k * 128:(k + 1) * 128, :]),
                      inc=("WO", 16))
            em.wait("tensor", "WO", em.now("WO"))
            ntok = 256 if g == 0 else 512
            ocol = 256 if g == 0 else 0
            for m in range(NCH):
                bank = PS[m % 2]
                em.wait("tensor", "V", em.now("V"))
                for k in range(NCH):
                    em.op("tensor", lambda e, m=m, k=k, b=bank, ocol=ocol, n=ntok:
                          e.matmul(b[:, 0:n], WOs(k)[:, m * 128:(m + 1) * 128],
                                   OUTFa(k)[:, ocol:ocol + n], start=(k == 0), stop=(k == NCH - 1)),
                          inc=("T", 1) if k == NCH - 1 else None)
                em.wait("vector", "T", em.now("T"))
                if g == 0:
                    em.op("vector", lambda e, m=m, b=bank:
                          e.tensor_tensor(XUH[:, m:m + 1], b[:, 255:256], XHLL[:, m:m + 1], ALU.add),
                          inc=("V", 1))
                else:
                    base = 512 * (g - 1)
                    em.op("vector", lambda e, m=m, b=bank, base=base:
                          e.tensor_tensor(XR[:, m * T_OWN + base:m * T_OWN + base + 512], b[:, 0:512],
                                          XR[:, m * T_OWN + base:m * T_OWN + base + 512], ALU.add),
                          inc=("V", 1))

        def emit_ffn_stage(l):
            em.wait("gpsimd", "T", em.now("T"))
            if not (DBG_FFN_XN_ONLY or DBG_FFN_NO_W2):
                for j in range(NPAIR):
                    em.op("gpsimd", lambda e, j=j, l=l:
                          e.dma_start(W2Sa(j), in_W2[l, j * 128:(j + 1) * 128, :]),
                          inc=("W2L", 16))
            w2_done = em.now("W2L")
            t_pair_done = {}
            for th in range(2):
                tb = th * 1024
                for fg in range(4):
                    t0 = tb + fg * 256
                    em.wait("vector", "T", em.now("T"))
                    emit_shf(l, t0)
                    slices = [SHF[:, k * 256:(k + 1) * 256] for k in range(NCH)]
                    ln_stats_rolling(slices, 256)
                    for k in range(NCH):
                        gcol = LNP[:, l * 16 + k:l * 16 + k + 1]
                        bcol = LNP[:, l * 16 + 8 + k:l * 16 + 8 + k + 1]
                        em.op("vector", lambda e, k=k: e.tensor_tensor(SHF[:, k * 256:(k + 1) * 256],
                                                                      SHF[:, k * 256:(k + 1) * 256], PS[6][:, 0:256], ALU.mult), inc=None)
                        em.op("vector", lambda e, k=k: e.tensor_tensor(SHF[:, k * 256:(k + 1) * 256],
                                                                      SHF[:, k * 256:(k + 1) * 256], PS[3][:, 0:256], ALU.add), inc=None)
                        em.op("vector", lambda e, k=k, t0=t0, tb=tb, g_=gcol, b_=bcol:
                              e.scalar_tensor_tensor(XNa(k)[:, t0 - tb:t0 - tb + 256],
                                                     SHF[:, k * 256:(k + 1) * 256], g_,
                                                     b_.to_broadcast((128, 256)), ALU.mult, ALU.add),
                              inc=("V", 1) if k == NCH - 1 else None)
                if th == 0:
                    em.op("vector", lambda e:
                          e.tensor_copy(SAVE2[:, 0:NCH],
                                        XR[:].rearrange("p (c t) -> p c t", c=NCH)[:, :, 1023]),
                          inc=("V", 1))
                for j in range(NPAIR if not DBG_FFN_XN_ONLY else 0):
                    buf = j % 2
                    gp = th * NPAIR + j
                    if gp >= 2:
                        em.wait("gpsimd", "T", t_pair_done[gp - 2])
                    em.op("gpsimd", lambda e, j=j, l=l, buf=buf:
                          e.dma_start(W1Ba(buf), in_W1[l].rearrange("(kk p) n -> p kk n", p=128)[:, :, j * 256:(j + 1) * 256]),
                          inc=("W10" if buf == 0 else "W11", 16))
                    em.wait("tensor", "W10" if buf == 0 else "W11", em.now("W10" if buf == 0 else "W11"))
                    for nt in range(2):
                        em.wait("tensor", "V", em.now("V"))
                        em.wait("tensor", "S", em.now("S"))
                        for k in range(NCH):
                            em.op("tensor", lambda e, k=k, nt=nt, buf=buf:
                                  e.matmul(PS[0][:], W1Ba(buf)[:, k * 256:k * 256 + 128],
                                           XNa(k)[:, nt * 512:(nt + 1) * 512], start=(k == 0), stop=(k == NCH - 1)),
                                  inc=None)
                            em.op("tensor", lambda e, k=k, nt=nt, buf=buf:
                                  e.matmul(PS[1][:], W1Ba(buf)[:, k * 256 + 128:k * 256 + 256],
                                           XNa(k)[:, nt * 512:(nt + 1) * 512], start=(k == 0), stop=(k == NCH - 1)),
                                  inc=("T", 1) if k == NCH - 1 else None)
                        em.wait("scalar", "T", em.now("T"))
                        b1g = B1P[:, l * 44 + 2 * j + 1:l * 44 + 2 * j + 2]
                        b1a = B1P[:, l * 44 + 2 * j:l * 44 + 2 * j + 1]
                        em.op("scalar", lambda e, b=b1g: e.activation(GAa, PS[1][:], AF.Gelu, bias=b), inc=("S", 1))
                        em.wait("vector", "S", em.now("S"))
                        em.op("vector", lambda e, j=j, nt=nt, b=b1a:
                              e.scalar_tensor_tensor(H2Ha(j)[:, nt * 512:(nt + 1) * 512], PS[0][:], b, GAa,
                                                     ALU.add, ALU.mult),
                              inc=("V", 1))
                    t_pair_done[gp] = em.now("T")
                em.wait("tensor", "W2L", w2_done)
                for tg in range(4 if not (DBG_FFN_XN_ONLY or DBG_FFN_NO_W2) else 0):
                    tc = tg * 256
                    em.wait("tensor", "V", em.now("V"))
                    for j in range(NPAIR):
                        for m2 in range(NCH):
                            bank = PS[2 + m2 // 2]
                            colb = (m2 % 2) * 256
                            em.op("tensor", lambda e, j=j, m2=m2, bank=bank, colb=colb, tc=tc:
                                  e.matmul(bank[:, colb:colb + 256], W2Sa(j)[:, m2 * 128:(m2 + 1) * 128],
                                           H2Ha(j)[:, tc:tc + 256],
                                           start=(j == 0 and m2 % 2 == 0), stop=(j == NPAIR - 1 and m2 % 2 == 1)),
                                  inc=("T", 1) if (j == NPAIR - 1 and m2 == NCH - 1) else None)
                    em.wait("vector", "T", em.now("T"))
                    for m2 in range(NCH):
                        bank = PS[2 + m2 // 2]
                        colb = (m2 % 2) * 256
                        b2col = B2P[:, l * 8 + m2:l * 8 + m2 + 1]
                        em.op("vector", lambda e, m2=m2, bank=bank, colb=colb, tc=tc, tb=tb, b2c=b2col:
                              e.scalar_tensor_tensor(XR[:, m2 * T_OWN + tb + tc:m2 * T_OWN + tb + tc + 256],
                                                     bank[:, colb:colb + 256], b2c,
                                                     XR[:, m2 * T_OWN + tb + tc:m2 * T_OWN + tb + tc + 256],
                                                     ALU.add, ALU.add),
                              inc=("V", 1) if m2 == NCH - 1 else None)

        def emit_shf(l, t0):
            for k in range(NCH):
                if k < 4:
                    em.op("vector", lambda e, k=k, t0=t0:
                          e.tensor_copy(SHF[:, k * 256:(k + 1) * 256],
                                        XR[:, k * T_OWN + t0:k * T_OWN + t0 + 256]),
                          inc=("V", 1) if k == NCH - 1 else None)
                else:
                    if t0 == 0:
                        em.op("vector", lambda e, k=k:
                              e.tensor_scalar_mul(SHF[:, k * 256:k * 256 + 1], XUH[:, k:k + 1], FLG[:, 0:1]),
                              inc=None)
                        em.op("vector", lambda e, k=k:
                              e.tensor_copy(SHF[:, k * 256 + 1:(k + 1) * 256],
                                            XR[:, k * T_OWN:k * T_OWN + 255]),
                              inc=("V", 1) if k == NCH - 1 else None)
                    elif t0 == 1024:
                        em.op("vector", lambda e, k=k:
                              e.tensor_copy(SHF[:, k * 256:k * 256 + 1], SAVE2[:, k:k + 1]),
                              inc=None)
                        em.op("vector", lambda e, k=k, t0=t0:
                              e.tensor_copy(SHF[:, k * 256 + 1:(k + 1) * 256],
                                            XR[:, k * T_OWN + t0:k * T_OWN + t0 + 255]),
                              inc=("V", 1) if k == NCH - 1 else None)
                    else:
                        em.op("vector", lambda e, k=k, t0=t0:
                              e.tensor_copy(SHF[:, k * 256:(k + 1) * 256],
                                            XR[:, k * T_OWN + t0 - 1:k * T_OWN + t0 + 255]),
                              inc=("V", 1) if k == NCH - 1 else None)

        def emit_layer(l):
            emit_attn_stage(l)
            if not DBG_NO_FFN:
                emit_ffn_stage(l)
            if l < DEPTH - 1:
                em.wait("sync", "V", em.now("V"))
                hsrc2 = XR[:].rearrange("p (c t) -> p c t", c=NCH)[:, :, T_OWN - 513:T_OWN]
                dma("sync", hin.ap().rearrange("p (c t) -> p c t", c=NCH), hsrc2, "L")
                em.wait("gpsimd", "L", em.now("L"))
                em.op("gpsimd", lambda e: e.collective_compute(
                    "AllGather", ALU.bypass, replica_groups=[[0, 1, 2, 3, 4, 5, 6, 7]],
                    ins=[hin.ap().opt()], outs=[hout.ap().opt()]), inc=("C", 1))

        def emit_final():
            for fg in range(8):
                t0 = fg * 256
                em.wait("sync", "V", em.now("V"))
                em.op("sync", lambda e, t0=t0:
                      e.dma_start(PFG.rearrange("p (c t) -> p c t", c=NCH),
                                  park.ap().rearrange("p (c t) -> p c t", c=NCH)[:, :, t0:t0 + 256]),
                      inc=("PK", 16))
                em.wait("vector", "PK", em.now("PK"))
                for k in range(NCH):
                    em.op("vector", lambda e, k=k, t0=t0:
                          e.tensor_tensor(PFG[:, k * 256:(k + 1) * 256], PFG[:, k * 256:(k + 1) * 256],
                                          XR[:, k * T_OWN + t0:k * T_OWN + t0 + 256], ALU.add),
                          inc=("V", 1) if k == NCH - 1 else None)
                slices = [PFG[:, k * 256:(k + 1) * 256] for k in range(NCH)]
                ln_stats_rolling(slices, 256)
                for k in range(NCH):
                    em.op("vector", lambda e, k=k, t0=t0:
                          e.tensor_tensor(PFG[:, k * 256:(k + 1) * 256], PFG[:, k * 256:(k + 1) * 256],
                                          PS[6][:, 0:256], ALU.mult),
                          inc=None)
                    em.op("vector", lambda e, k=k, t0=t0:
                          e.tensor_tensor(XR[:, k * T_OWN + t0:k * T_OWN + t0 + 256],
                                          PFG[:, k * 256:(k + 1) * 256], PS[3][:, 0:256], ALU.add),
                          inc=("V", 1) if k == NCH - 1 else None)

        for l in range(N_LAYERS):
            emit_layer(l)
        if N_LAYERS == DEPTH and not OUTPUT_X1:
            emit_final()

        for tt in range(16):
            for half in range(2):
                em.wait("tensor", "V", em.now("V"))
                em.wait("tensor", "O", em.now("O"))
                buf = XIN[half]
                for kk in range(4):
                    k = half * 4 + kk
                    bank = PS[kk % 2]
                    if kk >= 2:
                        em.wait("tensor", "V", em.now("V") - 1)
                    em.op("tensor", lambda e, b=bank, tt=tt, k=k:
                          e.transpose(b[:, 0:128], XR[:, k * T_OWN + tt * 128:k * T_OWN + (tt + 1) * 128], IDNF[:]),
                          inc=("T", 1))
                    em.wait("vector", "T", em.now("T"))
                    em.op("vector", lambda e, b=bank, kk=kk, buf=buf:
                          e.tensor_copy(buf[:, kk * 128:(kk + 1) * 128], b[:, 0:128]),
                          inc=("V", 1))
                em.wait("sync", "V", em.now("V"))
                em.op("sync", lambda e, tt=tt, half=half, buf=buf:
                      e.dma_start(out_tm[tt * 128:(tt + 1) * 128, half * 512:(half + 1) * 512], buf[:]),
                      inc=("O", 16))
        em.wait("sync", "O", em.now("O"))
        em.wait("sync", "L", em.now("L"))
        em.wait("sync", "G", em.now("G"))

        em.check_deadlock()
        sems = {name: ctx.enter_context(nc.semaphore(f"sem{name}"))
                for name in ("L", "G", "T", "V", "S", "C", "O", "X0", "X1",
                             "WQ", "WO", "W10", "W11", "W2L", "HS", "HG", "PK")}
        with nc.Block() as block:
            em.emit_into(block, sems)

    return nc


def _stage_inputs(inputs):
    f32 = lambda a: np.ascontiguousarray(np.asarray(a, dtype=np.float32))
    x = f32(inputs["L0_attn_x"])
    Wqkv = f32(inputs["L0_ffn_Wqkv"])
    qsc = f32(inputs["L0_ffn_q_scale"]); ksc = f32(inputs["L0_ffn_k_scale"])
    Wo = f32(inputs["L0_ffn_Wo"]); lng = f32(inputs["L0_ffn_ln_g"]); lnb = f32(inputs["L0_ffn_ln_b"])
    W1 = f32(inputs["L0_ffn_W1"]); b1 = f32(inputs["L0_ffn_b1"])
    W2 = f32(inputs["L0_ffn_W2"]); b2 = f32(inputs["L0_ffn_b2"])

    WqkvP = np.ascontiguousarray(np.concatenate(
        [Wqkv[:, :, 0:1024][:, :, PERM], Wqkv[:, :, 1024:2048][:, :, PERM], Wqkv[:, :, 2048:3072]], axis=2))
    W1p = np.zeros((DEPTH, D, D1P), np.float32)
    W1p[:, :, W1VALID] = W1[:, :, W1COLS[W1VALID]]
    b1pv = np.zeros((DEPTH, D1P), np.float32)
    b1pv[:, W1VALID] = b1[:, W1COLS[W1VALID]]
    W2p = np.zeros((DEPTH, NPAIR * 128, D), np.float32)
    for j in range(NPAIR):
        s0, e0 = j * 128, min(j * 128 + 128, 2730)
        W2p[:, j * 128:j * 128 + (e0 - s0), :] = W2[:, s0:e0, :]

    b1cols = np.zeros((128, DEPTH, 44), np.float32)
    for l in range(DEPTH):
        for j in range(NPAIR):
            b1cols[:, l, 2 * j] = b1pv[l, j * 256:j * 256 + 128]
            b1cols[:, l, 2 * j + 1] = b1pv[l, j * 256 + 128:j * 256 + 256]
    b2cols = np.zeros((128, DEPTH, 8), np.float32)
    for l in range(DEPTH):
        b2cols[:, l, :] = b2[l].reshape(8, 128).T
    lnp = np.zeros((128, DEPTH, 16), np.float32)
    for l in range(DEPTH):
        lnp[:, l, 0:8] = lng[l].reshape(8, 128).T
        lnp[:, l, 8:16] = lnb[l].reshape(8, 128).T
    qks = np.zeros((128, DEPTH, 4), np.float32)
    for l in range(DEPTH):
        qks[:, l, 0] = np.tile(qsc[l, 0:32], 4)
        qks[:, l, 1] = np.tile(qsc[l, 32:64], 4)
        qks[:, l, 2] = np.tile(ksc[l, 0:32], 4)
        qks[:, l, 3] = np.tile(ksc[l, 32:64], 4)

    in_maps = []
    for c in range(8):
        b, hh = c // 2, c % 2
        Cq, Sq, Ck, Sk = host_rotary_tables(hh)
        tabsT = np.zeros((128, 4 * TLOC), np.float32)
        for ti, tab in enumerate((Cq, Sq, Ck, Sk)):
            tabsT[:, ti * TLOC:(ti + 1) * TLOC] = np.tile(tab.T, (4, 1))
        flg = np.zeros((128, 4), np.float32)
        flg[:, 0] = 0.0 if hh == 0 else 1.0
        flg[:, 1] = 1.0 if hh == 0 else 0.0
        m = host_masks(hh)
        im = {
            "x_tm": np.ascontiguousarray(x[b, hh * T_OWN:(hh + 1) * T_OWN, :]),
            "tabs": tabsT,
            "masks": np.ascontiguousarray(m),
            "flags": flg,
            "qks": np.ascontiguousarray(qks.reshape(128, DEPTH * 4)),
            "lnp": np.ascontiguousarray(lnp.reshape(128, DEPTH * 16)),
            "b1p": np.ascontiguousarray(b1cols.reshape(128, DEPTH * 44)),
            "b2p": np.ascontiguousarray(b2cols.reshape(128, DEPTH * 8)),
            "dn_Wq": f32(inputs["L0_attn_Wq"]),
            "dn_Wk": f32(inputs["L0_attn_Wk"]),
            "dn_Wv": f32(inputs["L0_attn_Wv"]),
            "dn_Wb": f32(inputs["L0_attn_Wbeta"]),
            "dn_Wo": f32(inputs["L0_attn_Wout"]),
            "in_Wqkv": WqkvP,
            "in_Wo": Wo,
            "in_W1": W1p,
            "in_W2": W2p,
        }
        in_maps.append(im)
    return in_maps


def kernel(**inputs):
    in_maps = _stage_inputs(inputs)
    nc = build_program()
    res = run_bass_kernel_spmd(nc, in_maps, core_ids=list(range(8)))
    out = np.zeros((4, 4096, 1024), np.float32)
    for c in range(8):
        b, hh = c // 2, c % 2
        out[b, hh * T_OWN:(hh + 1) * T_OWN, :] = res.results[c]["out_tm"]
    return out


# version 06
# speedup vs baseline: 1.0673x; 1.0182x over previous
import numpy as np
import ml_dtypes
import concourse.bass as bass
import concourse.mybir as mybir
from concourse.bass_utils import run_bass_kernel_spmd

F32 = mybir.dt.float32
BF16 = mybir.dt.bfloat16
AF = mybir.ActivationFunctionType
ALU = mybir.AluOpType
AX = mybir.AxisListType
bft = ml_dtypes.bfloat16

H, DM = 16, 64
D = 1024
WIN = 256
T_OWN = 2048
HALO = 512
NCH = 8
DEPTH = 6
D1P = 5632
NPAIR = 22
EPS = 1e-5
TLOC = HALO + T_OWN

ENGINES = ("sync", "gpsimd", "scalar", "vector", "tensor")

N_LAYERS = 6
OUTPUT_X1 = False
DBG_NO_WO = False
DBG_NO_FFN = False
DBG_NO_ATT = False
DBG_FFN_XN_ONLY = False
DBG_FFN_NO_W2 = False
DBG_DUMP = False
DBG_DUMP2 = False
DBG_DUMP3 = False
SCAN_CHUNKS = list(range(15, -1, -1))


def qk_perm():
    p = []
    for g in range(4):
        for half in range(2):
            for hh in range(4):
                head = 4 * g + hh
                p.extend(range(head * 64 + half * 32, head * 64 + half * 32 + 32))
    return np.array(p)

PERM = qk_perm()

def w1_cols():
    cols = np.zeros(D1P, dtype=np.int64)
    valid = np.zeros(D1P, dtype=bool)
    o = 0
    for j in range(NPAIR):
        s0, e0 = j * 128, min(j * 128 + 128, 2730)
        n = e0 - s0
        cols[o:o + n] = np.arange(s0, e0); valid[o:o + n] = True
        o += 128
        cols[o:o + n] = 2730 + np.arange(s0, e0); valid[o:o + n] = True
        o += 128
    return cols, valid

W1COLS, W1VALID = w1_cols()

def host_rotary_tables(h):
    half = np.arange(0, DM, 2, dtype=np.float64)
    inv_freq = 1.0 / 10000.0 ** (half / DM)
    base_scale = (half + 0.4 * DM) / (1.4 * DM)
    start = h * T_OWN - HALO
    pos = np.clip(np.arange(start, start + TLOC), 0, 4095).astype(np.float64)
    fr = pos[:, None] * inv_freq[None, :]
    power = (pos - 2048.0) / (WIN // 2)
    scale = base_scale[None, :] ** power[:, None]
    c, s = np.cos(fr), np.sin(fr)
    return (np.asarray(c * scale, np.float32), np.asarray(s * scale, np.float32),
            np.asarray(c / scale, np.float32), np.asarray(s / scale, np.float32))

def host_masks(h):
    i = np.arange(128)
    j = np.arange(512)
    tiles = []
    for qt in range(2):
        qq = (i + qt * 128)[:, None] + 256
        keep = (j[None, :] >= qq - WIN) & (j[None, :] <= qq)
        tiles.append(keep.astype(np.float32))
    for qt in range(2):
        qq = (i + qt * 128)[:, None] + 256
        keep = (j[None, :] >= qq - WIN) & (j[None, :] <= qq)
        if h == 0:
            keep = keep & (j[None, :] >= 256)
        tiles.append(keep.astype(np.float32))
    return np.concatenate(tiles, axis=1).astype(bft)


class Em:
    def __init__(self):
        self.q = {e: [] for e in ENGINES}
        self.cnt = {}

    def now(self, sem):
        return self.cnt.get(sem, 0)

    def op(self, eng, fn, inc=None, desc=""):
        if inc is not None:
            self.cnt[inc[0]] = self.cnt.get(inc[0], 0) + inc[1]
        self.q[eng].append(("op", fn, inc, desc))
        return self.cnt.get(inc[0], 0) if inc else None

    def wait(self, eng, sem, val, desc=""):
        if val is None or val <= 0:
            return
        q = self.q[eng]
        if q and q[-1][0] == "wait" and q[-1][1] == sem:
            if q[-1][2] >= val:
                return
            q.pop()
        q.append(("wait", sem, val, desc))

    def check_deadlock(self):
        ptr = {e: 0 for e in ENGINES}
        val = {}
        total = sum(len(v) for v in self.q.values())
        done = 0
        progress = True
        while progress:
            progress = False
            for e in ENGINES:
                while ptr[e] < len(self.q[e]):
                    it = self.q[e][ptr[e]]
                    if it[0] == "wait":
                        if val.get(it[1], 0) < it[2]:
                            break
                    else:
                        if it[2] is not None:
                            val[it[2][0]] = val.get(it[2][0], 0) + it[2][1]
                    ptr[e] += 1
                    done += 1
                    progress = True
        if done != total:
            lines = [f"  {e}@{ptr[e]}/{len(self.q[e])}: {self.q[e][ptr[e]][:3]}"
                     for e in ENGINES if ptr[e] < len(self.q[e])]
            raise RuntimeError("DEADLOCK in schedule:\n" + "\n".join(lines))

    def emit_into(self, block, sems):
        for e in ENGINES:
            items = self.q[e]
            def body(eng, items=items):
                for it in items:
                    if it[0] == "wait":
                        eng.wait_ge(sems[it[1]], it[2])
                    else:
                        ins = it[1](eng)
                        if it[2] is not None:
                            assert ins is not None, f"inc on None: {it[3]}"
                            ins.then_inc(sems[it[2][0]], it[2][1])
            getattr(block, e)(body)


def build_program():
    nc = bass.Bass("TRN2", target_bir_lowering=False, detect_race_conditions=False)

    def din(name, shape, dt=F32):
        return nc.dram_tensor(name, shape, dt, kind="ExternalInput").ap()

    x_tm = din("x_tm", [T_OWN, D])
    tabs = din("tabs", [128, 4 * TLOC], BF16)
    masks_d = nc.dram_tensor("masks", [128, 4 * 512], BF16, kind="ExternalInput").ap()
    flags_d = din("flags", [128, 4])
    qks_d = din("qks", [128, DEPTH * 4])
    lnp_d = din("lnp", [128, DEPTH * 16])
    b1p_d = din("b1p", [128, DEPTH * 44])
    b2p_d = din("b2p", [128, DEPTH * 8])
    dn_Wq = din("dn_Wq", [D, D]); dn_Wk = din("dn_Wk", [D, D])
    dn_Wv = din("dn_Wv", [D, D]); dn_Wb = din("dn_Wb", [D, 16])
    dn_Wo = din("dn_Wo", [D, D])
    in_Wqkv = din("in_Wqkv", [DEPTH, D, 3 * D], BF16)
    in_Wo = din("in_Wo", [DEPTH, D, D], BF16)
    in_W1 = din("in_W1", [DEPTH, D, D1P], BF16)
    in_W2 = din("in_W2", [DEPTH, NPAIR * 128, D], BF16)

    out_tm = nc.dram_tensor("out_tm", [T_OWN, D], F32, kind="ExternalOutput").ap()

    park = nc.dram_tensor("park", [128, NCH * T_OWN], F32)
    cin = nc.dram_tensor("cin", [64, H * 64], F32)
    cout = nc.dram_tensor("cout", [8 * 64, H * 64], F32)
    hin = nc.dram_tensor("hin", [128, NCH * 513], F32)
    hout = nc.dram_tensor("hout", [8 * 128, NCH * 513], F32)
    hloc = nc.dram_tensor("hloc", [128, NCH * 513], F32)

    idn_b_d = nc.inline_tensor(np.eye(128, dtype=bft), name="c_idnb").ap()
    idn_f_d = nc.inline_tensor(np.eye(128, dtype=np.float32), name="c_idnf").ap()
    pat_dn = np.zeros((128, NCH * 16), dtype=bft)
    for c in range(NCH):
        pat_dn[0:64, c * 16 + 2 * c] = 1
        pat_dn[64:128, c * 16 + 2 * c + 1] = 1
    pat_dn_d = nc.inline_tensor(pat_dn, name="c_patdn").ap()
    bcp_dn = np.zeros((16, NCH * 128), dtype=bft)
    for c in range(NCH):
        bcp_dn[2 * c, c * 128 + 0:c * 128 + 64] = 1
        bcp_dn[2 * c + 1, c * 128 + 64:c * 128 + 128] = 1
    bcp_dn_d = nc.inline_tensor(bcp_dn, name="c_bcpdn").ap()
    sel = np.zeros((128, 128), dtype=bft)
    sel[0:64, 0:64] = np.eye(64)
    sel[64:128, 64:128] = np.eye(64)
    sel_d = nc.inline_tensor(sel, name="c_sel2").ap()
    um = (np.arange(128)[:, None] >= np.arange(128)[None, :]).astype(bft)
    um_d = nc.inline_tensor(um, name="c_um").ap()
    pat_in = np.zeros((128, 4 * 16), dtype=bft)
    for g in range(4):
        for j in range(4):
            pat_in[j * 32:(j + 1) * 32, g * 16 + 4 * g + j] = 1
    pat_in_d = nc.inline_tensor(pat_in, name="c_patin").ap()
    bcp_in = np.zeros((16, 4 * 128), dtype=bft)
    for g in range(4):
        for j in range(4):
            bcp_in[4 * g + j, g * 128 + j * 32:g * 128 + (j + 1) * 32] = 1
    bcp_in_d = nc.inline_tensor(bcp_in, name="c_bcpin").ap()
    ones_col_d = nc.inline_tensor(np.ones((128, 2), dtype=bft), name="c_onesc").ap()
    ones_row_d = nc.inline_tensor(np.ones((1, 128), dtype=bft), name="c_onesr").ap()

    em = Em()
    from contextlib import ExitStack
    with ExitStack() as ctx:
        sbt = lambda n, s, d: ctx.enter_context(nc.sbuf_tensor(n, s, d))
        ptt = lambda n, s, d: ctx.enter_context(nc.psum_tensor(n, s, d))

        XR = sbt("XR", [128, NCH * T_OWN], F32)
        IDNB = sbt("IDNB", [128, 128], BF16)
        IDNF = sbt("IDNF", [128, 128], F32)
        PATD = sbt("PATD", [128, NCH * 16], BF16)
        BCPD = sbt("BCPD", [16, NCH * 128], BF16)
        PATI = sbt("PATI", [128, 64], BF16)
        BCPI = sbt("BCPI", [16, 512], BF16)
        SEL2 = sbt("SEL2", [128, 128], BF16)
        UM = sbt("UM", [128, 128], BF16)
        FLG = sbt("FLG", [128, 4], F32)
        QKS = sbt("QKS", [128, DEPTH * 4], F32)
        LNP = sbt("LNP", [128, DEPTH * 16], F32)
        B1P = sbt("B1P", [128, DEPTH * 44], F32)
        B2P = sbt("B2P", [128, DEPTH * 8], F32)
        ONEC = sbt("ONEC", [128, 2], BF16)
        ONER = sbt("ONER", [1, 128], BF16)

        AB = sbt("AB", [128, 61600], BF16)
        ARF = sbt("ARF", [128, 2048], F32)
        SS = sbt("SS", [128, 512], F32)
        SSB = sbt("SSB", [128, 3200], BF16)

        PS = [ptt(f"ps{i}", [128, 512], F32) for i in range(7)]
        PB = ptt("psb", [128, 1024], BF16)

        QHAT = AB[:, 0:16384]
        KHAT = AB[:, 16384:32768]
        VFM = AB[:, 32768:49152]
        WB = AB[:, 49152:57344]
        X8 = AB[:, 57344:61440]
        VTM = AB[:, 57344:58368]
        KTM = AB[:, 58368:59392]
        HM = AB[:, 59392:59904]
        BBS = AB[:, 59904:60416]
        OUTB = AB[:, 60416:61440]
        WTS = AB[:, 61440:61568]
        PRJ = AB[:, 32768:33280]
        XB1 = AB[:, 57344:58368]
        XB2 = AB[:, 58368:59392]
        BETA = SSB[0:16, 0:2048]
        S_SB = SSB[0:64, 2048:3072]
        WBETA = SSB[:, 3072:3200]
        RSQ = SS[0:16, 0:512]
        MUs = SS[0:1, 0:128]
        MSQ = SS[0:1, 256:384]
        TMP1 = SS[0:1, 384:512]
        XIN = [ARF[:, 0:512], ARF[:, 512:1024]]
        CARF = ARF[0:64, 0:1024]
        S_F32 = ARF[0:64, 0:1024]
        XSG = ARF[:, 1024:2048]
        ZSC = SS[:, 128:256]

        def dma(engq, dst, src, sem):
            em.op(engq, lambda e, dst=dst, src=src: e.dma_start(dst, src), inc=(sem, 16))

        for dst, src in ((IDNB[:], idn_b_d), (IDNF[:], idn_f_d), (PATD[:], pat_dn_d),
                         (BCPD[:], bcp_dn_d), (PATI[:], pat_in_d), (BCPI[:], bcp_in_d),
                         (SEL2[:], sel_d), (UM[:], um_d), (FLG[:], flags_d),
                         (QKS[:], qks_d), (LNP[:], lnp_d), (B1P[:], b1p_d),
                         (B2P[:], b2p_d), (ONEC[:], ones_col_d),
                         (ONER[:], ones_row_d)):
            dma("sync", dst, src, "L")
        consts_L = em.now("L")
        for e_ in ("tensor", "vector", "scalar", "gpsimd"):
            em.wait(e_, "L", consts_L)

        for th in range(32):
            tt, half = th // 2, th % 2
            buf = XIN[th % 2]
            bsem = "X0" if th % 2 == 0 else "X1"
            if th >= 2:
                em.wait("sync", "T", em.now("T"))
            dma("sync", buf, x_tm[tt * 128:(tt + 1) * 128, half * 512:(half + 1) * 512], bsem)
            em.wait("tensor", bsem, em.now(bsem))
            for kk in range(4):
                k = half * 4 + kk
                bank = PS[kk % 2]
                if th > 0 or kk >= 2:
                    em.wait("tensor", "V", em.now("V") - 1)
                em.op("tensor", lambda e, b=bank, src=buf, kk=kk:
                      e.transpose(b[:, 0:128], src[:, kk * 128:(kk + 1) * 128], IDNF[:]),
                      inc=("T", 1))
                em.wait("vector", "T", em.now("T"))
                em.op("vector", lambda e, b=bank, tt=tt, k=k:
                      e.tensor_copy(XR[:, k * T_OWN + tt * 128:k * T_OWN + (tt + 1) * 128], b[:, 0:128]),
                      inc=("V", 1))


        def load_w_slabs(wsrc, ncols=1024):
            for k in range(NCH):
                em.op("gpsimd", lambda e, k=k, w=wsrc, nc_=ncols:
                      e.dma_start(WB[:, k * nc_:(k + 1) * nc_], w[k * 128:(k + 1) * 128, :]),
                      inc=("G", 16))
            return em.now("G")

        def cast_group(n):
            for k in range(NCH):
                em.op("vector", lambda e, k=k, n=n:
                      e.tensor_copy(X8[:, k * 512:(k + 1) * 512],
                                    XR[:, k * T_OWN + n * 512:k * T_OWN + (n + 1) * 512]),
                      inc=("V", 1))
            return em.now("V")

        def dn_projection(dst, want_norm, sqrt_scale, g_after):
            for n in range(4):
                em.wait("vector", "T", em.now("T"))
                cast_v = cast_group(n)
                em.wait("tensor", "V", cast_v)
                em.wait("tensor", "G", g_after)
                for m in range(NCH):
                    bank = PS[m % 2]
                    em.wait("tensor", "V", em.now("V") if m >= 2 else cast_v)
                    if want_norm:
                        em.wait("tensor", "S", em.now("S"))
                    for k in range(NCH):
                        em.op("tensor", lambda e, m=m, k=k, b=bank:
                              e.matmul(b[:], WB[:, k * 1024 + m * 128:k * 1024 + (m + 1) * 128],
                                       X8[:, k * 512:(k + 1) * 512],
                                       start=(k == 0), stop=(k == NCH - 1)),
                              inc=("T", 1) if k == NCH - 1 else None)
                    t_mm = em.now("T")
                    if want_norm:
                        em.wait("scalar", "T", t_mm)
                        em.op("scalar", lambda e, b=bank: e.activation(PRJ[:, 0:512], b[:], AF.Square),
                              inc=("S", 1))
                        em.wait("tensor", "S", em.now("S"))
                        em.op("tensor", lambda e, m=m:
                              e.matmul(PS[3][0:16, 0:512], PATD[:, m * 16:(m + 1) * 16], PRJ[:, 0:512],
                                       start=(m == 0), stop=(m == NCH - 1)),
                              inc=("T", 1))
                        em.wait("vector", "T", em.now("T"))
                        em.op("vector", lambda e, m=m, n=n, b=bank, d=dst:
                              e.tensor_copy(d[:, m * T_OWN + n * 512:m * T_OWN + (n + 1) * 512], b[:]),
                              inc=("V", 1))
                    else:
                        em.wait("vector", "T", t_mm)
                        em.op("vector", lambda e, m=m, n=n, b=bank, d=dst:
                              e.tensor_copy(d[:, m * T_OWN + n * 512:m * T_OWN + (n + 1) * 512], b[:]),
                              inc=("V", 1))
                if want_norm:
                    em.wait("vector", "T", em.now("T"))
                    em.op("vector", lambda e: e.tensor_scalar_max(RSQ, PS[3][0:16, 0:512], 1e-24), inc=("V", 1))
                    em.op("vector", lambda e: e.reciprocal(RSQ, RSQ), inc=("V", 1))
                    em.wait("scalar", "V", em.now("V"))
                    em.op("scalar", lambda e, sc=sqrt_scale: e.activation(RSQ, RSQ, AF.Sqrt, scale=sc), inc=("S", 1))
                    em.wait("vector", "S", em.now("S"))
                    em.op("vector", lambda e: e.tensor_copy(PRJ[0:16, 0:512], RSQ), inc=("V", 1))
                    for m in range(NCH):
                        em.wait("tensor", "V", em.now("V"))
                        em.op("tensor", lambda e, m=m:
                              e.matmul(PS[2][:], BCPD[:, m * 128:(m + 1) * 128], PRJ[0:16, 0:512],
                                       start=True, stop=True),
                              inc=("T", 1))
                        em.wait("vector", "T", em.now("T"))
                        em.op("vector", lambda e, m=m, n=n, d=dst:
                              e.tensor_tensor(d[:, m * T_OWN + n * 512:m * T_OWN + (n + 1) * 512],
                                              d[:, m * T_OWN + n * 512:m * T_OWN + (n + 1) * 512],
                                              PS[2][:], ALU.mult),
                              inc=("V", 1))

        g = load_w_slabs(dn_Wq)
        for k in range(NCH):
            em.op("gpsimd", lambda e, k=k: e.dma_start(WBETA[:, k * 16:(k + 1) * 16], dn_Wb[k * 128:(k + 1) * 128, :]),
                  inc=("G", 16))
        g_beta = em.now("G")
        dn_projection(QHAT, True, 1.0 / 64.0, g_beta)

        for n in range(4):
            em.wait("vector", "T", em.now("T"))
            cast_v = cast_group(n)
            em.wait("tensor", "V", cast_v)
            em.wait("tensor", "G", g_beta)
            em.wait("tensor", "S", em.now("S"))
            for k in range(NCH):
                em.op("tensor", lambda e, k=k:
                      e.matmul(PS[6][0:16, 0:512], WBETA[:, k * 16:(k + 1) * 16], X8[:, k * 512:(k + 1) * 512],
                               start=(k == 0), stop=(k == NCH - 1)),
                      inc=("T", 1) if k == NCH - 1 else None)
            em.wait("scalar", "T", em.now("T"))
            em.op("scalar", lambda e, n=n: e.activation(BETA[:, n * 512:(n + 1) * 512], PS[6][0:16, 0:512], AF.Sigmoid),
                  inc=("S", 1))

        em.wait("gpsimd", "T", em.now("T"))
        g = load_w_slabs(dn_Wk)
        dn_projection(KHAT, True, 1.0, g)
        em.wait("gpsimd", "T", em.now("T"))
        g = load_w_slabs(dn_Wv)
        dn_projection(VFM, False, None, g)

        def s_ap(h):
            b = PS[4 + h // 8]
            return b[0:64, (h % 8) * 64:(h % 8 + 1) * 64]

        def transpose_chunk(src_hb, dst, c):
            em.wait("tensor", "S", em.now("S"))
            for p in range(NCH):
                em.op("tensor", lambda e, p=p, c=c, s_=src_hb:
                      e.transpose(PB[:, p * 128:(p + 1) * 128],
                                  s_[:, p * T_OWN + c * 128:p * T_OWN + (c + 1) * 128], IDNB[:]),
                      inc=("T", 1) if p == NCH - 1 else None)
            em.wait("scalar", "T", em.now("T"))
            em.op("scalar", lambda e, d=dst: e.copy(d, PB[:]), inc=("S", 1))

        for c in range(16):
            transpose_chunk(KHAT, KTM[:], c)
            transpose_chunk(VFM, VTM[:], c)
            em.wait("tensor", "S", em.now("S"))
            for h in range(H):
                em.op("tensor", lambda e, h=h, c=c:
                      e.matmul(s_ap(h), KTM[:, h * 64:(h + 1) * 64], VTM[:, h * 64:(h + 1) * 64],
                               start=(c == 0 and h % 8 == 0), stop=(c == 15 and h % 8 == 7)),
                      inc=("T", 1) if h == H - 1 else None)

        em.wait("vector", "T", em.now("T"))
        SOUT = ARF[0:64, 0:512]
        SOUT2 = ARF[0:64, 512:1024]
        em.op("vector", lambda e: e.tensor_copy(SOUT, PS[4][0:64, 0:512]), inc=("V", 1))
        em.op("vector", lambda e: e.tensor_copy(SOUT2, PS[5][0:64, 0:512]), inc=("V", 1))
        em.wait("sync", "V", em.now("V"))
        dma("sync", cin.ap()[0:64, 0:512], SOUT, "L")
        dma("sync", cin.ap()[0:64, 512:1024], SOUT2, "L")
        em.wait("gpsimd", "L", em.now("L"))
        em.op("gpsimd", lambda e: e.collective_compute(
            "AllGather", ALU.bypass, replica_groups=[[0, 1, 2, 3, 4, 5, 6, 7]],
            ins=[cin.ap().opt()], outs=[cout.ap().opt()]), inc=("C", 1))
        em.wait("sync", "C", em.now("C"))
        def recv_carry(e):
            pid = e.partition_id()
            off = ((pid + 1) % 8) * (64 * 1024)
            return e.dma_start(CARF, bass.AP(cout, off, [[1024, 64], [1, 1024]]))
        em.op("sync", recv_carry, inc=("L", 16))
        em.wait("vector", "L", em.now("L"))
        em.op("vector", lambda e: e.tensor_scalar_mul(S_F32, CARF, FLG[0:64, 1:2]), inc=None)
        em.op("vector", lambda e: e.tensor_copy(S_SB[:, 0:1024], S_F32), inc=("V", 1))

        em.wait("gpsimd", "T", em.now("T"))
        g_wo = load_w_slabs(dn_Wo)

        def ln_chunk(dst_col):
            em.op("vector", lambda e: e.tensor_copy(XB1[:, 0:1024], XSG), inc=("V", 1))
            em.wait("scalar", "V", em.now("V"))
            em.op("scalar", lambda e: e.activation(XB2[:, 0:1024], XSG, AF.Square), inc=("S", 1))
            em.wait("tensor", "S", em.now("S"))
            em.wait("tensor", "V", em.now("V"))
            for p in range(NCH):
                em.op("tensor", lambda e, p=p:
                      e.matmul(PS[6][0:1, 0:128], ONEC[:, 0:1], XB1[:, p * 128:(p + 1) * 128],
                               start=(p == 0), stop=(p == NCH - 1)),
                      inc=("T", 1) if p == NCH - 1 else None)
            for p in range(NCH):
                em.op("tensor", lambda e, p=p:
                      e.matmul(PS[3][0:1, 0:128], ONEC[:, 0:1], XB2[:, p * 128:(p + 1) * 128],
                               start=(p == 0), stop=(p == NCH - 1)),
                      inc=("T", 1) if p == NCH - 1 else None)
            em.wait("vector", "T", em.now("T"))
            em.op("vector", lambda e: e.tensor_scalar_mul(MUs, PS[6][0:1, 0:128], 1.0 / D), inc=("V", 1))
            em.op("vector", lambda e: e.tensor_scalar_mul(MSQ, PS[3][0:1, 0:128], 1.0 / D), inc=("V", 1))
            em.op("vector", lambda e: e.tensor_tensor(TMP1, MUs, MUs, ALU.mult), inc=("V", 1))
            em.op("vector", lambda e: e.tensor_sub(MSQ, MSQ, TMP1), inc=("V", 1))
            em.op("vector", lambda e: e.tensor_scalar_add(MSQ, MSQ, EPS), inc=("V", 1))
            em.wait("scalar", "V", em.now("V"))
            em.op("scalar", lambda e: e.activation(MSQ, MSQ, AF.Sqrt), inc=("S", 1))
            em.wait("vector", "S", em.now("S"))
            em.op("vector", lambda e: e.reciprocal(MSQ, MSQ), inc=("V", 1))
            em.op("vector", lambda e: e.scalar_tensor_tensor(MUs, MUs, -1.0, MSQ, ALU.mult, ALU.mult), inc=("V", 1))
            em.op("vector", lambda e: e.tensor_copy(BBS[0:1, 0:128], MSQ), inc=None)
            em.op("vector", lambda e: e.tensor_copy(BBS[0:1, 128:256], MUs), inc=("V", 1))
            em.wait("tensor", "V", em.now("V"))
            em.op("tensor", lambda e: e.matmul(PS[6][:, 0:128], ONER[:], BBS[0:1, 0:128], start=True, stop=True), inc=None)
            em.op("tensor", lambda e: e.matmul(PS[3][:, 0:128], ONER[:], BBS[0:1, 128:256], start=True, stop=True), inc=("T", 1))
            em.wait("vector", "T", em.now("T"))
            for p in range(NCH):
                em.op("vector", lambda e, p=p:
                      e.tensor_tensor(XSG[:, p * 128:(p + 1) * 128], XSG[:, p * 128:(p + 1) * 128],
                                      PS[6][:, 0:128], ALU.mult),
                      inc=None)
                em.op("vector", lambda e, p=p, w=dst_col:
                      e.tensor_tensor(XR[:, p * T_OWN + w:p * T_OWN + w + 128],
                                      XSG[:, p * 128:(p + 1) * 128], PS[3][:, 0:128], ALU.add),
                      inc=("V", 1) if p == NCH - 1 else None)

        for c in SCAN_CHUNKS:
            transpose_chunk(KHAT, KTM[:], c)
            transpose_chunk(VFM, VTM[:], c)
            for p in range(NCH):
                em.wait("tensor", "S", em.now("S"))
                em.wait("tensor", "V", em.now("V"))
                for (slot, src, selc) in ((0, QHAT, 0), (1, QHAT, 64), (2, KHAT, 0), (3, KHAT, 64)):
                    em.op("tensor", lambda e, p=p, c=c, slot=slot, src=src, selc=selc:
                          e.matmul(PS[2][0:64, slot * 128:(slot + 1) * 128], SEL2[:, selc:selc + 64],
                                   src[:, p * T_OWN + c * 128:p * T_OWN + (c + 1) * 128],
                                   start=True, stop=True),
                          inc=("T", 1) if slot == 3 else None)
                em.wait("scalar", "T", em.now("T"))
                em.op("scalar", lambda e: e.copy(HM[0:64, :], PS[2][0:64, 0:512]), inc=("S", 1))
                em.op("tensor", lambda e, p=p, c=c:
                      e.matmul(PS[6][:, 0:128], BCPD[:, p * 128:(p + 1) * 128], BETA[:, c * 128:(c + 1) * 128],
                               start=True, stop=True),
                      inc=("T", 1))
                em.wait("vector", "T", em.now("T"))
                em.op("vector", lambda e: e.tensor_copy(BBS[:, 0:128], PS[6][:, 0:128]), inc=None)
                em.op("vector", lambda e, p=p, c=c:
                      e.tensor_tensor(ZSC, VFM[:, p * T_OWN + c * 128:p * T_OWN + (c + 1) * 128],
                                      BBS[:, 0:128], ALU.mult),
                      inc=("V", 1))
                for hh in range(2):
                    h = 2 * p + hh
                    em.wait("tensor", "S", em.now("S"))
                    em.wait("tensor", "V", em.now("V"))
                    em.op("tensor", lambda e, hh=hh:
                          e.matmul(PS[6][:, 128:256], HM[0:64, 256 + hh * 128:256 + (hh + 1) * 128],
                                   HM[0:64, hh * 128:(hh + 1) * 128], start=True, stop=True),
                          inc=("T", 1))
                    em.wait("vector", "T", em.now("T"))
                    em.op("vector", lambda e: e.tensor_tensor(WTS[:, 0:128], PS[6][:, 128:256], UM[:], ALU.mult), inc=("V", 1))
                    em.wait("tensor", "V", em.now("V"))
                    em.op("tensor", lambda e, h=h:
                          e.matmul(PS[0][0:64, 0:128], VTM[:, h * 64:(h + 1) * 64], WTS[:, 0:128],
                                   start=True, stop=False), inc=None)
                    em.op("tensor", lambda e, h=h, hh=hh:
                          e.matmul(PS[0][0:64, 0:128], S_SB[:, h * 64:(h + 1) * 64],
                                   HM[0:64, hh * 128:(hh + 1) * 128], start=False, stop=True),
                          inc=("T", 1))
                    em.wait("vector", "T", em.now("T"))
                    em.op("vector", lambda e, hh=hh, p=p:
                          e.tensor_tensor(OUTB[hh * 64:(hh + 1) * 64, p * 128:(p + 1) * 128],
                                          PS[0][0:64, 0:128], ZSC[hh * 64:(hh + 1) * 64, :], ALU.add),
                          inc=("V", 1))
                    em.wait("tensor", "V", em.now("V"))
                    em.op("tensor", lambda e, h=h, c=c:
                          e.matmul(s_ap(h), KTM[:, h * 64:(h + 1) * 64], VTM[:, h * 64:(h + 1) * 64],
                                   start=(h % 8 == 0), stop=(h % 8 == 7)),
                          inc=("T", 1))
            em.wait("vector", "T", em.now("T"))
            em.op("vector", lambda e: e.tensor_tensor(S_F32[:, 0:512], S_F32[:, 0:512], PS[4][0:64, 0:512], ALU.add), inc=None)
            em.op("vector", lambda e: e.tensor_tensor(S_F32[:, 512:1024], S_F32[:, 512:1024], PS[5][0:64, 0:512], ALU.add), inc=None)
            em.op("vector", lambda e: e.tensor_copy(S_SB[:, 0:1024], S_F32), inc=("V", 1))
            em.wait("tensor", "S", em.now("S"))
            em.wait("tensor", "V", em.now("V"))
            em.wait("tensor", "G", g_wo)
            for m in range(NCH):
                em.wait("tensor", "V", em.now("V"))
                for k in range(NCH):
                    em.op("tensor", lambda e, m=m, k=k:
                          e.matmul(PS[1][:, 0:128], WB[:, k * 1024 + m * 128:k * 1024 + (m + 1) * 128],
                                   OUTB[:, k * 128:(k + 1) * 128], start=(k == 0), stop=(k == NCH - 1)),
                          inc=("T", 1) if k == NCH - 1 else None)
                em.wait("vector", "T", em.now("T"))
                em.op("vector", lambda e, m=m, c=c:
                      e.tensor_tensor(XSG[:, m * 128:(m + 1) * 128], PS[1][:, 0:128],
                                      XR[:, m * T_OWN + c * 128:m * T_OWN + (c + 1) * 128], ALU.add),
                      inc=("V", 1))
            ln_chunk(c * 128)

        em.wait("sync", "V", em.now("V"))
        dma("sync", park.ap(), XR[:], "L")
        hsrc = XR[:].rearrange("p (c t) -> p c t", c=NCH)[:, :, T_OWN - 513:T_OWN]
        dma("sync", hin.ap().rearrange("p (c t) -> p c t", c=NCH), hsrc, "L")
        if N_LAYERS > 0:
            em.wait("gpsimd", "L", em.now("L"))
            em.op("gpsimd", lambda e: e.collective_compute(
                "AllGather", ALU.bypass, replica_groups=[[0, 1, 2, 3, 4, 5, 6, 7]],
                ins=[hin.ap().opt()], outs=[hout.ap().opt()]), inc=("C", 1))
            em.wait("sync", "T", em.now("T"))
            dma("sync", SSB[:, 0:2048], masks_d, "L")

        WQKV = AB[:, 0:24576]
        SHBa = lambda k: AB[:, 24576 + k * 512:24576 + (k + 1) * 512]
        TABS = lambda t: AB[:, 28672 + t * TLOC:28672 + (t + 1) * TLOC]
        QGa = lambda k: AB[:, 38912 + k * 512:38912 + (k + 1) * 512]
        KSa = lambda k, sl_: AB[:, 43008 + k * 768 + sl_ * 256:43008 + k * 768 + (sl_ + 1) * 256]
        VTa = lambda sl_: AB[:, 49152 + sl_ * 1024:49152 + (sl_ + 1) * 1024]
        UBa = lambda k: AB[:, 55296 + k * 512:55296 + (k + 1) * 512]
        OUTFa = lambda k: AB[:, 55296 + k * 512:55296 + (k + 1) * 512]
        PBUFa = AB[:, 59392:59904]
        PTBa = AB[:, 59904:60928]
        SQBa = AB[:, 60928:61440]
        WOs = lambda k: (AB[:, 24576 + k * 1024:24576 + (k + 1) * 1024] if k < 4
                         else AB[:, 38912 + (k - 4) * 1024:38912 + (k - 3) * 1024])
        MSKA = lambda t: SSB[:, t * 512:(t + 1) * 512]
        RTT = SSB[0:16, 2048:2304]
        RCB = SSB[:, 2304:2320]
        NMX = SS[:, 260:261]
        SUMS = [SS[:, 264:280], SS[:, 280:296]]
        RCF = SS[:, 296:312]
        RCF2 = SS[:, 364:380]
        RSQA = ARF[0:16, 0:512]
        PBUF2 = AB[:, 24576:25088]
        PTB2 = AB[:, 25088:26112]
        NMX2 = SS[:, 261:262]
        XHLL = SS[:, 312:320]
        FBC = SS[:, 320:328]
        SAVEC = SS[:, 328:336]
        XUH = SS[:, 336:344]
        SAVE2 = SS[:, 344:352]
        W2Sa = lambda j: AB[:, j * 1024:(j + 1) * 1024]
        XNa = lambda k: AB[:, 22528 + k * 1024:22528 + (k + 1) * 1024]
        H2Ha = lambda j: AB[:, 30720 + j * 1024:30720 + (j + 1) * 1024]
        W1Ba = lambda buf: AB[:, 53248 + buf * 2048:53248 + (buf + 1) * 2048]
        GAa = AB[:, 57344 + 0:57344 + 512]
        CHS = AB[:, 57856:58368]
        SHF = ARF[:, 0:2048]
        PFG = ARF[:, 0:2048]

        def ln_stats_rolling(src_f32_slices, ntok, bias_eps=EPS):
            for p in range(NCH):
                t_prev = em.now("T")
                em.wait("vector", "T", t_prev)
                em.op("vector", lambda e, p=p: e.tensor_copy(CHS[:, 0:ntok], src_f32_slices[p]), inc=("V", 1))
                em.wait("scalar", "V", em.now("V"))
                em.wait("scalar", "T", t_prev)
                em.op("scalar", lambda e, p=p: e.activation(CHS[:, 256:256 + ntok], src_f32_slices[p], AF.Square), inc=("S", 1))
                em.wait("tensor", "V", em.now("V"))
                em.wait("tensor", "S", em.now("S"))
                em.op("tensor", lambda e, p=p, n=ntok:
                      e.matmul(PS[6][0:1, 0:n], ONEC[:, 0:1], CHS[:, 0:n], start=(p == 0), stop=(p == NCH - 1)),
                      inc=None)
                em.op("tensor", lambda e, p=p, n=ntok:
                      e.matmul(PS[3][0:1, 0:n], ONEC[:, 0:1], CHS[:, 256:256 + n], start=(p == 0), stop=(p == NCH - 1)),
                      inc=("T", 1))
            em.wait("vector", "T", em.now("T"))
            for h in range(ntok // 128):
                hsl = slice(h * 128, (h + 1) * 128)
                em.op("vector", lambda e, hsl=hsl: e.tensor_scalar_mul(MUs, PS[6][0:1, hsl], 1.0 / D), inc=("V", 1))
                em.op("vector", lambda e, hsl=hsl: e.tensor_scalar_mul(MSQ, PS[3][0:1, hsl], 1.0 / D), inc=("V", 1))
                em.op("vector", lambda e: e.tensor_tensor(TMP1, MUs, MUs, ALU.mult), inc=("V", 1))
                em.op("vector", lambda e: e.tensor_sub(MSQ, MSQ, TMP1), inc=("V", 1))
                em.op("vector", lambda e, be=bias_eps: e.tensor_scalar_add(MSQ, MSQ, be), inc=("V", 1))
                em.wait("scalar", "V", em.now("V"))
                em.op("scalar", lambda e: e.activation(MSQ, MSQ, AF.Sqrt), inc=("S", 1))
                em.wait("vector", "S", em.now("S"))
                em.op("vector", lambda e: e.reciprocal(MSQ, MSQ), inc=("V", 1))
                em.op("vector", lambda e: e.scalar_tensor_tensor(MUs, MUs, -1.0, MSQ, ALU.mult, ALU.mult), inc=("V", 1))
                em.op("vector", lambda e, hsl=hsl: e.tensor_copy(CHS[0:1, hsl], MSQ), inc=None)
                em.op("vector", lambda e, h=h: e.tensor_copy(CHS[0:1, 256 + h * 128:256 + (h + 1) * 128], MUs), inc=("V", 1))
            em.wait("tensor", "V", em.now("V"))
            em.op("tensor", lambda e, n=ntok: e.matmul(PS[6][:, 0:n], ONER[:], CHS[0:1, 0:n], start=True, stop=True), inc=None)
            em.op("tensor", lambda e, n=ntok: e.matmul(PS[3][:, 0:n], ONER[:], CHS[0:1, 256:256 + n], start=True, stop=True), inc=("T", 1))
            em.wait("vector", "T", em.now("T"))


        def emit_attn_stage(l):
            em.wait("gpsimd", "T", em.now("T"))
            for k in range(NCH):
                em.op("gpsimd", lambda e, k=k, l=l:
                      e.dma_start(WQKV[:, k * 3072:(k + 1) * 3072], in_Wqkv[l, k * 128:(k + 1) * 128, :]),
                      inc=("WQ", 16))
            em.op("gpsimd", lambda e: e.dma_start(AB[:, 28672:38912], tabs), inc=("HG", 16))
            em.wait("vector", "HG", em.now("HG"))
            em.wait("vector", "L", em.now("L"))
            wq_done = em.now("WQ")
            em.wait("sync", "C", em.now("C"))
            def dma_slab(e):
                pid = e.partition_id()
                off = ((pid + 7) % 8) * (128 * NCH * 513)
                return e.dma_start(hloc.ap(), bass.AP(hout, off, [[NCH * 513, 128], [1, NCH * 513]]))
            em.op("sync", dma_slab, inc=("HS", 16))
            em.wait("sync", "HS", em.now("HS"))
            em.op("sync", lambda e: e.dma_start(XHLL[:, 0:NCH],
                                                bass.AP(hloc, 512, [[NCH * 513, 128], [513, NCH]]),
                                                allow_slow_non_contiguous=True),
                  inc=("HS", 16))
            em.wait("vector", "HS", em.now("HS"))
            em.op("vector", lambda e: e.tensor_scalar_mul(FBC[:, 0:NCH], XHLL[:, 0:NCH], FLG[:, 0:1]), inc=("V", 1))

            for g in range(5):
                if g == 0:
                    em.wait("gpsimd", "HS", em.now("HS"))
                    for k in range(NCH):
                        toff = 1 if k < 4 else 0
                        em.op("gpsimd", lambda e, k=k, toff=toff:
                              e.dma_start(SHBa(k), bass.AP(hloc, k * 513 + toff, [[NCH * 513, 128], [1, 512]])),
                              inc=("HG", 16))
                    em.wait("tensor", "HG", em.now("HG"))
                else:
                    em.wait("vector", "T", em.now("T"))
                    base = 512 * (g - 1)
                    for k in range(NCH):
                        if k < 4:
                            em.op("vector", lambda e, k=k, base=base:
                                  e.tensor_copy(SHBa(k), XR[:, k * T_OWN + base:k * T_OWN + base + 512]),
                                  inc=("V", 1))
                        else:
                            if g == 1:
                                em.op("vector", lambda e, k=k:
                                      e.tensor_copy(SHBa(k)[:, 0:1], FBC[:, k:k + 1]),
                                      inc=None)
                            else:
                                em.op("vector", lambda e, k=k:
                                      e.tensor_copy(SHBa(k)[:, 0:1], SAVEC[:, k:k + 1]),
                                      inc=None)
                            em.op("vector", lambda e, k=k, base=base:
                                  e.tensor_copy(SHBa(k)[:, 1:512], XR[:, k * T_OWN + base:k * T_OWN + base + 511]),
                                  inc=("V", 1))
                    em.wait("tensor", "V", em.now("V"))
                if g >= 1:
                    em.op("vector", lambda e, g=g:
                          e.tensor_copy(SAVEC[:, 0:NCH],
                                        XR[:].rearrange("p (c t) -> p c t", c=NCH)[:, :, 512 * g - 1]),
                          inc=("V", 1))

                em.wait("tensor", "WQ", wq_done)
                for stream in range(2):
                    for cc in range(NCH):
                        m = stream * 8 + cc
                        bank = PS[cc % 2]
                        em.wait("tensor", "V", em.now("V"))
                        em.wait("tensor", "S", em.now("S"))
                        for k in range(NCH):
                            em.op("tensor", lambda e, m=m, k=k, b=bank:
                                  e.matmul(b[:], WQKV[:, k * 3072 + m * 128:k * 3072 + (m + 1) * 128],
                                           SHBa(k), start=(k == 0), stop=(k == NCH - 1)),
                                  inc=("T", 1) if k == NCH - 1 else None)
                        t_mm = em.now("T")
                        em.wait("scalar", "T", t_mm)
                        em.op("scalar", lambda e, b=bank: e.activation(SQBa, b[:], AF.Square), inc=("S", 1))
                        em.wait("tensor", "S", em.now("S"))
                        em.op("tensor", lambda e, cc=cc:
                              e.matmul(PS[2][0:16, 0:512], PATI[:, (cc // 2) * 16:(cc // 2 + 1) * 16], SQBa,
                                       start=(cc == 0), stop=(cc == NCH - 1)),
                              inc=("T", 1))
                        em.wait("vector", "T", em.now("T"))
                        if stream == 0:
                            em.op("vector", lambda e, cc=cc, b=bank: e.tensor_copy(QGa(cc), b[:]), inc=("V", 1))
                        else:
                            em.op("vector", lambda e, cc=cc, b=bank: e.tensor_copy(UBa(cc), b[:]), inc=("V", 1))
                    em.wait("vector", "T", em.now("T"))
                    em.op("vector", lambda e: e.tensor_scalar_max(RSQA, PS[2][0:16, 0:512], 1e-24), inc=("V", 1))
                    em.op("vector", lambda e: e.reciprocal(RSQA, RSQA), inc=("V", 1))
                    em.wait("scalar", "V", em.now("V"))
                    em.op("scalar", lambda e, sc=(64.0 if stream == 0 else 1.0):
                          e.activation(RSQA, RSQA, AF.Sqrt, scale=sc), inc=("S", 1))
                    em.wait("vector", "S", em.now("S"))
                    RSQB_in = SSB[0:16, 2320:2832]
                    em.op("vector", lambda e: e.tensor_copy(RSQB_in, RSQA), inc=("V", 1))
                    for cc in range(NCH):
                        em.wait("tensor", "V", em.now("V"))
                        em.op("tensor", lambda e, cc=cc:
                              e.matmul(PS[3][:], BCPI[:, (cc // 2) * 128:(cc // 2 + 1) * 128], RSQB_in,
                                       start=True, stop=True),
                              inc=("T", 1))
                        em.wait("vector", "T", em.now("T"))
                        qscol = QKS[:, l * 4 + (2 * stream) + (cc % 2):l * 4 + (2 * stream) + (cc % 2) + 1]
                        if stream == 0:
                            em.op("vector", lambda e, cc=cc, qs=qscol:
                                  e.scalar_tensor_tensor(UBa(cc), QGa(cc), qs, PS[3][:], ALU.mult, ALU.mult),
                                  inc=("V", 1))
                        else:
                            em.op("vector", lambda e, cc=cc, qs=qscol:
                                  e.scalar_tensor_tensor(UBa(cc), UBa(cc), qs, PS[3][:], ALU.mult, ALU.mult),
                                  inc=("V", 1))
                    tC = TABS(0 if stream == 0 else 2)[:, g * 512:(g + 1) * 512]
                    tS = TABS(1 if stream == 0 else 3)[:, g * 512:(g + 1) * 512]
                    for cc in range(NCH):
                        sgn = -1.0 if cc % 2 == 0 else 1.0
                        part = cc + 1 if cc % 2 == 0 else cc - 1
                        em.op("vector", lambda e, cc=cc, tC=tC: e.tensor_tensor(PBUFa, UBa(cc), tC, ALU.mult), inc=None)
                        em.op("vector", lambda e, part=part, sgn=sgn, tS=tS:
                              e.scalar_tensor_tensor(PTBa[:, 0:512], UBa(part), sgn, tS, ALU.mult, ALU.mult), inc=None)
                        if stream == 0:
                            em.op("vector", lambda e, cc=cc: e.tensor_add(QGa(cc), PBUFa, PTBa[:, 0:512]), inc=("V", 1))
                        else:
                            em.op("vector", lambda e, cc=cc, g=g:
                                  e.tensor_add(KSa(cc, (2 * g) % 3), PBUFa[:, 0:256], PTBa[:, 0:256]), inc=None)
                            em.op("vector", lambda e, cc=cc, g=g:
                                  e.tensor_add(KSa(cc, (2 * g + 1) % 3), PBUFa[:, 256:512], PTBa[:, 256:512]), inc=("V", 1))
                for tt in range(4):
                    gslot = (g * 4 + tt) % 6
                    for nh in range(2):
                        bank = PS[(tt * 2 + nh) % 2]
                        em.wait("tensor", "V", em.now("V"))
                        for k in range(NCH):
                            em.op("tensor", lambda e, k=k, tt=tt, nh=nh, b=bank:
                                  e.matmul(b[:], SHBa(k)[:, tt * 128:(tt + 1) * 128],
                                           WQKV[:, k * 3072 + 2048 + nh * 512:k * 3072 + 2048 + (nh + 1) * 512],
                                           start=(k == 0), stop=(k == NCH - 1)),
                                  inc=("T", 1) if k == NCH - 1 else None)
                        em.wait("vector", "T", em.now("T"))
                        em.op("vector", lambda e, gslot=gslot, nh=nh, b=bank:
                              e.tensor_copy(VTa(gslot)[:, nh * 512:(nh + 1) * 512], b[:]),
                              inc=("V", 1))

                if not DBG_NO_ATT:
                    emit_attn_windows(l, g)

                if not DBG_NO_WO:
                    emit_wo_group(l, g)

                if DBG_DUMP3 and l == 0 and g == 0:
                    em.wait("sync", "V", em.now("V"))
                    em.wait("sync", "T", em.now("T"))
                    em.op("sync", lambda e: e.dma_start(bass.AP(park, 0, [[16384, 128], [1, 8]]), XUH), inc=("L", 16))
                    em.op("sync", lambda e: e.dma_start(bass.AP(park, 8, [[16384, 128], [1, 8]]), XHLL), inc=("L", 16))
                    em.op("sync", lambda e: e.dma_start(bass.AP(park, 16, [[16384, 128], [1, 8]]), FBC), inc=("L", 16))
                    em.wait("gpsimd", "V", em.now("V"))
                    em.wait("gpsimd", "T", em.now("T"))
                    for p_ in range(NCH):
                        em.op("gpsimd", lambda e, p_=p_:
                              e.dma_start(bass.AP(park, 32 + p_ * 512, [[16384, 128], [1, 512]]), OUTFa(p_)),
                              inc=("HG", 16))

                if DBG_DUMP2 and l == 0 and g == 0:
                    em.wait("vector", "T", em.now("T"))
                    em.wait("vector", "S", em.now("S"))
                    d2 = [(XR[:, 1536:1552], SUMS[0]), (XR[:, 1552:1568], SUMS[1]),
                          (XR[:, 2080:2096], RCB[:, 0:16]), (XR[:, 2096:2112], SSB[:, 2832:2848]),
                          (XR[0:16, 1568:1824], RTT[0:16, 0:256])]
                    for p_ in range(NCH):
                        d2.append((XR[:, 4096 + p_ * 512:4096 + (p_ + 1) * 512], OUTFa(p_)))
                    for i, (d_, s_) in enumerate(d2):
                        em.op("vector", lambda e, d_=d_, s_=s_: e.tensor_copy(d_, s_),
                              inc=("V", 1) if i == len(d2) - 1 else None)

            if DBG_DUMP and l == 0:
                em.wait("vector", "T", em.now("T"))
                em.wait("vector", "S", em.now("S"))
                dumps = [
                    (XR[:, 0:512], PBUFa),
                    (XR[:, 512:1536], PTBa),
                    (XR[:, 1536:1552], SUMS[0]),
                    (XR[:, 1552:1568], SUMS[1]),
                    (XR[0:16, 1568:1824], RTT[0:16, 0:256]),
                    (XR[:, 1824:2080], PS[3][:, 0:256]),
                    (XR[:, 2080:2096], RCB[:, 0:16]),
                    (XR[:, 2096:2112], SSB[:, 2832:2848]),
                ]
                for p_ in range(NCH):
                    dumps.append((XR[:, 4096 + p_ * 512:4096 + (p_ + 1) * 512], OUTFa(p_)))
                    dumps.append((XR[:, 8192 + p_ * 512:8192 + (p_ + 1) * 512], QGa(p_)))
                    dumps.append((XR[:, 12288 + p_ * 256:12288 + (p_ + 1) * 256], KSa(p_, 0)))
                dumps.append((XR[:, 14336:15360], VTa(0)))
                dumps.append((XR[:, 15360:16384], VTa(1)))
                for i, (d_, s_) in enumerate(dumps):
                    em.op("vector", lambda e, d_=d_, s_=s_: e.tensor_copy(d_, s_),
                          inc=("V", 1) if i == len(dumps) - 1 else None)

        def emit_attn_windows(l, g):
            for w in (2 * g, 2 * g + 1):
                if w == 0:
                    continue
                ps_slot = (w - 1) % 3
                cs_slot = w % 3
                pv_slots = [((w - 1) * 2) % 6, ((w - 1) * 2 + 1) % 6, (w * 2) % 6, (w * 2 + 1) % 6]
                wcol = (w % 2) * 256
                mbase = 2 if w == 2 else 0
                s_rd = [em.now("S"), em.now("S")]
                v_rd = [em.now("V"), em.now("V")]
                s_pb = [em.now("S"), em.now("S")]
                pv_guard = [em.now("V"), em.now("V")]
                for p in range(NCH):
                    for hh in range(2):
                        h = 2 * p + hh
                        gq = h // 4
                        rb = (h % 4) * 32
                        par = h % 2
                        SC = PS[4] if par == 0 else PS[6]
                        PBU = PBUFa if par == 0 else PBUF2
                        PTB_ = PTBa if par == 0 else PTB2
                        NMXs = NMX if par == 0 else NMX2
                        pbc = par * 512
                        PVB = PS[5] if p % 2 == 0 else PS[1]
                        for qt in range(2):
                            em.wait("tensor", "S", s_rd[par])
                            em.wait("tensor", "V", v_rd[par])
                            for half in range(2):
                                chunk = 2 * gq + half
                                lq = QGa(chunk)[rb:rb + 32, wcol + qt * 128:wcol + qt * 128 + 128]
                                tpk = {"tile_position": (96, 0)} if rb == 96 else {}
                                for side, ksl in ((0, ps_slot), (1, cs_slot)):
                                    em.op("tensor", lambda e, chunk=chunk, rb=rb, lq=lq, ksl=ksl, side=side, half=half, tpk=tpk, SC=SC:
                                          e.matmul(SC[:, side * 256:(side + 1) * 256],
                                                   lq, KSa(chunk, ksl)[rb:rb + 32, :],
                                                   start=(half == 0 and side == 0), stop=(half == 1 and side == 1), **tpk),
                                          inc=("T", 1) if (half == 1 and side == 1) else None)
                            em.wait("vector", "T", em.now("T"))
                            em.op("vector", lambda e, SC=SC, NMXs=NMXs:
                                  e.tensor_reduce(NMXs, SC[:, 0:512], AX.X, ALU.max, negate=True),
                                  inc=("V", 1))
                            em.wait("scalar", "V", em.now("V"))
                            em.op("scalar", lambda e, SC=SC, PBU=PBU, NMXs=NMXs:
                                  e.activation(PBU, SC[:, 0:512], AF.Exp, bias=NMXs), inc=("S", 1))
                            em.wait("vector", "S", em.now("S"))
                            em.op("vector", lambda e, qt=qt, h=h, mbase=mbase, PBU=PBU:
                                  e.scalar_tensor_tensor(PBU, PBU, 1.0, MSKA(mbase + qt), ALU.mult, ALU.mult,
                                                         accum_out=SUMS[qt][:, h:h + 1]),
                                  inc=("V", 1))
                            em.wait("tensor", "V", em.now("V"))
                            em.wait("tensor", "S", s_pb[par])
                            for kc in range(4):
                                em.op("tensor", lambda e, kc=kc, PBU=PBU, pbc=pbc:
                                      e.transpose(PB[:, pbc + kc * 128:pbc + (kc + 1) * 128],
                                                  PBU[:, kc * 128:(kc + 1) * 128], IDNB[:]),
                                      inc=("T", 1) if kc == 3 else None)
                            em.wait("scalar", "T", em.now("T"))
                            for kc in range(4):
                                em.op("scalar", lambda e, kc=kc, qt=qt, PTB_=PTB_, pbc=pbc:
                                      e.copy(PTB_[:, kc * 256 + qt * 128:kc * 256 + qt * 128 + 128],
                                             PB[:, pbc + kc * 128:pbc + (kc + 1) * 128]),
                                      inc=("S", 1) if kc == 3 else None)
                        s_rd[par] = em.now("S")
                        v_rd[par] = em.now("V")
                        s_pb[par] = em.now("S")
                        em.wait("tensor", "S", em.now("S"))
                        em.wait("tensor", "V", pv_guard[p % 2])
                        for kc in range(4):
                            vsl = pv_slots[kc]
                            em.op("tensor", lambda e, kc=kc, vsl=vsl, h=h, hh=hh, PVB=PVB, PTB_=PTB_:
                                  e.matmul(PVB[hh * 64:(hh + 1) * 64, 0:256], VTa(vsl)[:, h * 64:(h + 1) * 64],
                                           PTB_[:, kc * 256:(kc + 1) * 256], start=(kc == 0), stop=(kc == 3)),
                                  inc=("T", 1) if kc == 3 else None)
                    em.wait("vector", "T", em.now("T"))
                    em.op("vector", lambda e, p=p, wcol=wcol, PVB=(PS[5] if p % 2 == 0 else PS[1]):
                          e.tensor_copy(OUTFa(p)[:, wcol:wcol + 256], PVB[:, 0:256]),
                          inc=("V", 1))
                    pv_guard[p % 2] = em.now("V")
                em.wait("scalar", "V", em.now("V"))
                em.op("scalar", lambda e: e.activation(RCF[:, 0:16], SUMS[0], AF.Ln), inc=None)
                em.op("scalar", lambda e: e.activation(RCF[:, 0:16], RCF[:, 0:16], AF.Exp, scale=-1.0), inc=None)
                em.op("scalar", lambda e: e.activation(RCF2[:, 0:16], SUMS[1], AF.Ln), inc=None)
                em.op("scalar", lambda e: e.activation(RCF2[:, 0:16], RCF2[:, 0:16], AF.Exp, scale=-1.0), inc=("S", 1))
                em.wait("vector", "S", em.now("S"))
                em.op("vector", lambda e: e.tensor_copy(RCB[:, 0:16], RCF[:, 0:16]), inc=None)
                em.op("vector", lambda e: e.tensor_copy(SSB[:, 2832:2848], RCF2[:, 0:16]), inc=("V", 1))
                em.wait("tensor", "V", em.now("V"))
                em.op("tensor", lambda e: e.transpose(PB[0:16, 0:128], RCB[:, 0:16], IDNB[:]), inc=None)
                em.op("tensor", lambda e: e.transpose(PB[0:16, 128:256], SSB[:, 2832:2848], IDNB[:]), inc=("T", 1))
                em.wait("scalar", "T", em.now("T"))
                em.op("scalar", lambda e: e.copy(RTT[0:16, 0:128], PB[0:16, 0:128]), inc=None)
                em.op("scalar", lambda e: e.copy(RTT[0:16, 128:256], PB[0:16, 128:256]), inc=("S", 1))
                for p in range(NCH):
                    em.wait("tensor", "S", em.now("S"))
                    em.wait("tensor", "V", em.now("V"))
                    em.op("tensor", lambda e, p=p:
                          e.matmul(PS[3][:, 0:256], BCPD[:, p * 128:(p + 1) * 128], RTT[0:16, 0:256],
                                   start=True, stop=True),
                          inc=("T", 1))
                    em.wait("vector", "T", em.now("T"))
                    em.op("vector", lambda e, p=p, wcol=wcol:
                          e.tensor_tensor(OUTFa(p)[:, wcol:wcol + 256], OUTFa(p)[:, wcol:wcol + 256],
                                          PS[3][:, 0:256], ALU.mult),
                          inc=("V", 1))

        def emit_wo_group(l, g):
            em.wait("gpsimd", "V", em.now("V"))
            for k in range(NCH):
                em.op("gpsimd", lambda e, k=k, l=l:
                      e.dma_start(WOs(k), in_Wo[l, k * 128:(k + 1) * 128, :]),
                      inc=("WO", 16))
            em.wait("tensor", "WO", em.now("WO"))
            ntok = 256 if g == 0 else 512
            ocol = 256 if g == 0 else 0
            for m in range(NCH):
                bank = PS[m % 2]
                em.wait("tensor", "V", em.now("V"))
                for k in range(NCH):
                    em.op("tensor", lambda e, m=m, k=k, b=bank, ocol=ocol, n=ntok:
                          e.matmul(b[:, 0:n], WOs(k)[:, m * 128:(m + 1) * 128],
                                   OUTFa(k)[:, ocol:ocol + n], start=(k == 0), stop=(k == NCH - 1)),
                          inc=("T", 1) if k == NCH - 1 else None)
                em.wait("vector", "T", em.now("T"))
                if g == 0:
                    em.op("vector", lambda e, m=m, b=bank:
                          e.tensor_tensor(XUH[:, m:m + 1], b[:, 255:256], XHLL[:, m:m + 1], ALU.add),
                          inc=("V", 1))
                else:
                    base = 512 * (g - 1)
                    em.op("vector", lambda e, m=m, b=bank, base=base:
                          e.tensor_tensor(XR[:, m * T_OWN + base:m * T_OWN + base + 512], b[:, 0:512],
                                          XR[:, m * T_OWN + base:m * T_OWN + base + 512], ALU.add),
                          inc=("V", 1))

        def emit_ffn_stage(l):
            em.wait("gpsimd", "T", em.now("T"))
            if not (DBG_FFN_XN_ONLY or DBG_FFN_NO_W2):
                for j in range(NPAIR):
                    em.op("gpsimd", lambda e, j=j, l=l:
                          e.dma_start(W2Sa(j), in_W2[l, j * 128:(j + 1) * 128, :]),
                          inc=("W2L", 16))
            w2_done = em.now("W2L")
            t_pair_done = {}
            for th in range(2):
                tb = th * 1024
                for fg in range(4):
                    t0 = tb + fg * 256
                    em.wait("vector", "T", em.now("T"))
                    emit_shf(l, t0)
                    slices = [SHF[:, k * 256:(k + 1) * 256] for k in range(NCH)]
                    ln_stats_rolling(slices, 256)
                    for k in range(NCH):
                        gcol = LNP[:, l * 16 + k:l * 16 + k + 1]
                        bcol = LNP[:, l * 16 + 8 + k:l * 16 + 8 + k + 1]
                        em.op("vector", lambda e, k=k: e.tensor_tensor(SHF[:, k * 256:(k + 1) * 256],
                                                                      SHF[:, k * 256:(k + 1) * 256], PS[6][:, 0:256], ALU.mult), inc=None)
                        em.op("vector", lambda e, k=k: e.tensor_tensor(SHF[:, k * 256:(k + 1) * 256],
                                                                      SHF[:, k * 256:(k + 1) * 256], PS[3][:, 0:256], ALU.add), inc=None)
                        em.op("vector", lambda e, k=k, t0=t0, tb=tb, g_=gcol, b_=bcol:
                              e.scalar_tensor_tensor(XNa(k)[:, t0 - tb:t0 - tb + 256],
                                                     SHF[:, k * 256:(k + 1) * 256], g_,
                                                     b_.to_broadcast((128, 256)), ALU.mult, ALU.add),
                              inc=("V", 1) if k == NCH - 1 else None)
                if th == 0:
                    em.op("vector", lambda e:
                          e.tensor_copy(SAVE2[:, 0:NCH],
                                        XR[:].rearrange("p (c t) -> p c t", c=NCH)[:, :, 1023]),
                          inc=("V", 1))
                for j in range(NPAIR if not DBG_FFN_XN_ONLY else 0):
                    buf = j % 2
                    gp = th * NPAIR + j
                    if gp >= 2:
                        em.wait("gpsimd", "T", t_pair_done[gp - 2])
                    em.op("gpsimd", lambda e, j=j, l=l, buf=buf:
                          e.dma_start(W1Ba(buf), in_W1[l].rearrange("(kk p) n -> p kk n", p=128)[:, :, j * 256:(j + 1) * 256]),
                          inc=("W10" if buf == 0 else "W11", 16))
                    em.wait("tensor", "W10" if buf == 0 else "W11", em.now("W10" if buf == 0 else "W11"))
                    for nt in range(2):
                        em.wait("tensor", "V", em.now("V"))
                        em.wait("tensor", "S", em.now("S"))
                        for k in range(NCH):
                            em.op("tensor", lambda e, k=k, nt=nt, buf=buf:
                                  e.matmul(PS[0][:], W1Ba(buf)[:, k * 256:k * 256 + 128],
                                           XNa(k)[:, nt * 512:(nt + 1) * 512], start=(k == 0), stop=(k == NCH - 1)),
                                  inc=None)
                            em.op("tensor", lambda e, k=k, nt=nt, buf=buf:
                                  e.matmul(PS[1][:], W1Ba(buf)[:, k * 256 + 128:k * 256 + 256],
                                           XNa(k)[:, nt * 512:(nt + 1) * 512], start=(k == 0), stop=(k == NCH - 1)),
                                  inc=("T", 1) if k == NCH - 1 else None)
                        em.wait("scalar", "T", em.now("T"))
                        b1g = B1P[:, l * 44 + 2 * j + 1:l * 44 + 2 * j + 2]
                        b1a = B1P[:, l * 44 + 2 * j:l * 44 + 2 * j + 1]
                        em.op("scalar", lambda e, b=b1g: e.activation(GAa, PS[1][:], AF.Gelu, bias=b), inc=("S", 1))
                        em.wait("vector", "S", em.now("S"))
                        em.op("vector", lambda e, j=j, nt=nt, b=b1a:
                              e.scalar_tensor_tensor(H2Ha(j)[:, nt * 512:(nt + 1) * 512], PS[0][:], b, GAa,
                                                     ALU.add, ALU.mult),
                              inc=("V", 1))
                    t_pair_done[gp] = em.now("T")
                em.wait("tensor", "W2L", w2_done)
                for tg in range(4 if not (DBG_FFN_XN_ONLY or DBG_FFN_NO_W2) else 0):
                    tc = tg * 256
                    em.wait("tensor", "V", em.now("V"))
                    for j in range(NPAIR):
                        for m2 in range(NCH):
                            bank = PS[2 + m2 // 2]
                            colb = (m2 % 2) * 256
                            em.op("tensor", lambda e, j=j, m2=m2, bank=bank, colb=colb, tc=tc:
                                  e.matmul(bank[:, colb:colb + 256], W2Sa(j)[:, m2 * 128:(m2 + 1) * 128],
                                           H2Ha(j)[:, tc:tc + 256],
                                           start=(j == 0 and m2 % 2 == 0), stop=(j == NPAIR - 1 and m2 % 2 == 1)),
                                  inc=("T", 1) if (j == NPAIR - 1 and m2 == NCH - 1) else None)
                    em.wait("vector", "T", em.now("T"))
                    for m2 in range(NCH):
                        bank = PS[2 + m2 // 2]
                        colb = (m2 % 2) * 256
                        b2col = B2P[:, l * 8 + m2:l * 8 + m2 + 1]
                        em.op("vector", lambda e, m2=m2, bank=bank, colb=colb, tc=tc, tb=tb, b2c=b2col:
                              e.scalar_tensor_tensor(XR[:, m2 * T_OWN + tb + tc:m2 * T_OWN + tb + tc + 256],
                                                     bank[:, colb:colb + 256], b2c,
                                                     XR[:, m2 * T_OWN + tb + tc:m2 * T_OWN + tb + tc + 256],
                                                     ALU.add, ALU.add),
                              inc=("V", 1) if m2 == NCH - 1 else None)

        def emit_shf(l, t0):
            for k in range(NCH):
                if k < 4:
                    em.op("vector", lambda e, k=k, t0=t0:
                          e.tensor_copy(SHF[:, k * 256:(k + 1) * 256],
                                        XR[:, k * T_OWN + t0:k * T_OWN + t0 + 256]),
                          inc=("V", 1) if k == NCH - 1 else None)
                else:
                    if t0 == 0:
                        em.op("vector", lambda e, k=k:
                              e.tensor_scalar_mul(SHF[:, k * 256:k * 256 + 1], XUH[:, k:k + 1], FLG[:, 0:1]),
                              inc=None)
                        em.op("vector", lambda e, k=k:
                              e.tensor_copy(SHF[:, k * 256 + 1:(k + 1) * 256],
                                            XR[:, k * T_OWN:k * T_OWN + 255]),
                              inc=("V", 1) if k == NCH - 1 else None)
                    elif t0 == 1024:
                        em.op("vector", lambda e, k=k:
                              e.tensor_copy(SHF[:, k * 256:k * 256 + 1], SAVE2[:, k:k + 1]),
                              inc=None)
                        em.op("vector", lambda e, k=k, t0=t0:
                              e.tensor_copy(SHF[:, k * 256 + 1:(k + 1) * 256],
                                            XR[:, k * T_OWN + t0:k * T_OWN + t0 + 255]),
                              inc=("V", 1) if k == NCH - 1 else None)
                    else:
                        em.op("vector", lambda e, k=k, t0=t0:
                              e.tensor_copy(SHF[:, k * 256:(k + 1) * 256],
                                            XR[:, k * T_OWN + t0 - 1:k * T_OWN + t0 + 255]),
                              inc=("V", 1) if k == NCH - 1 else None)

        def emit_layer(l):
            emit_attn_stage(l)
            if not DBG_NO_FFN:
                emit_ffn_stage(l)
            if l < DEPTH - 1:
                em.wait("sync", "V", em.now("V"))
                hsrc2 = XR[:].rearrange("p (c t) -> p c t", c=NCH)[:, :, T_OWN - 513:T_OWN]
                dma("sync", hin.ap().rearrange("p (c t) -> p c t", c=NCH), hsrc2, "L")
                em.wait("gpsimd", "L", em.now("L"))
                em.op("gpsimd", lambda e: e.collective_compute(
                    "AllGather", ALU.bypass, replica_groups=[[0, 1, 2, 3, 4, 5, 6, 7]],
                    ins=[hin.ap().opt()], outs=[hout.ap().opt()]), inc=("C", 1))

        def emit_final():
            for fg in range(8):
                t0 = fg * 256
                em.wait("sync", "V", em.now("V"))
                em.op("sync", lambda e, t0=t0:
                      e.dma_start(PFG.rearrange("p (c t) -> p c t", c=NCH),
                                  park.ap().rearrange("p (c t) -> p c t", c=NCH)[:, :, t0:t0 + 256]),
                      inc=("PK", 16))
                em.wait("vector", "PK", em.now("PK"))
                for k in range(NCH):
                    em.op("vector", lambda e, k=k, t0=t0:
                          e.tensor_tensor(PFG[:, k * 256:(k + 1) * 256], PFG[:, k * 256:(k + 1) * 256],
                                          XR[:, k * T_OWN + t0:k * T_OWN + t0 + 256], ALU.add),
                          inc=("V", 1) if k == NCH - 1 else None)
                slices = [PFG[:, k * 256:(k + 1) * 256] for k in range(NCH)]
                ln_stats_rolling(slices, 256)
                for k in range(NCH):
                    em.op("vector", lambda e, k=k, t0=t0:
                          e.tensor_tensor(PFG[:, k * 256:(k + 1) * 256], PFG[:, k * 256:(k + 1) * 256],
                                          PS[6][:, 0:256], ALU.mult),
                          inc=None)
                    em.op("vector", lambda e, k=k, t0=t0:
                          e.tensor_tensor(XR[:, k * T_OWN + t0:k * T_OWN + t0 + 256],
                                          PFG[:, k * 256:(k + 1) * 256], PS[3][:, 0:256], ALU.add),
                          inc=("V", 1) if k == NCH - 1 else None)

        for l in range(N_LAYERS):
            emit_layer(l)
        if N_LAYERS == DEPTH and not OUTPUT_X1:
            emit_final()

        for tt in range(16):
            for half in range(2):
                em.wait("tensor", "V", em.now("V"))
                em.wait("tensor", "O", em.now("O"))
                buf = XIN[half]
                for kk in range(4):
                    k = half * 4 + kk
                    bank = PS[kk % 2]
                    if kk >= 2:
                        em.wait("tensor", "V", em.now("V") - 1)
                    em.op("tensor", lambda e, b=bank, tt=tt, k=k:
                          e.transpose(b[:, 0:128], XR[:, k * T_OWN + tt * 128:k * T_OWN + (tt + 1) * 128], IDNF[:]),
                          inc=("T", 1))
                    em.wait("vector", "T", em.now("T"))
                    em.op("vector", lambda e, b=bank, kk=kk, buf=buf:
                          e.tensor_copy(buf[:, kk * 128:(kk + 1) * 128], b[:, 0:128]),
                          inc=("V", 1))
                em.wait("sync", "V", em.now("V"))
                em.op("sync", lambda e, tt=tt, half=half, buf=buf:
                      e.dma_start(out_tm[tt * 128:(tt + 1) * 128, half * 512:(half + 1) * 512], buf[:]),
                      inc=("O", 16))
        em.wait("sync", "O", em.now("O"))
        em.wait("sync", "L", em.now("L"))
        em.wait("sync", "G", em.now("G"))

        em.check_deadlock()
        sems = {name: ctx.enter_context(nc.semaphore(f"sem{name}"))
                for name in ("L", "G", "T", "V", "S", "C", "O", "X0", "X1",
                             "WQ", "WO", "W10", "W11", "W2L", "HS", "HG", "PK")}
        with nc.Block() as block:
            em.emit_into(block, sems)

    return nc


def _stage_inputs(inputs):
    f32 = lambda a: np.ascontiguousarray(np.asarray(a, dtype=np.float32))
    x = f32(inputs["L0_attn_x"])
    Wqkv = f32(inputs["L0_ffn_Wqkv"])
    qsc = f32(inputs["L0_ffn_q_scale"]); ksc = f32(inputs["L0_ffn_k_scale"])
    Wo = f32(inputs["L0_ffn_Wo"]); lng = f32(inputs["L0_ffn_ln_g"]); lnb = f32(inputs["L0_ffn_ln_b"])
    W1 = f32(inputs["L0_ffn_W1"]); b1 = f32(inputs["L0_ffn_b1"])
    W2 = f32(inputs["L0_ffn_W2"]); b2 = f32(inputs["L0_ffn_b2"])

    WqkvP = np.ascontiguousarray(np.concatenate(
        [Wqkv[:, :, 0:1024][:, :, PERM], Wqkv[:, :, 1024:2048][:, :, PERM], Wqkv[:, :, 2048:3072]], axis=2))
    W1p = np.zeros((DEPTH, D, D1P), np.float32)
    W1p[:, :, W1VALID] = W1[:, :, W1COLS[W1VALID]]
    b1pv = np.zeros((DEPTH, D1P), np.float32)
    b1pv[:, W1VALID] = b1[:, W1COLS[W1VALID]]
    W2p = np.zeros((DEPTH, NPAIR * 128, D), np.float32)
    for j in range(NPAIR):
        s0, e0 = j * 128, min(j * 128 + 128, 2730)
        W2p[:, j * 128:j * 128 + (e0 - s0), :] = W2[:, s0:e0, :]

    b1cols = np.zeros((128, DEPTH, 44), np.float32)
    for l in range(DEPTH):
        for j in range(NPAIR):
            b1cols[:, l, 2 * j] = b1pv[l, j * 256:j * 256 + 128]
            b1cols[:, l, 2 * j + 1] = b1pv[l, j * 256 + 128:j * 256 + 256]
    b2cols = np.zeros((128, DEPTH, 8), np.float32)
    for l in range(DEPTH):
        b2cols[:, l, :] = b2[l].reshape(8, 128).T
    lnp = np.zeros((128, DEPTH, 16), np.float32)
    for l in range(DEPTH):
        lnp[:, l, 0:8] = lng[l].reshape(8, 128).T
        lnp[:, l, 8:16] = lnb[l].reshape(8, 128).T
    qks = np.zeros((128, DEPTH, 4), np.float32)
    for l in range(DEPTH):
        qks[:, l, 0] = np.tile(qsc[l, 0:32], 4)
        qks[:, l, 1] = np.tile(qsc[l, 32:64], 4)
        qks[:, l, 2] = np.tile(ksc[l, 0:32], 4)
        qks[:, l, 3] = np.tile(ksc[l, 32:64], 4)

    in_maps = []
    for c in range(8):
        b, hh = c // 2, c % 2
        Cq, Sq, Ck, Sk = host_rotary_tables(hh)
        tabsT = np.zeros((128, 4 * TLOC), np.float32)
        for ti, tab in enumerate((Cq, Sq, Ck, Sk)):
            tabsT[:, ti * TLOC:(ti + 1) * TLOC] = np.tile(tab.T, (4, 1))
        flg = np.zeros((128, 4), np.float32)
        flg[:, 0] = 0.0 if hh == 0 else 1.0
        flg[:, 1] = 1.0 if hh == 0 else 0.0
        m = host_masks(hh)
        im = {
            "x_tm": np.ascontiguousarray(x[b, hh * T_OWN:(hh + 1) * T_OWN, :]),
            "tabs": tabsT.astype(ml_dtypes.bfloat16),
            "masks": np.ascontiguousarray(m),
            "flags": flg,
            "qks": np.ascontiguousarray(qks.reshape(128, DEPTH * 4)),
            "lnp": np.ascontiguousarray(lnp.reshape(128, DEPTH * 16)),
            "b1p": np.ascontiguousarray(b1cols.reshape(128, DEPTH * 44)),
            "b2p": np.ascontiguousarray(b2cols.reshape(128, DEPTH * 8)),
            "dn_Wq": f32(inputs["L0_attn_Wq"]),
            "dn_Wk": f32(inputs["L0_attn_Wk"]),
            "dn_Wv": f32(inputs["L0_attn_Wv"]),
            "dn_Wb": f32(inputs["L0_attn_Wbeta"]),
            "dn_Wo": f32(inputs["L0_attn_Wout"]),
            "in_Wqkv": WqkvP.astype(ml_dtypes.bfloat16),
            "in_Wo": Wo.astype(ml_dtypes.bfloat16),
            "in_W1": W1p.astype(ml_dtypes.bfloat16),
            "in_W2": W2p.astype(ml_dtypes.bfloat16),
        }
        in_maps.append(im)
    return in_maps


def kernel(**inputs):
    in_maps = _stage_inputs(inputs)
    nc = build_program()
    res = run_bass_kernel_spmd(nc, in_maps, core_ids=list(range(8)))
    out = np.zeros((4, 4096, 1024), np.float32)
    for c in range(8):
        b, hh = c // 2, c % 2
        out[b, hh * T_OWN:(hh + 1) * T_OWN, :] = res.results[c]["out_tm"]
    return out
